# version 03
# baseline (speedup 1.0000x reference)
import numpy as np
import ml_dtypes
import concourse.bass as bass
import concourse.mybir as mybir
from concourse.bass_utils import run_bass_kernel_spmd
from concourse.bass_core_rust import IndirectOffsetOnAxis
from contextlib import ExitStack

F32 = mybir.dt.float32
BF16 = mybir.dt.bfloat16
U32 = mybir.dt.uint32
AF = mybir.ActivationFunctionType
ALU = mybir.AluOpType
AX = mybir.AxisListType
BF16NP = ml_dtypes.bfloat16

T, D, H = 8192, 2048, 16
NCORE = 8
STRIP = T // NCORE
CHK = 128
NCH = STRIP // CHK
KT = D // 128
REFI = 63
SCALE = 128.0 ** -0.5
C0F = 20.0
GA = 1.5957691216057308
GB = 0.07135481283462803

ENGINES = ("sync", "tensor", "vector", "scalar", "gpsimd")
_t = np.ascontiguousarray

PHASES = dict(A=True, X1=True, C=True, F0=True, F1=True, F2=True)
DBGRAW = False
TAPS = []



def prep_inputs(inp):
    x0 = np.asarray(inp["L0_attn_hidden_states"], np.float32).reshape(T, D)
    g = lambda k: np.asarray(inp[k], np.float32)

    sh = {}
    for nm, key in (("Wq", "L0_attn_Wq"), ("Wk", "L0_attn_Wk"),
                    ("Wv", "L0_attn_Wv"), ("Ww", "L0_attn_Ww")):
        w = g(key)
        sh[nm] = _t(w.reshape(KT, 128, KT, 128).transpose(2, 1, 0, 3))
    for nm, key in (("cq", "L0_attn_cq"), ("ck", "L0_attn_ck"),
                    ("cv", "L0_attn_cv")):
        sh[nm] = _t(g(key).reshape(KT, 128, 4).transpose(1, 0, 2).reshape(128, KT * 4))
    sh["Wf1"] = _t(g("L0_attn_Wf1").reshape(KT, 128, 128).transpose(1, 0, 2))
    sh["Wf2"] = g("L0_attn_Wf2")
    sh["Wb"] = _t(g("L0_attn_Wb").reshape(KT, 128, H).transpose(1, 0, 2))
    sh["Wg1"] = _t(g("L0_attn_Wg1").reshape(KT, 128, 128).transpose(1, 0, 2))
    sh["Wg2"] = g("L0_attn_Wg2")
    sh["bg2"] = _t(g("L0_attn_bg2").reshape(1, D))
    sh["nw"] = _t(g("L0_attn_norm_w").reshape(128, 1))
    sh["Wo"] = _t(g("L0_attn_Wo").reshape(KT, 128, KT, 128).transpose(1, 0, 2, 3))
    sh["rtr"] = _t(g("L0_ffn_w_router").T.reshape(KT, 128, 3).transpose(1, 0, 2))
    sh["b0"] = _t(g("L0_ffn_simple_bias").reshape(KT, 128).T)
    sh["wdn"] = _t(g("L0_ffn_w_down").T.reshape(KT, 128, 32).transpose(1, 0, 2))
    sh["wup"] = _t(g("L0_ffn_w_up").T.reshape(32, KT, 128))
    for nm, key in (("l1s", "L0_ffn_ln1_scale"), ("l1b", "L0_ffn_ln1_bias"),
                    ("l2s", "L0_ffn_ln2_scale"), ("l2b", "L0_ffn_ln2_bias")):
        sh[nm] = _t(g(key).reshape(KT, 128).T)
    wo = g("L0_ffn_wo")
    sh["fwo"] = _t(wo.reshape(KT, 128, KT, 128).transpose(0, 3, 2, 1))
    sh["ff1"] = _t(g("L0_ffn_w_ff1").reshape(64, 128, KT, 128).transpose(0, 3, 2, 1))
    sh["ff2"] = _t(g("L0_ffn_w_ff2").reshape(KT, 128, 4, 16, 128).transpose(
        0, 2, 4, 3, 1))

    wq, wk, wv = g("L0_ffn_wq"), g("L0_ffn_wk"), g("L0_ffn_wv")
    maps = []
    for c in range(NCORE):
        m = dict(sh)
        m["x_strip"] = _t(x0[c * STRIP:(c + 1) * STRIP])
        halo = np.zeros((128, D), np.float32)
        if c > 0:
            halo[125:128] = x0[c * STRIP - 3:c * STRIP]
        m["x_halo"] = halo
        mu = np.zeros((128, NCORE), np.float32)
        mu[:, :c] = 1.0
        m["mu"] = mu
        ci = np.zeros((128, KT), np.uint32)
        for kt in range(KT):
            ci[:, kt] = (kt // 2) * D + c * 256 + (kt % 2) * 128 + np.arange(128)
        m["ctxidx"] = ci
        for nm, w in (("fwq", wq), ("fwk", wk), ("fwv", wv)):
            ws = w[256 * c:256 * (c + 1)]
            m[nm] = _t(ws.reshape(2, 128, KT, 128).transpose(3, 2, 0, 1))
        maps.append(m)
    return maps



class Sem:
    __slots__ = ("h", "n", "name")

    def __init__(self, ctx, nc, name):
        self.h = ctx.enter_context(nc.semaphore(name))
        self.n = 0
        self.name = name


class Em:
    def __init__(self, nc, ctx):
        self.nc = nc
        self.ctx = ctx
        self.ops = {e: [] for e in ENGINES}

    def sem(self, name):
        return Sem(self.ctx, self.nc, name)

    def q(self, eng, fn, inc=None, k=1):
        incs = []
        if inc is not None:
            incs = [(inc, k)] if isinstance(inc, Sem) else list(inc)
        for s, kk in incs:
            s.n += kk

        def thunk(e, fn=fn, incs=tuple(incs)):
            inst = fn(e)
            for s, kk in incs:
                inst.then_inc(s.h, kk)

        self.ops[eng].append(thunk)

    def wait(self, eng, s, val=None):
        v = s.n if val is None else val
        if v > 0:
            self.ops[eng].append(lambda e, h=s.h, v=v: e.wait_ge(h, v))

    def barrier(self, name, engines=ENGINES):
        prog = getattr(self, "progress_sems", [])
        for e in engines:
            for s in prog:
                self.wait(e, s)
        b = self.sem("bar_" + name)
        for e in engines:
            self.ops[e].append(lambda en, h=b.h: en.nop().then_inc(h, 1))
            b.n += 1
        tgt = b.n
        for e in engines:
            self.wait(e, b, tgt)



EXPECTED_IN_SHAPES = {}


def build_program(taps=()):
    nc = bass.Bass("TRN2", target_bir_lowering=False)
    gctx = ExitStack()
    io = {}

    def din(name, shape, dt=F32):
        EXPECTED_IN_SHAPES[name] = tuple(shape)
        io[name] = nc.dram_tensor(name, list(shape), dt, kind="ExternalInput").ap()

    def dout(name, shape, dt=F32):
        io[name] = nc.dram_tensor(name, list(shape), dt, kind="ExternalOutput").ap()

    def dint(name, shape, dt=F32):
        io[name] = nc.dram_tensor(name, list(shape), dt, kind="Internal").ap()

    din("x_strip", (STRIP, D)); din("x_halo", (128, D)); din("mu", (128, NCORE))
    din("ctxidx", (128, KT), U32)
    for nm in ("Wq", "Wk", "Wv", "Ww"):
        din(nm, (KT, 128, KT, 128))
    din("Wo", (128, KT, KT, 128))
    for nm in ("cq", "ck", "cv"):
        din(nm, (128, KT * 4))
    din("Wf1", (128, KT, 128)); din("Wf2", (128, D)); din("Wb", (128, KT, H))
    din("Wg1", (128, KT, 128)); din("Wg2", (128, D)); din("bg2", (1, D))
    din("nw", (128, 1)); din("rtr", (128, KT, 3)); din("b0", (128, KT))
    din("wdn", (128, KT, 32)); din("wup", (32, KT, 128))
    for nm in ("l1s", "l1b", "l2s", "l2b"):
        din(nm, (128, KT))
    din("fwo", (KT, 128, KT, 128))
    din("ff1", (64, 128, KT, 128)); din("ff2", (KT, 4, 128, 16, 128))
    for nm in ("fwq", "fwk", "fwv"):
        din(nm, (128, KT, 2, 128))

    dout("y_strip", (STRIP, D))

    for nm in ("st_q", "st_k", "st_Kt", "st_Vt", "st_U", "st_A", "st_E", "st_SIG"):
        dint(nm, (NCH, 128, D), BF16)
    dint("st_rows", (NCH, 2, D))
    dint("st_raw", (128, KT * 1027), BF16)
    dint("st_hln", (128, KT * (STRIP + 128)), BF16)
    dint("st_dbg2", (128, 64))
    dint("st_dbg3", (NCH, 128, D), BF16)
    dint("st_dbg4", (NCH, 128, D), BF16)
    dint("cc_in", (258, D)); dint("cc_out", (NCORE * 258, D))
    dint("cc_xn_in", (D, STRIP), BF16); dint("cc_xn_out", (NCORE * D, STRIP), BF16)
    dint("cc_ctx_in", (D, STRIP), BF16)
    dint("cc_ctx_out", (NCORE * D, STRIP), BF16)
    dint("sp_x1", (NCH, KT, 128, 128))
    dint("sp_hs", (KT, 128, STRIP), BF16)
    dint("sp_xt", (KT, 128, STRIP), BF16)
    dint("sp_lr", (KT, 128, STRIP), BF16)
    dint("sp_x2", (KT, 128, STRIP))
    dint("st_ffq", (2, 128, T), BF16)
    dint("st_ffk", (2, 128, T), BF16)
    dint("st_ffv", (2, 128, T), BF16)
    dint("st_a", (64, 128, STRIP), BF16)

    cst = {}
    cst["ident"] = nc.inline_tensor(np.eye(128, dtype=BF16NP), name="c_id").ap()
    cst["triu"] = nc.inline_tensor(
        np.triu(np.ones((128, 128), np.float32)), name="c_triu").ap()
    s_ = np.arange(128)[:, None]
    t_ = np.arange(128)[None, :]
    lneg = (s_ <= REFI).astype(np.float32) - (s_ <= t_).astype(np.float32)
    cst["lneg"] = nc.inline_tensor(_t(lneg), name="c_lneg").ap()
    ext = np.zeros((128, 65), np.float32)
    ext[:, 0] = -(np.arange(128) > REFI).astype(np.float32)
    ext[:, 64] = -(np.arange(128) <= REFI).astype(np.float32)
    cst["gext"] = nc.inline_tensor(ext, name="c_gext").ap()
    cst["ones_r"] = nc.inline_tensor(np.ones((1, 128), np.float32), name="c_or").ap()
    o65 = np.zeros((65, 128), np.float32); o65[0] = 1.0; o65[64] = 1.0
    cst["ones_r65"] = nc.inline_tensor(o65, name="c_or65").ap()
    cst["ones_r16"] = nc.inline_tensor(np.ones((1, 128), BF16NP), name="c_or16").ap()
    cst["ones_c16"] = nc.inline_tensor(np.ones((128, 1), BF16NP), name="c_oc16").ap()
    dm = np.zeros((128, 4 * 512), BF16NP)
    kv_ = np.arange(128)[:, None]
    ql_ = np.arange(512)[None, :]
    for j in range(4):
        dm[:, j * 512:(j + 1) * 512] = (ql_ >= 128 * j + kv_).astype(BF16NP)
    cst["dmask"] = nc.inline_tensor(dm, name="c_dmask").ap()
    cst["ones_c32"] = nc.inline_tensor(np.ones((128, 1), np.float32),
                                       name="c_oc32").ap()
    cst["ident32"] = nc.inline_tensor(np.eye(128, dtype=np.float32),
                                      name="c_id32").ap()

    em = Em(nc, gctx)

    gsb = lambda n, s, d=F32: gctx.enter_context(nc.sbuf_tensor(n, list(s), d))
    P1 = gsb("P1", (128, 2048)); P2 = gsb("P2", (128, 2048))
    P3 = gsb("P3", (128, 2048)); P4 = gsb("P4", (128, 2048))
    B1 = gsb("B1", (128, 2048), BF16); B2 = gsb("B2", (128, 2048), BF16)
    B3 = gsb("B3", (128, 2048), BF16); B4 = gsb("B4", (128, 2048), BF16)
    B5 = gsb("B5", (128, 2048), BF16)
    rowA = gsb("rowA", (65, 2048)); rowB = rowA
    rowC = gsb("rowC", (1, 2048)); rowD = gsb("rowD", (65, 2048))
    colA = gsb("colA", (128, 64)); colB = gsb("colB", (128, 64))
    beta = gsb("beta", (128, NCH * H))
    onesR = gsb("onesR", (1, 128)); onesR16 = gsb("onesR16", (1, 128), BF16)
    onesR65 = gsb("onesR65", (65, 128))
    onesC16 = gsb("onesC16", (128, 1), BF16)
    idS = gsb("idS", (128, 128), BF16)
    triuS = gsb("triuS", (128, 128))
    lnegS = gsb("lnegS", (128, 128)); gextS = gsb("gextS", (128, 65))
    t1S = gsb("t1S", (128, 128), BF16); t2S = gsb("t2S", (128, 128), BF16)
    onesC32 = gsb("onesC32", (128, 1)); idS32 = gsb("idS32", (128, 128))

    G = dict(nc=nc, em=em, io=io, cst=cst, gctx=gctx,
             P1=P1, P2=P2, P3=P3, P4=P4, B1=B1, B2=B2, B3=B3, B4=B4, B5=B5,
             rowA=rowA, rowB=rowB, rowC=rowC, rowD=rowD,
             colA=colA, colB=colB, beta=beta, onesR=onesR, onesR16=onesR16,
             onesC16=onesC16, idS=idS, triuS=triuS, lnegS=lnegS, gextS=gextS,
             t1S=t1S, t2S=t2S, onesC32=onesC32, idS32=idS32, onesR65=onesR65)

    for s in ("ld", "gw", "mm", "ev", "sc", "sg", "pr", "cc", "fin"):
        G[s] = em.sem(s)
    em.progress_sems = [G[s] for s in ("ld", "gw", "mm", "ev", "sc", "sg", "cc")]

    colZ = gsb("colZ", (128, 4))
    nc.const_aps.register(F32, 1e-5, colZ[:, 0:1])
    nc.const_aps.register(F32, 1e-6, colZ[:, 1:2])
    nc.const_aps.register(F32, -C0F, colZ[:, 2:3])
    czv = np.zeros((128, 4), np.float32)
    czv[:, 0] = 1e-5; czv[:, 1] = 1e-6; czv[:, 2] = -C0F
    cst["colz"] = nc.inline_tensor(czv, name="c_colz").ap()
    em.q("sync", lambda s: s.dma_start(out=colZ[:], in_=cst["colz"]),
         inc=G["ld"], k=16)
    em.wait("scalar", G["ld"])
    em.wait("vector", G["ld"])

    ld = G["ld"]
    for dst, src in ((onesR, "ones_r"), (onesR16, "ones_r16"),
                     (onesC16, "ones_c16"), (idS, "ident"), (triuS, "triu"),
                     (lnegS, "lneg"), (gextS, "gext"), (onesC32, "ones_c32"),
                     (idS32, "ident32"), (onesR65, "ones_r65")):
        em.q("sync", lambda s, d=dst, c=cst[src]: s.dma_start(out=d[:], in_=c),
             inc=ld, k=16)

    if PHASES["A"]:
        phaseA(G)
    if PHASES["X1"]:
        phaseX1(G)
    if PHASES["C"]:
        phaseC(G)
    if PHASES["F0"]:
        phaseF0(G)
    if PHASES["F1"]:
        phaseF1(G)
    if PHASES["F2"]:
        phaseF2(G)

    if taps:
        em.barrier("taps")
        ts = em.sem("tapsem")
        for nm in taps:
            ap = io[nm]
            tp = nc.dram_tensor("tap_" + nm, list(ap.shape), ap.dtype,
                                kind="ExternalOutput").ap()
            em.q("gpsimd", lambda g, o=tp, i=ap: g.dma_start(out=o, in_=i),
                 inc=ts, k=16)
        em.wait("gpsimd", ts)

    blk = gctx.enter_context(nc.Block())
    for eng in ENGINES:
        ops = em.ops[eng]
        getattr(blk, eng)(lambda e, ops=ops: [op(e) for op in ops])
    gctx.close()
    return nc



def phaseA(G):
    nc, em, io, cst = G["nc"], G["em"], G["io"], G["cst"]
    ld, gw, mm, ev, sc, sg, pr = (G[k] for k in
                                  ("ld", "gw", "mm", "ev", "sc", "sg", "pr"))
    P1, P2, P3, P4 = G["P1"], G["P2"], G["P3"], G["P4"]
    B1, B2, B3 = G["B1"], G["B2"], G["B3"]
    rowA, rowC, rowD, colA = G["rowA"], G["rowC"], G["rowD"], G["colA"]
    beta = G["beta"]

    actx = ExitStack()
    asb = lambda n, s, d=F32: actx.enter_context(nc.sbuf_tensor(n, list(s), d))
    aps = lambda n, s, d=F32: actx.enter_context(nc.psum_tensor(n, list(s), d))

    HW = STRIP + 128
    hlnT = asb("hlnT", (128, KT * HW), BF16)
    SkR = asb("SkR", (128, 2048))
    SvR = asb("SvR", (128, 2048))
    G["SkR"], G["SvR"] = SkR, SvR

    em.q("vector", lambda v: v.memset(SkR[:], 0.0), inc=ev)
    em.q("vector", lambda v: v.memset(SvR[:], 0.0), inc=ev)

    for nm in ("Wf1", "Wg1"):
        em.q("gpsimd", lambda g, n=nm: g.dma_start(
            out=smallw[n][:].rearrange("p (a c) -> p a c", a=KT),
            in_=io[n]), inc=gw, k=16)
    for nm in ("Wf2", "Wg2", "bg2"):
        em.q("gpsimd", lambda g, n=nm: g.dma_start(out=smallw[n][:], in_=io[n]),
             inc=gw, k=16)
    em.q("gpsimd", lambda g: g.dma_start(
        out=smallw["Wb"][:].rearrange("p (a c) -> p a c", a=KT),
        in_=io["Wb"]), inc=gw, k=16)
    for nm in ("cq", "ck", "cv"):
        em.q("sync", lambda s, n=nm: s.dma_start(out=convw[n][:], in_=io[n]),
             inc=ld, k=16)

    convw = {}
    for nm in ("cq", "ck", "cv"):
        convw[nm] = asb("sw_" + nm, (128, KT * 4))
    a1ctx = ExitStack()
    a1ps = lambda n, s, d=F32: a1ctx.enter_context(nc.psum_tensor(n, list(s), d))
    a1sb = lambda n, s, d=F32: a1ctx.enter_context(nc.sbuf_tensor(n, list(s), d))
    smallw = {}
    for nm, shape in (("Wf1", (128, KT * 128)), ("Wf2", (128, D)),
                      ("Wb", (128, KT * H)), ("Wg1", (128, KT * 128)),
                      ("Wg2", (128, D)), ("bg2", (1, D))):
        smallw[nm] = a1sb("sw_" + nm, shape, BF16)
    p_tr = a1ps("pA_tr", (128, 1024), BF16)
    p_g1 = a1ps("pA_g1", (128, 128))
    p_gg = a1ps("pA_gg", (128, 512))
    p_cum = a1ps("pA_cum", (128, 1024))
    p_row = a1ps("pA_row", (65, 512))
    p_bet = a1ps("pA_bet", (128, H))

    xrow, xnt = P1, P2
    gsp, gtmp = P3, P4
    xn16, At16, SIG16 = B1, B2, B3
    mvc = colA

    def ln_tile(j, is_halo):
        em.wait("scalar", sg)
        em.wait("vector", sg)
        em.wait("sync", ev)
        em.wait("sync", sc)
        src = (io["x_halo"] if is_halo
               else io["x_strip"][j * 128:(j + 1) * 128, :])
        em.q("sync", lambda s, src=src: s.dma_start(out=xrow[:], in_=src),
             inc=ld, k=16)
        em.wait("vector", ld)
        em.wait("scalar", ld)
        em.q("vector", lambda v: v.reduce_sum(mvc[:, 2:3], xrow[:], axis=AX.X),
             inc=ev)
        em.q("scalar", lambda s: s.activation(xnt[:], xrow[:], AF.Square), inc=sc)
        em.wait("vector", sc)
        em.q("vector", lambda v: v.reduce_sum(mvc[:, 3:4], xnt[:], axis=AX.X),
             inc=ev)
        em.wait("vector", ev)
        em.q("vector", lambda v: v.tensor_scalar(
            mvc[:, 0:1], mvc[:, 2:3], 1.0 / D, None, op0=ALU.mult), inc=ev)
        em.wait("vector", ev)
        em.q("vector", lambda v: v.tensor_mul(mvc[:, 4:5], mvc[:, 0:1], mvc[:, 0:1]),
             inc=ev)
        em.q("vector", lambda v: v.tensor_scalar(
            mvc[:, 5:6], mvc[:, 3:4], 1.0 / D, None, op0=ALU.mult), inc=ev)
        em.wait("vector", ev)
        em.q("vector", lambda v: v.tensor_sub(mvc[:, 1:2], mvc[:, 5:6], mvc[:, 4:5]),
             inc=ev)
        em.wait("scalar", ev)
        em.q("scalar", lambda s: s.activation(mvc[:, 1:2], mvc[:, 1:2], AF.Sqrt,
                                              bias=1e-5), inc=sc)
        em.wait("vector", sc)
        em.q("vector", lambda v: v.reciprocal(mvc[:, 1:2], mvc[:, 1:2]), inc=ev)
        em.wait("vector", ev)
        em.q("vector", lambda v: v.tensor_scalar(
            xnt[:], xrow[:], mvc[:, 0:1], mvc[:, 1:2],
            op0=ALU.subtract, op1=ALU.mult), inc=ev)
        em.wait("vector", ev)
        em.q("vector", lambda v: v.tensor_copy(xn16[:], xnt[:]), inc=ev)
        em.wait("tensor", ev)
        col0 = 0 if is_halo else 128 + j * 128
        h3 = hlnT[:].rearrange("p (a w) -> p a w", a=KT)
        for b in range(2):
            for k in range(8):
                kt = b * 8 + k
                em.q("tensor", lambda t, kt=kt, k=k: t.transpose(
                    p_tr[:, k * 128:(k + 1) * 128],
                    xn16[:, kt * 128:(kt + 1) * 128], G["idS"][:]),
                    inc=(mm if k == 7 else None))
            em.wait("vector", mm)
            em.q("vector", lambda v, b=b, col0=col0: v.tensor_copy(
                h3[:, b * 8:(b + 1) * 8, col0:col0 + 128],
                p_tr[:].rearrange("p (a w) -> p a w", a=8)), inc=ev)
            em.wait("tensor", ev)

    def gpath_tile(j):
        col0 = 128 + j * 128
        h3 = hlnT[:].rearrange("p (a w) -> p a w", a=KT)
        hj = h3[:, :, col0:col0 + 128]
        em.wait("tensor", G["gw"])
        for kt in range(KT):
            em.q("tensor", lambda t, kt=kt: t.matmul(
                p_g1[:], smallw["Wf1"][:, kt * 128:(kt + 1) * 128], hj[:, kt, :],
                start=(kt == 0), stop=(kt == KT - 1)),
                inc=(mm if kt == KT - 1 else None))
        em.wait("vector", mm)
        em.q("vector", lambda v: v.tensor_copy(G["t1S"][:], p_g1[:]), inc=ev)
        em.wait("tensor", ev)
        for kt in range(KT):
            em.q("tensor", lambda t, kt=kt: t.matmul(
                p_bet[:], hj[:, kt, :],
                smallw["Wb"][:, kt * H:(kt + 1) * H],
                start=(kt == 0), stop=(kt == KT - 1)),
                inc=(mm if kt == KT - 1 else None))
        em.wait("scalar", mm)
        em.q("scalar", lambda s, j=j: s.activation(
            beta[:, j * H:(j + 1) * H], p_bet[:], AF.Sigmoid), inc=sc)
        em.wait("tensor", sc)
        for kt in range(KT):
            em.q("tensor", lambda t, kt=kt: t.matmul(
                p_g1[:], smallw["Wg1"][:, kt * 128:(kt + 1) * 128], hj[:, kt, :],
                start=(kt == 0), stop=(kt == KT - 1)),
                inc=(mm if kt == KT - 1 else None))
        em.wait("vector", mm)
        em.q("vector", lambda v: v.tensor_copy(G["t2S"][:], p_g1[:]), inc=ev)
        for q4 in range(4):
            em.q("tensor", lambda t, q4=q4: t.matmul(
                p_gg[:], G["t1S"][:], smallw["Wf2"][:, q4 * 512:(q4 + 1) * 512],
                start=True, stop=True), inc=mm)
            em.wait("scalar", mm)
            em.q("scalar", lambda s, q4=q4: s.activation(
                gtmp[:, q4 * 512:(q4 + 1) * 512], p_gg[:], AF.Exp, scale=-1.0),
                inc=sc)
            em.wait("tensor", sc)
        em.q("scalar", lambda s: s.activation(gsp[:], gtmp[:], AF.Ln, bias=1.0),
             inc=sc)
        em.wait("tensor", ev)
        for q4 in range(4):
            em.q("tensor", lambda t, q4=q4: t.matmul(
                p_gg[:], G["t2S"][:], smallw["Wg2"][:, q4 * 512:(q4 + 1) * 512],
                start=True, stop=False))
            em.q("tensor", lambda t, q4=q4: t.matmul(
                p_gg[:], G["onesR16"][:], smallw["bg2"][:, q4 * 512:(q4 + 1) * 512],
                start=False, stop=True), inc=mm)
            em.wait("scalar", mm)
            em.q("scalar", lambda s, q4=q4: s.activation(
                SIG16[:, q4 * 512:(q4 + 1) * 512], p_gg[:], AF.Sigmoid), inc=sc)
            em.wait("tensor", sc)
        em.wait("sync", sc)
        em.q("sync", lambda s, j=j: s.dma_start(out=io["st_SIG"][j], in_=SIG16[:]),
             inc=sg, k=16)
        em.wait("tensor", sc)
        for half in range(2):
            for q2 in range(2):
                em.q("tensor", lambda t, half=half, q2=q2: t.matmul(
                    p_cum[:, q2 * 512:(q2 + 1) * 512], G["lnegS"][:],
                    gsp[:, half * 1024 + q2 * 512:half * 1024 + (q2 + 1) * 512],
                    start=True, stop=True), inc=(mm if q2 == 1 else None))
            em.wait("scalar", mm)
            em.q("scalar", lambda s, half=half: s.activation(
                At16[:, half * 1024:(half + 1) * 1024], p_cum[:], AF.Exp), inc=sc)
            em.q("scalar", lambda s, half=half: s.activation(
                xn16[:, half * 1024:(half + 1) * 1024], p_cum[:], AF.Exp,
                scale=-1.0), inc=sc)
            em.wait("tensor", sc)
        em.wait("sync", sc)
        em.q("sync", lambda s, j=j: s.dma_start(out=io["st_A"][j], in_=At16[:]),
             inc=sg, k=16)
        em.q("sync", lambda s, j=j: s.dma_start(out=io["st_E"][j], in_=xn16[:]),
             inc=sg, k=16)
        for q4 in range(4):
            em.q("tensor", lambda t, q4=q4: t.matmul(
                p_row[:], G["gextS"][:], gsp[:, q4 * 512:(q4 + 1) * 512],
                start=True, stop=True), inc=mm)
            em.wait("scalar", mm)
            em.q("scalar", lambda s, q4=q4: s.activation(
                rowA[:, q4 * 512:(q4 + 1) * 512], p_row[:], AF.Exp), inc=sc)
            em.wait("tensor", sc)
        em.wait("sync", sc)
        em.q("sync", lambda s, j=j: s.dma_start(out=io["st_rows"][j][0:1],
                                                in_=rowA[0:1, :]), inc=sg, k=16)
        em.q("sync", lambda s, j=j: s.dma_start(out=io["st_rows"][j][1:2],
                                                in_=rowA[64:65, :]), inc=sg, k=16)
        em.wait("vector", sc)
        em.q("vector", lambda v: v.tensor_copy(rowD[0:1, :], rowA[64:65, :]),
             inc=ev)
        em.wait("vector", ev)
        if j == 0:
            em.q("vector", lambda v: v.tensor_mul(rowC[:], rowA[0:1, :],
                                                  rowD[0:1, :]), inc=ev)
        else:
            em.q("vector", lambda v: v.tensor_mul(gtmp[0:1, :], rowA[0:1, :],
                                                  rowD[0:1, :]), inc=ev)
            em.wait("vector", ev)
            em.q("vector", lambda v: v.tensor_mul(rowC[:], rowC[:], gtmp[0:1, :]),
                 inc=ev)

    ln_tile(0, True)
    for j in range(NCH):
        ln_tile(j, False)
        gpath_tile(j)
    if DBGRAW:
        em.wait("sync", G["ev"])
        em.wait("sync", G["sc"])
        em.q("sync", lambda s: s.dma_start(out=io["st_hln"], in_=hlnT[:]),
             inc=G["sg"], k=16)
    em.barrier("endA1")
    a1ctx.close()

    phaseA2(G, actx, asb, aps, hlnT, convw)
    gw, sg2 = G["gw"], G["sg"]
    em.wait("gpsimd", G["ev"])
    em.wait("gpsimd", G["sg"])
    em.q("gpsimd", lambda g: g.dma_start(out=io["cc_in"][0:128, :], in_=SkR[:]),
         inc=gw, k=16)
    em.q("gpsimd", lambda g: g.dma_start(out=io["cc_in"][128:256, :], in_=SvR[:]),
         inc=gw, k=16)
    em.q("gpsimd", lambda g: g.dma_start(out=io["cc_in"][256:257, :],
                                         in_=G["rowC"][:]), inc=gw, k=16)
    em.wait("gpsimd", gw)
    em.barrier("endA")
    actx.close()


def phaseA2(G, actx, asb, aps, hlnT, convw):
    nc, em, io = G["nc"], G["em"], G["io"]
    ld, gw, mm, ev, sc, sg, pr = (G[k] for k in
                                  ("ld", "gw", "mm", "ev", "sc", "sg", "pr"))
    P1, P2, P3, P4 = G["P1"], G["P2"], G["P3"], G["P4"]
    B1, B2, B3, B4, B5 = G["B1"], G["B2"], G["B3"], G["B4"], G["B5"]
    rowB, colA = G["rowB"], G["colA"]
    beta, SkR, SvR = G["beta"], G["SkR"], G["SvR"]

    raw = asb("rawA", (128, KT * 1027), BF16)
    gcolsA = asb("gcolsA", (128, 32))
    cvo = asb("cvoA", (128, KT * 1024), BF16)
    wsl = asb("wslA", (128, 2 * 2048), BF16)
    p_h = aps("pA_h", (128, 128))
    p_s0 = aps("pA_s0", (128, 512))
    p_s1 = aps("pA_s1", (128, 512))
    p_tp = aps("pA_tp", (128, 1024), BF16)
    p_c1 = aps("pA_c1", (128, 512))
    p_c2 = aps("pA_c2", (128, 512))
    p_gb = aps("pA_gb", (128, 512))
    prj = em.sem("a_prj")

    h3 = hlnT[:].rearrange("p (a w) -> p a w", a=KT)
    raw3 = raw[:].rearrange("p (a w) -> p a w", a=KT)
    cvo3 = cvo[:].rearrange("p (a w) -> p a w", a=KT)

    def proj_conv(wname, cname):
        em.wait("vector", sg)
        for ot in range(KT):
            half = ot % 2
            dst = wsl[:, half * 2048:(half + 1) * 2048]
            if prj.n >= 2:
                em.wait("gpsimd", prj, prj.n - 1)
            em.q("gpsimd", lambda g, ot=ot, dst=dst, w=wname: g.dma_start(
                out=dst.rearrange("p (a c) -> p a c", a=KT),
                in_=io[w][ot]), inc=gw, k=16)
            em.wait("tensor", gw)
            for kt in range(KT):
                em.q("tensor", lambda t, kt=kt, dst=dst: t.matmul(
                    p_h[:], dst[:, kt * 128:(kt + 1) * 128], h3[:, kt, 0:128],
                    start=(kt == 0), stop=(kt == KT - 1)),
                    inc=(mm if kt == KT - 1 else None))
            for th, ps in ((0, p_s0), (1, p_s1)):
                for kt in range(KT):
                    inc = None
                    if kt == KT - 1:
                        inc = prj if th == 1 else mm
                    em.q("tensor", lambda t, kt=kt, th=th, ps=ps, dst=dst: t.matmul(
                        ps[:], dst[:, kt * 128:(kt + 1) * 128],
                        h3[:, kt, 128 + th * 512:128 + (th + 1) * 512],
                        start=(kt == 0), stop=(kt == KT - 1)),
                        inc=inc)
            em.wait("vector", mm)
            em.wait("vector", prj)
            em.q("vector", lambda v, ot=ot: v.tensor_copy(
                raw3[:, ot, 0:3], p_h[:, 125:128]), inc=ev)
            em.q("vector", lambda v, ot=ot: v.tensor_copy(
                raw3[:, ot, 3:515], p_s0[:]), inc=ev)
            em.q("vector", lambda v, ot=ot: v.tensor_copy(
                raw3[:, ot, 515:1027], p_s1[:]), inc=ev)
            em.wait("tensor", ev)
        if DBGRAW and wname == "Wq":
            em.wait("sync", ev)
            em.q("sync", lambda s: s.dma_start(out=io["st_raw"], in_=raw[:]),
                 inc=sg, k=16)
        cw = convw[cname]
        em.wait("vector", sg)
        em.wait("scalar", sg)
        em.wait("vector", ev)
        for ot in range(KT):
            em.q("vector", lambda v, ot=ot: v.tensor_scalar(
                P1[:, 0:1024], raw3[:, ot, 0:1024], cw[:, ot * 4:ot * 4 + 1],
                None, op0=ALU.mult))
            for tap in range(1, 4):
                em.q("vector", lambda v, ot=ot, tap=tap: v.scalar_tensor_tensor(
                    P1[:, 0:1024], raw3[:, ot, tap:tap + 1024],
                    cw[:, ot * 4 + tap:ot * 4 + tap + 1], P1[:, 0:1024],
                    op0=ALU.mult, op1=ALU.add), inc=(ev if tap == 3 else None))
            em.wait("scalar", ev)
            em.q("scalar", lambda s, ot=ot: s.activation(
                cvo3[:, ot, :], P1[:, 0:1024], AF.Silu), inc=sc)
            em.wait("vector", sc)

    def stage_chan(stname):
        em.wait("sync", G["sc"])
        for j in range(NCH):
            em.q("sync", lambda s, j=j, st=stname: s.dma_start(
                out=io[st][j].rearrange("p (a w) -> p a w", a=KT),
                in_=cvo[:].rearrange("p (a c w) -> p a c w", a=KT, c=NCH)[
                    :, :, j, :]), inc=sg, k=16)

    def transpose_chunk(j, dst16):
        cvo4 = cvo[:].rearrange("p (a c w) -> p a c w", a=KT, c=NCH)
        em.wait("tensor", sc)
        for b in range(2):
            for k in range(8):
                kt = b * 8 + k
                em.q("tensor", lambda t, kt=kt, k=k, j=j: t.transpose(
                    p_tp[:, k * 128:(k + 1) * 128], cvo4[:, kt, j, :],
                    G["idS"][:]), inc=(mm if k == 7 else None))
            em.wait("vector", mm)
            em.q("vector", lambda v, b=b: v.tensor_copy(
                dst16[:].rearrange("p (a w) -> p a w", a=KT)[:, b * 8:(b + 1) * 8, :],
                p_tp[:].rearrange("p (a w) -> p a w", a=8)), inc=ev)
            em.wait("tensor", ev)

    proj_conv("Wq", "cq")
    stage_chan("st_q")
    proj_conv("Wk", "ck")
    stage_chan("st_k")
    for j in range(NCH):
        transpose_chunk(j, B1)
        em.wait("sync", ev)
        em.q("sync", lambda s, j=j: s.dma_start(out=io["st_Kt"][j], in_=B1[:]),
             inc=sg, k=16)
        em.wait("vector", sg)
    proj_conv("Wv", "cv")
    for j in range(NCH):
        transpose_chunk(j, B1)
        em.wait("sync", ev)
        em.q("sync", lambda s, j=j: s.dma_start(out=io["st_Vt"][j], in_=B1[:]),
             inc=sg, k=16)
        em.wait("vector", sg)
    proj_conv("Ww", "cv")
    for j in range(NCH):
        transpose_chunk(j, B1)
        em.wait("sync", mm)
        em.wait("sync", ev)
        em.wait("sync", sc)
        em.q("sync", lambda s, j=j: s.dma_start(out=B3[:], in_=io["st_E"][j]),
             inc=ld, k=16)
        em.q("sync", lambda s, j=j: s.dma_start(out=B4[:], in_=io["st_Kt"][j]),
             inc=ld, k=16)
        em.q("sync", lambda s, j=j: s.dma_start(out=B5[:], in_=io["st_Vt"][j]),
             inc=ld, k=16)
        em.q("sync", lambda s, j=j: s.dma_start(out=rowB[0:1, :],
                                                in_=io["st_rows"][j][0:1]),
             inc=ld, k=16)
        em.q("sync", lambda s, j=j: s.dma_start(out=rowB[64:65, :],
                                                in_=io["st_rows"][j][1:2]),
             inc=ld, k=16)
        em.wait("scalar", ev)
        em.q("scalar", lambda s: s.activation(P2[:], B1[:], AF.Square), inc=sc)
        em.wait("vector", sc)
        em.q("vector", lambda v: v.reduce_sum(
            colA[:, 0:16].rearrange("p (h o) -> p h o", o=1),
            P2[:].rearrange("p (h m) -> p h m", m=128), axis=AX.X), inc=ev)
        em.wait("scalar", ev)
        em.q("scalar", lambda s: s.activation(colA[:, 0:16], colA[:, 0:16],
                                              AF.Sqrt, bias=1e-6), inc=sc)
        em.wait("vector", sc)
        em.q("vector", lambda v: v.reciprocal(colA[:, 0:16], colA[:, 0:16]),
             inc=ev)
        em.wait("vector", ev)
        em.q("vector", lambda v, j=j: v.tensor_mul(
            colA[:, 0:16], colA[:, 0:16], beta[:, j * H:(j + 1) * H]), inc=ev)
        em.wait("vector", ld)
        em.wait("vector", ev)
        for h in range(H):
            em.q("vector", lambda v, h=h: v.tensor_scalar(
                P2[:, h * 128:(h + 1) * 128], B1[:, h * 128:(h + 1) * 128],
                colA[:, h:h + 1], None, op0=ALU.mult),
                inc=(ev if h == H - 1 else None))
        em.wait("vector", ev)
        em.q("vector", lambda v: v.tensor_mul(B2[:], P2[:], B3[:]), inc=ev)
        em.wait("sync", ev)
        em.q("sync", lambda s, j=j: s.dma_start(out=io["st_U"][j], in_=B2[:]),
             inc=sg, k=16)
        em.wait("tensor", ld)
        for q4 in range(4):
            em.q("tensor", lambda t, q4=q4: t.matmul(
                p_gb[:], G["onesR"][:], rowB[0:1, q4 * 512:(q4 + 1) * 512],
                start=True, stop=True), inc=mm)
            em.wait("vector", mm)
            em.q("vector", lambda v, q4=q4: v.tensor_copy(
                P3[:, q4 * 512:(q4 + 1) * 512], p_gb[:]), inc=ev)
            em.wait("tensor", ev)
            em.q("tensor", lambda t, q4=q4: t.matmul(
                p_gb[:], G["onesR65"][64:65, :], rowB[64:65, q4 * 512:(q4 + 1) * 512],
                start=True, stop=True), inc=mm)
            em.wait("vector", mm)
            em.q("vector", lambda v, q4=q4: v.tensor_copy(
                P4[:, q4 * 512:(q4 + 1) * 512], p_gb[:]), inc=ev)
            em.wait("tensor", ev)
        for h in range(H):
            em.q("tensor", lambda t, h=h: t.transpose(
                p_gb[:, 2 * h:2 * h + 1], rowB[0:1, h * 128:(h + 1) * 128],
                G["idS32"][0:1, 0:1]))
            em.q("tensor", lambda t, h=h: t.transpose(
                p_gb[:, 2 * h + 1:2 * h + 2], rowB[64:65, h * 128:(h + 1) * 128],
                G["idS32"][64:65, 64:65]), inc=(mm if h == H - 1 else None))
        em.wait("vector", mm)
        em.q("vector", lambda v: v.tensor_copy(gcolsA[:], p_gb[:, 0:32]), inc=ev)
        em.wait("tensor", ev)
        for hg in range(4):
            for hh in range(4):
                h = hg * 4 + hh
                em.q("tensor", lambda t, h=h, hh=hh: t.matmul(
                    p_c1[:, hh * 128:(hh + 1) * 128],
                    B4[:, h * 128:(h + 1) * 128], B2[:, h * 128:(h + 1) * 128],
                    start=True, stop=True), inc=(mm if hh == 3 else None))
            for hh in range(4):
                h = hg * 4 + hh
                em.q("tensor", lambda t, h=h, hh=hh: t.matmul(
                    p_c2[:, hh * 128:(hh + 1) * 128],
                    B2[:, h * 128:(h + 1) * 128], B5[:, h * 128:(h + 1) * 128],
                    start=True, stop=True), inc=(mm if hh == 3 else None))
            em.wait("vector", mm)
            sl = slice(hg * 512, (hg + 1) * 512)
            em.q("vector", lambda v, sl=sl: v.tensor_mul(
                SkR[:, sl], SkR[:, sl], P4[:, sl]))
            em.q("vector", lambda v, sl=sl: v.tensor_add(
                SkR[:, sl], SkR[:, sl], p_c1[:]))
            em.q("vector", lambda v, sl=sl: v.tensor_mul(
                SkR[:, sl], SkR[:, sl], P3[:, sl]))
            for hh in range(4):
                h = hg * 4 + hh
                em.q("vector", lambda v, h=h: v.tensor_scalar(
                    SvR[:, h * 128:(h + 1) * 128], SvR[:, h * 128:(h + 1) * 128],
                    gcolsA[:, 2 * h + 1:2 * h + 2], None, op0=ALU.mult))
            em.q("vector", lambda v, sl=sl: v.tensor_add(
                SvR[:, sl], SvR[:, sl], p_c2[:]))
            for hh in range(4):
                h = hg * 4 + hh
                em.q("vector", lambda v, h=h: v.tensor_scalar(
                    SvR[:, h * 128:(h + 1) * 128], SvR[:, h * 128:(h + 1) * 128],
                    gcolsA[:, 2 * h:2 * h + 1], None, op0=ALU.mult),
                    inc=(ev if hh == 3 else None))
            em.wait("tensor", ev)



def phaseX1(G):
    nc, em, io = G["nc"], G["em"], G["io"]
    ld, gw, mm, ev, sc, sg, cc = (G[k] for k in
                                  ("ld", "gw", "mm", "ev", "sc", "sg", "cc"))
    rowC, rowD = G["rowC"], G["rowD"]
    P1, P2, P3 = G["P1"], G["P2"], G["P3"]

    xctx = ExitStack()
    muS = xctx.enter_context(nc.sbuf_tensor("muS", [128, NCORE], F32))
    SkI = xctx.enter_context(nc.sbuf_tensor("SkI", [128, 2048], F32))
    SvI = xctx.enter_context(nc.sbuf_tensor("SvI", [128, 2048], F32))
    G["SkI"], G["SvI"] = SkI, SvI
    G["xctx"] = xctx
    x1ps = ExitStack()
    p_gb = x1ps.enter_context(nc.psum_tensor("pX_gb", [128, 512], F32))

    em.q("sync", lambda s: s.dma_start(out=muS[:], in_=io["mu"]), inc=ld, k=16)
    em.q("gpsimd", lambda g: g.collective_compute(
        "AllGather", ALU.bypass, replica_groups=[list(range(NCORE))],
        ins=[io["cc_in"].opt()], outs=[io["cc_out"].opt()]), inc=cc, k=1)
    em.wait("gpsimd", cc)
    em.q("vector", lambda v: v.memset(SkI[:], 0.0), inc=ev)
    em.q("vector", lambda v: v.memset(SvI[:], 0.0), inc=ev)
    for j in range(NCORE):
        base = j * 258
        em.q("gpsimd", lambda g, base=base: g.dma_start(
            out=P1[:], in_=io["cc_out"][base:base + 128, :]), inc=gw, k=16)
        em.q("gpsimd", lambda g, base=base: g.dma_start(
            out=P2[:], in_=io["cc_out"][base + 128:base + 256, :]), inc=gw, k=16)
        em.q("gpsimd", lambda g, base=base: g.dma_start(
            out=rowD[0:1, :], in_=io["cc_out"][base + 256:base + 257, :]),
            inc=gw, k=16)
        em.wait("vector", gw)
        em.wait("vector", ld)
        em.q("vector", lambda v, j=j: v.tensor_scalar(
            rowD[64:65, :], rowD[0:1, :], 1.0, muS[0:1, j:j + 1],
            op0=ALU.subtract, op1=ALU.mult), inc=ev)
        em.wait("vector", ev)
        em.q("vector", lambda v: v.tensor_scalar(
            rowD[64:65, :], rowD[64:65, :], 1.0, None, op0=ALU.add), inc=ev)
        em.wait("tensor", ev)
        for q4 in range(4):
            em.q("tensor", lambda t, q4=q4: t.matmul(
                p_gb[:], G["onesR65"][64:65, :], rowD[64:65, q4 * 512:(q4 + 1) * 512],
                start=True, stop=True), inc=mm)
            em.wait("vector", mm)
            em.q("vector", lambda v, q4=q4: v.tensor_copy(
                P3[:, q4 * 512:(q4 + 1) * 512], p_gb[:]), inc=ev)
            em.wait("tensor", ev)
        em.q("vector", lambda v: v.tensor_mul(SkI[:], SkI[:], P3[:]))
        em.q("vector", lambda v, j=j: v.scalar_tensor_tensor(
            P1[:], P1[:], muS[:, j:j + 1], SkI[:], op0=ALU.mult, op1=ALU.add))
        em.q("vector", lambda v: v.tensor_copy(SkI[:], P1[:]))
        em.q("vector", lambda v: v.tensor_mul(SvI[:], SvI[:], P3[:]))
        em.q("vector", lambda v, j=j: v.scalar_tensor_tensor(
            P2[:], P2[:], muS[:, j:j + 1], SvI[:], op0=ALU.mult, op1=ALU.add))
        em.q("vector", lambda v: v.tensor_copy(SvI[:], P2[:]), inc=ev)
        em.wait("gpsimd", ev)
    em.barrier("endX1")
    x1ps.close()



def phaseC(G):
    nc, em, io = G["nc"], G["em"], G["io"]
    ld, gw, mm, ev, sc, sg = (G[k] for k in ("ld", "gw", "mm", "ev", "sc", "sg"))
    P1, P2, P3, P4 = G["P1"], G["P2"], G["P3"], G["P4"]
    B1, B2, B3, B4, B5 = (G[k] for k in ("B1", "B2", "B3", "B4", "B5"))
    rowB, colA, colB = G["rowB"], G["colA"], G["colB"]
    SkI, SvI = G["SkI"], G["SvI"]

    cctx = ExitStack()
    csb = lambda n, s, d=F32: cctx.enter_context(nc.sbuf_tensor(n, list(s), d))
    B6 = csb("B6c", (128, 2048), BF16)
    B7 = csb("B7c", (128, 2048), BF16)
    cps = lambda n, s, d=F32: cctx.enter_context(nc.psum_tensor(n, list(s), d))
    WoS = csb("WoS", (128, KT * KT * 128), BF16)
    attnC = csb("attnC", (128, 2048), BF16)
    SkP16 = csb("SkP16", (128, 2048), BF16)
    SvP16 = csb("SvP16", (128, 2048), BF16)
    esx = csb("esC", (128, 128))
    aA16 = csb("aA16", (128, 128), BF16)
    aAT16 = csb("aAT16", (128, 128), BF16)
    UT16 = csb("UT16", (128, 128), BF16)
    Pm16 = csb("Pm16", (128, 128), BF16)
    W2m16 = csb("W2m16", (128, 128), BF16)
    gcols = csb("gcolsC", (128, 32))
    nwS = csb("nwS", (128, 1))

    p_P = cps("pC_P", (128, 128))
    p_s = cps("pC_s", (128, 128))
    p_o = cps("pC_o", (128, 128))
    p_t16 = cps("pC_t16", (128, 256), BF16)
    p_to = cps("pC_to", (128, 1024), BF16)
    p_c1 = cps("pC_c1", (128, 512))
    p_c2 = cps("pC_c2", (128, 512))
    p_gb = cps("pC_gb", (128, 512))

    em.q("gpsimd", lambda g: g.dma_start(
        out=WoS[:].rearrange("p (a b c) -> p a b c", a=KT, b=KT),
        in_=io["Wo"]), inc=gw, k=16)
    em.q("sync", lambda s: s.dma_start(out=nwS[:], in_=io["nw"]), inc=ld, k=16)
    em.wait("vector", gw)
    em.wait("vector", ld)
    em.q("vector", lambda v: v.tensor_scalar(
        WoS[:], WoS[:], nwS[:, 0:1], None, op0=ALU.mult), inc=ev)

    for j in range(NCH):
        em.wait("sync", mm)
        em.wait("sync", ev)
        em.wait("sync", sc)
        em.q("sync", lambda s, j=j: s.dma_start(out=B1[:], in_=io["st_q"][j]),
             inc=ld, k=16)
        em.q("sync", lambda s, j=j: s.dma_start(out=B2[:], in_=io["st_k"][j]),
             inc=ld, k=16)
        em.q("sync", lambda s, j=j: s.dma_start(out=B3[:], in_=io["st_Kt"][j]),
             inc=ld, k=16)
        em.q("sync", lambda s, j=j: s.dma_start(out=B4[:], in_=io["st_Vt"][j]),
             inc=ld, k=16)
        em.q("sync", lambda s, j=j: s.dma_start(out=B5[:], in_=io["st_U"][j]),
             inc=ld, k=16)
        em.q("sync", lambda s, j=j: s.dma_start(out=B6[:], in_=io["st_A"][j]),
             inc=ld, k=16)
        em.q("sync", lambda s, j=j: s.dma_start(out=rowB[0:1, :],
                                                in_=io["st_rows"][j][0:1]),
             inc=ld, k=16)
        em.q("sync", lambda s, j=j: s.dma_start(out=rowB[64:65, :],
                                                in_=io["st_rows"][j][1:2]),
             inc=ld, k=16)
        em.q("sync", lambda s, j=j: s.dma_start(out=B7[:], in_=io["st_SIG"][j]),
             inc=ld, k=16)
        em.wait("tensor", ld)
        em.wait("vector", ld)
        for q4 in range(4):
            em.q("tensor", lambda t, q4=q4: t.matmul(
                p_gb[:], G["onesR"][:], rowB[0:1, q4 * 512:(q4 + 1) * 512],
                start=True, stop=True), inc=mm)
            em.wait("vector", mm)
            em.q("vector", lambda v, q4=q4: v.tensor_copy(
                P3[:, q4 * 512:(q4 + 1) * 512], p_gb[:]), inc=ev)
            em.wait("tensor", ev)
            em.q("tensor", lambda t, q4=q4: t.matmul(
                p_gb[:], G["onesR65"][64:65, :], rowB[64:65, q4 * 512:(q4 + 1) * 512],
                start=True, stop=True), inc=mm)
            em.wait("vector", mm)
            em.q("vector", lambda v, q4=q4: v.tensor_copy(
                P4[:, q4 * 512:(q4 + 1) * 512], p_gb[:]), inc=ev)
            em.wait("tensor", ev)
        for h in range(H):
            em.q("tensor", lambda t, h=h: t.transpose(
                p_s[:, 2 * h:2 * h + 1], rowB[0:1, h * 128:(h + 1) * 128],
                G["idS32"][0:1, 0:1]))
            em.q("tensor", lambda t, h=h: t.transpose(
                p_s[:, 2 * h + 1:2 * h + 2], rowB[64:65, h * 128:(h + 1) * 128],
                G["idS32"][64:65, 64:65]), inc=(mm if h == H - 1 else None))
        em.wait("vector", mm)
        em.q("vector", lambda v: v.tensor_copy(gcols[:], p_s[:, 0:32]), inc=ev)
        em.wait("tensor", ev)
        em.q("vector", lambda v: v.tensor_mul(SkI[:], SkI[:], P4[:]), inc=ev)
        for h in range(H):
            em.q("vector", lambda v, h=h: v.tensor_scalar(
                SvI[:, h * 128:(h + 1) * 128], SvI[:, h * 128:(h + 1) * 128],
                gcols[:, 2 * h + 1:2 * h + 2], None, op0=ALU.mult),
                inc=(ev if h == H - 1 else None))
        em.wait("vector", ev)
        em.q("vector", lambda v: v.tensor_copy(SkP16[:], SkI[:]), inc=ev)
        em.q("vector", lambda v: v.tensor_copy(SvP16[:], SvI[:]), inc=ev)
        em.wait("tensor", ev)
        for h in range(H):
            hs = slice(h * 128, (h + 1) * 128)
            em.q("tensor", lambda t, hs=hs: t.matmul(
                p_P[:], B2[:, hs], B1[:, hs], start=True, stop=True), inc=mm)
            em.wait("vector", mm)
            em.q("vector", lambda v: v.tensor_mul(Pm16[:], p_P[:], G["triuS"][:]),
                 inc=ev)
            em.wait("tensor", ev)
            em.q("tensor", lambda t, hs=hs: t.matmul(
                p_s[:], Pm16[:], B5[:, hs], start=True, stop=False))
            em.q("tensor", lambda t, hs=hs: t.matmul(
                p_s[:], B1[:, hs], SkP16[:, hs], start=False, stop=True), inc=mm)
            em.wait("vector", mm)
            em.q("vector", lambda v, hs=hs: v.tensor_mul(esx[:], p_s[:], B6[:, hs]),
                 inc=ev)
            em.wait("scalar", ev)
            em.q("scalar", lambda s: s.activation(
                esx[:], esx[:], AF.Exp, scale=SCALE, accum_out=colB[:, 32:33]),
                inc=sc)
            em.wait("vector", sc)
            em.q("vector", lambda v: v.reciprocal(colB[:, 33:34], colB[:, 32:33]),
                 inc=ev)
            em.wait("vector", ev)
            em.q("vector", lambda v, hs=hs: v.scalar_tensor_tensor(
                aA16[:], esx[:], colB[:, 33:34], B6[:, hs],
                op0=ALU.mult, op1=ALU.mult), inc=ev)
            em.wait("tensor", ev)
            em.q("tensor", lambda t: t.transpose(
                p_t16[:, 0:128], aA16[:], G["idS"][:]))
            em.q("tensor", lambda t, hs=hs: t.transpose(
                p_t16[:, 128:256], B5[:, hs], G["idS"][:]), inc=mm)
            em.wait("vector", mm)
            em.q("vector", lambda v: v.tensor_copy(aAT16[:], p_t16[:, 0:128]))
            em.q("vector", lambda v: v.tensor_copy(UT16[:], p_t16[:, 128:256]),
                 inc=ev)
            em.wait("tensor", ev)
            em.q("tensor", lambda t: t.matmul(
                p_P[:], UT16[:], aAT16[:], start=True, stop=True), inc=mm)
            em.wait("vector", mm)
            em.q("vector", lambda v: v.tensor_mul(W2m16[:], p_P[:], G["triuS"][:]),
                 inc=ev)
            em.wait("tensor", ev)
            em.q("tensor", lambda t, hs=hs: t.matmul(
                p_o[:], W2m16[:], B4[:, hs], start=True, stop=False))
            em.q("tensor", lambda t, hs=hs: t.matmul(
                p_o[:], aAT16[:], SvP16[:, hs], start=False, stop=True), inc=mm)
            em.wait("vector", mm)
            em.q("vector", lambda v, hs=hs: v.tensor_copy(P1[:, hs], p_o[:]),
                 inc=ev)
            em.wait("tensor", ev)
            hh = h % 4
            em.q("tensor", lambda t, hs=hs, hh=hh: t.matmul(
                p_c1[:, hh * 128:(hh + 1) * 128], B3[:, hs], B5[:, hs],
                start=True, stop=True))
            em.q("tensor", lambda t, hs=hs, hh=hh: t.matmul(
                p_c2[:, hh * 128:(hh + 1) * 128], B5[:, hs], B4[:, hs],
                start=True, stop=True), inc=mm)
            if hh == 3:
                hg = h // 4
                sl = slice(hg * 512, (hg + 1) * 512)
                em.wait("vector", mm)
                em.q("vector", lambda v, sl=sl: v.tensor_add(
                    SkI[:, sl], SkI[:, sl], p_c1[:]))
                em.q("vector", lambda v, sl=sl: v.tensor_mul(
                    SkI[:, sl], SkI[:, sl], P3[:, sl]))
                em.q("vector", lambda v, sl=sl: v.tensor_add(
                    SvI[:, sl], SvI[:, sl], p_c2[:]), inc=ev)
                for h2 in range(hg * 4, hg * 4 + 4):
                    em.q("vector", lambda v, h2=h2: v.tensor_scalar(
                        SvI[:, h2 * 128:(h2 + 1) * 128],
                        SvI[:, h2 * 128:(h2 + 1) * 128],
                        gcols[:, 2 * h2:2 * h2 + 1], None, op0=ALU.mult),
                        inc=(ev if h2 % 4 == 3 else None))
                em.wait("tensor", ev)
        em.wait("scalar", ev)
        em.q("scalar", lambda s: s.activation(P2[:], P1[:], AF.Square), inc=sc)
        em.wait("vector", sc)
        em.q("vector", lambda v: v.reduce_sum(
            colA[:, 0:16].rearrange("p (h o) -> p h o", o=1),
            P2[:].rearrange("p (h m) -> p h m", m=128), axis=AX.X), inc=ev)
        em.wait("scalar", ev)
        em.q("scalar", lambda s: s.activation(colA[:, 0:16], colA[:, 0:16],
                                              AF.Sqrt, scale=1.0 / 128,
                                              bias=1e-5), inc=sc)
        em.wait("vector", sc)
        em.q("vector", lambda v: v.reciprocal(colA[:, 0:16], colA[:, 0:16]),
             inc=ev)
        em.wait("vector", ev)
        for h in range(H):
            em.q("vector", lambda v, h=h: v.tensor_scalar(
                P1[:, h * 128:(h + 1) * 128], P1[:, h * 128:(h + 1) * 128],
                colA[:, h:h + 1], None, op0=ALU.mult),
                inc=(ev if h == H - 1 else None))
        em.wait("vector", ev)
        em.q("vector", lambda v: v.tensor_mul(B1[:], P1[:], B7[:]), inc=ev)
        em.wait("tensor", ev)
        for b in range(2):
            for k in range(8):
                kt = b * 8 + k
                em.q("tensor", lambda t, kt=kt, k=k: t.transpose(
                    p_to[:, k * 128:(k + 1) * 128],
                    B1[:, kt * 128:(kt + 1) * 128], G["idS"][:]),
                    inc=(mm if k == 7 else None))
            em.wait("vector", mm)
            em.q("vector", lambda v, b=b: v.tensor_copy(
                B2[:].rearrange("p (a w) -> p a w", a=KT)[:, b * 8:(b + 1) * 8, :],
                p_to[:].rearrange("p (a w) -> p a w", a=8)), inc=ev)
            em.wait("tensor", ev)
        Wo3 = WoS[:].rearrange("p (a b c) -> p a b c", a=KT, b=KT)
        for dt in range(KT):
            for kt in range(KT):
                em.q("tensor", lambda t, dt=dt, kt=kt: t.matmul(
                    p_o[:], Wo3[:, kt, dt, :], B2[:, kt * 128:(kt + 1) * 128],
                    start=(kt == 0), stop=(kt == KT - 1)),
                    inc=(mm if kt == KT - 1 else None))
            em.wait("vector", mm)
            em.q("vector", lambda v, dt=dt: v.tensor_copy(
                attnC[:, dt * 128:(dt + 1) * 128], p_o[:]), inc=ev)
            em.wait("tensor", ev)
        em.wait("sync", ev)
        em.wait("sync", sc)
        em.q("sync", lambda s, j=j: s.dma_start(
            out=P1[:, 0:2048], in_=io["x_strip"][j * 128:(j + 1) * 128, :]),
            inc=ld, k=16)
        em.wait("vector", ld)
        em.q("vector", lambda v: v.tensor_copy(B3[:], P1[:]), inc=ev)
        em.wait("tensor", ev)
        for b in range(2):
            for k in range(8):
                kt = b * 8 + k
                em.q("tensor", lambda t, kt=kt, k=k: t.transpose(
                    p_to[:, k * 128:(k + 1) * 128],
                    B3[:, kt * 128:(kt + 1) * 128], G["idS"][:]),
                    inc=(mm if k == 7 else None))
            em.wait("vector", mm)
            em.q("vector", lambda v, b=b: v.tensor_copy(
                B4[:].rearrange("p (a w) -> p a w", a=KT)[:, b * 8:(b + 1) * 8, :],
                p_to[:].rearrange("p (a w) -> p a w", a=8)), inc=ev)
            em.wait("tensor", ev)
        em.q("vector", lambda v: v.tensor_add(P2[:], B4[:], attnC[:]), inc=ev)
        em.wait("sync", ev)
        em.q("sync", lambda s, j=j: s.dma_start(
            out=io["sp_x1"][j].rearrange("a p w -> p a w"),
            in_=P2[:].rearrange("p (a w) -> p a w", a=KT)), inc=sg, k=16)
        em.wait("vector", sg)
    em.barrier("endC")
    cctx.close()
    G["xctx"].close()



def phaseF0(G):
    nc, em, io = G["nc"], G["em"], G["io"]
    ld, gw, mm, ev, sc, sg, cc = (G[k] for k in
                                  ("ld", "gw", "mm", "ev", "sc", "sg", "cc"))
    P1, P2, P3, P4 = G["P1"], G["P2"], G["P3"], G["P4"]
    B1, B2 = G["B1"], G["B2"]
    rowA, colA, colB = G["rowA"], G["colA"], G["colB"]
    rowC, rowD = G["rowC"], G["rowD"]

    fctx = ExitStack()
    fsb = lambda n, s, d=F32: fctx.enter_context(nc.sbuf_tensor(n, list(s), d))
    fps = lambda n, s, d=F32: fctx.enter_context(nc.psum_tensor(n, list(s), d))
    rtrS = fsb("rtrS", (128, KT * 3))
    wdnS = fsb("wdnS", (128, KT * 32))
    wupS = fsb("wupS", (32, KT * 128))
    l1sS = fsb("l1sS", (128, KT)); l1bS = fsb("l1bS", (128, KT))
    tdn = fsb("tdn", (32, 1024))
    p_stA = fps("pF_stA", (128, 512))
    p_stB = fps("pF_stB", (128, 512))
    p_sa = p_stA[0:1, :]; p_sb = p_stA[64:65, :]
    p_sc = p_stB[0:1, :]; p_sd = p_stB[64:65, :]
    p_r = fps("pF_r", (128, 24))
    p_dn = fps("pF_dn", (32, 1024))
    p_gb = fps("pF_gb", (128, 512))

    em.q("sync", lambda s: s.dma_start(
        out=rtrS[:].rearrange("p (a c) -> p a c", a=KT), in_=io["rtr"]),
        inc=ld, k=16)
    em.q("sync", lambda s: s.dma_start(
        out=wdnS[:].rearrange("p (a c) -> p a c", a=KT), in_=io["wdn"]),
        inc=ld, k=16)
    em.q("sync", lambda s: s.dma_start(
        out=wupS[:].rearrange("p (a c) -> p a c", a=KT), in_=io["wup"]),
        inc=ld, k=16)
    em.q("sync", lambda s: s.dma_start(out=l1sS[:], in_=io["l1s"]), inc=ld, k=16)
    em.q("sync", lambda s: s.dma_start(out=l1bS[:], in_=io["l1b"]), inc=ld, k=16)
    em.wait("tensor", ld)

    pr = G["pr"]
    for kt in range(KT):
        em.wait("sync", pr)
        em.wait("sync", sc)
        em.q("sync", lambda s, kt=kt: s.dma_start(
            out=P1[:, 0:1024].rearrange("p (a w) -> p a w", a=NCH),
            in_=io["sp_x1"][:, kt].rearrange("a p w -> p a w")), inc=ld, k=16)
        em.wait("scalar", ld)
        em.wait("tensor", ld)
        em.q("scalar", lambda s: s.activation(P2[:, 0:1024], P1[:, 0:1024],
                                              AF.Square), inc=sc)
        em.wait("tensor", sc)
        for q2, ps in ((0, p_sa), (1, p_sb)):
            em.q("tensor", lambda t, q2=q2, ps=ps, kt=kt: t.matmul(
                ps, G["onesC32"][:], P1[:, q2 * 512:(q2 + 1) * 512],
                start=(kt == 0), stop=(kt == KT - 1)))
        for q2, ps in ((0, p_sc), (1, p_sd)):
            em.q("tensor", lambda t, q2=q2, ps=ps, kt=kt: t.matmul(
                ps, G["onesC32"][:], P2[:, q2 * 512:(q2 + 1) * 512],
                start=(kt == 0), stop=(kt == KT - 1)),
                inc=((pr if kt < KT - 1 else mm) if q2 == 1 else None))
    em.wait("vector", mm)
    em.q("vector", lambda v: v.tensor_scalar(
        rowA[0:1, 0:512], p_sa, 1.0 / D, None, op0=ALU.mult))
    em.q("vector", lambda v: v.tensor_scalar(
        rowA[0:1, 512:1024], p_sb, 1.0 / D, None, op0=ALU.mult))
    em.q("vector", lambda v: v.tensor_scalar(
        rowC[0:1, 0:512], p_sc, 1.0 / D, None, op0=ALU.mult))
    em.q("vector", lambda v: v.tensor_scalar(
        rowC[0:1, 512:1024], p_sd, 1.0 / D, None, op0=ALU.mult), inc=ev)
    em.wait("vector", ev)
    em.q("vector", lambda v: v.tensor_mul(rowD[0:1, 0:1024],
                                          rowA[0:1, 0:1024], rowA[0:1, 0:1024]),
         inc=ev)
    em.wait("vector", ev)
    em.q("vector", lambda v: v.tensor_sub(rowC[0:1, 0:1024], rowC[0:1, 0:1024],
                                          rowD[0:1, 0:1024]), inc=ev)
    em.wait("scalar", ev)
    em.q("scalar", lambda s: s.activation(rowC[0:1, 0:1024], rowC[0:1, 0:1024],
                                          AF.Sqrt, bias=1e-5), inc=sc)
    em.wait("vector", sc)
    em.q("vector", lambda v: v.reciprocal(rowC[0:1, 0:1024], rowC[0:1, 0:1024]),
         inc=ev)
    em.wait("tensor", ev)
    for q2 in range(2):
        em.q("tensor", lambda t, q2=q2: t.matmul(
            p_gb[:], G["onesR"][:], rowA[0:1, q2 * 512:(q2 + 1) * 512],
            start=True, stop=True), inc=mm)
        em.wait("vector", mm)
        em.q("vector", lambda v, q2=q2: v.tensor_copy(
            P3[:, q2 * 512:(q2 + 1) * 512], p_gb[:]), inc=ev)
        em.wait("tensor", ev)
        em.q("tensor", lambda t, q2=q2: t.matmul(
            p_gb[:], G["onesR"][:], rowC[0:1, q2 * 512:(q2 + 1) * 512],
            start=True, stop=True), inc=mm)
        em.wait("vector", mm)
        em.q("vector", lambda v, q2=q2: v.tensor_copy(
            P4[:, q2 * 512:(q2 + 1) * 512], p_gb[:]), inc=ev)
        em.wait("tensor", ev)
    for kt in range(KT):
        em.wait("sync", ev)
        em.q("sync", lambda s, kt=kt: s.dma_start(
            out=P1[:, 0:1024].rearrange("p (a w) -> p a w", a=NCH),
            in_=io["sp_x1"][:, kt].rearrange("a p w -> p a w")), inc=ld, k=16)
        em.wait("vector", ld)
        em.wait("vector", pr)
        em.q("vector", lambda v: v.tensor_sub(P2[:, 0:1024], P1[:, 0:1024],
                                              P3[:, 0:1024]), inc=ev)
        em.wait("vector", ev)
        em.q("vector", lambda v: v.tensor_mul(P2[:, 0:1024], P2[:, 0:1024],
                                              P4[:, 0:1024]), inc=ev)
        em.wait("vector", ev)
        em.q("vector", lambda v: v.tensor_copy(B1[:, 0:1024], P2[:, 0:1024]),
             inc=ev)
        em.wait("vector", ev)
        em.q("vector", lambda v, kt=kt: v.tensor_scalar(
            B2[:, 0:1024], P2[:, 0:1024], l1sS[:, kt:kt + 1],
            l1bS[:, kt:kt + 1], op0=ALU.mult, op1=ALU.add), inc=ev)
        em.wait("tensor", ev)
        for tt in range(8):
            em.q("tensor", lambda t, tt=tt, kt=kt: t.matmul(
                p_r[:, tt * 3:(tt + 1) * 3], P2[:, tt * 128:(tt + 1) * 128],
                rtrS[:, kt * 3:(kt + 1) * 3],
                start=(kt == 0 and tt == 0),
                stop=(kt == KT - 1 and tt == 7), skip_group_check=True))
        for q2 in range(2):
            em.q("tensor", lambda t, q2=q2, kt=kt: t.matmul(
                p_dn[:, q2 * 512:(q2 + 1) * 512],
                wdnS[:, kt * 32:(kt + 1) * 32], P2[:, q2 * 512:(q2 + 1) * 512],
                start=(kt == 0), stop=(kt == KT - 1)),
                inc=((pr if kt < KT - 1 else mm) if q2 == 1 else None))
        em.wait("sync", ev)
        em.q("sync", lambda s, kt=kt: s.dma_start(
            out=io["sp_hs"][kt], in_=B1[:, 0:1024]), inc=sg, k=16)
        em.q("sync", lambda s, kt=kt: s.dma_start(
            out=io["cc_xn_in"][kt * 128:(kt + 1) * 128, :], in_=B2[:, 0:1024]),
            inc=sg, k=16)
        em.wait("vector", sg)
    em.wait("vector", mm)
    em.q("vector", lambda v: v.tensor_copy(colA[:, 0:24], p_r[:]), inc=ev)
    em.wait("vector", ev)
    lA3 = colA[:, 0:24].rearrange("p (a c) -> p a c", a=8)
    l0s, l1s, l2s = lA3[:, :, 0], lA3[:, :, 1], lA3[:, :, 2]
    ge01, ge02, ge12 = colA[:, 32:40], colA[:, 40:48], colA[:, 48:56]
    nd0 = colA[:, 56:64]
    cB3 = colB[:, 0:16].rearrange("p (a c) -> p a c", a=8)
    d0s, d1s = cB3[:, :, 0], cB3[:, :, 1]
    em.q("vector", lambda v: v.tensor_tensor(ge01, l0s, l1s, op=ALU.is_ge))
    em.q("vector", lambda v: v.tensor_tensor(ge02, l0s, l2s, op=ALU.is_ge))
    em.q("vector", lambda v: v.tensor_tensor(ge12, l1s, l2s, op=ALU.is_ge),
         inc=ev)
    em.wait("vector", ev)
    em.q("vector", lambda v: v.tensor_mul(d0s, ge01, ge02), inc=ev)
    em.wait("vector", ev)
    em.q("vector", lambda v: v.tensor_scalar(
        nd0, d0s, -1.0, 1.0, op0=ALU.mult, op1=ALU.add), inc=ev)
    em.wait("vector", ev)
    em.q("vector", lambda v: v.tensor_mul(d1s, ge12, nd0), inc=ev)
    em.wait("vector", ev)
    em.wait("vector", mm)
    em.q("vector", lambda v: v.tensor_copy(tdn[:], p_dn[:]), inc=ev)
    em.wait("tensor", ev)
    for dt in range(KT):
        for q2 in range(2):
            em.q("tensor", lambda t, dt=dt, q2=q2: t.matmul(
                p_gb[:], wupS[:, dt * 128:(dt + 1) * 128],
                tdn[:, q2 * 512:(q2 + 1) * 512],
                start=True, stop=True), inc=mm)
            em.wait("vector", mm)
            em.q("vector", lambda v, q2=q2: v.tensor_copy(
                B1[:, q2 * 512:(q2 + 1) * 512], p_gb[:]), inc=ev)
            em.wait("tensor", ev)
        em.wait("sync", ev)
        em.q("sync", lambda s, dt=dt: s.dma_start(
            out=io["sp_lr"][dt], in_=B1[:, 0:1024]), inc=sg, k=16)
        em.wait("vector", sg)
    em.wait("gpsimd", sg)
    em.q("gpsimd", lambda g: g.collective_compute(
        "AllGather", ALU.bypass, replica_groups=[list(range(NCORE))],
        ins=[io["cc_xn_in"].opt()], outs=[io["cc_xn_out"].opt()]), inc=cc, k=1)
    em.wait("gpsimd", cc)
    em.barrier("endF0")
    fctx.close()



def phaseF1(G):
    nc, em, io = G["nc"], G["em"], G["io"]
    ld, gw, mm, ev, sc, sg, cc = (G[k] for k in
                                  ("ld", "gw", "mm", "ev", "sc", "sg", "cc"))
    P1 = G["P1"]
    rowD = G["rowD"]

    fctx = ExitStack()
    fsb = lambda n, s, d=F32: fctx.enter_context(nc.sbuf_tensor(n, list(s), d))
    fps = lambda n, s, d=F32: fctx.enter_context(nc.psum_tensor(n, list(s), d))
    rz16 = fsb("rz16", (1, 512), BF16)
    dmS = fsb("dmS", (128, 2048), BF16)
    f1a = ExitStack()
    fsa = lambda n, s, d=F32: f1a.enter_context(nc.sbuf_tensor(n, list(s), d))
    xnp = fsa("xnp", (128, KT * 1024), BF16)
    wqS = fsa("wqS", (128, KT * 256), BF16)
    wkS = fsa("wkS", (128, KT * 256), BF16)
    wvS = fsa("wvS", (128, KT * 256), BF16)
    evq = fsa("evq", (128, 1024), BF16)

    p_q0 = fps("pG_q0", (128, 512)); p_q1 = fps("pG_q1", (128, 512))
    p_S0 = fps("pG_S0", (128, 512)); p_S1 = fps("pG_S1", (128, 512))
    p_O = fps("pG_O", (128, 512))
    p_Z = fps("pG_Z", (1, 512))
    p_bc = fps("pG_bc", (128, 512))
    p_tv = fps("pG_tv", (128, 1024), BF16)

    em.q("sync", lambda s: s.dma_start(out=dmS[:], in_=G["cst"]["dmask"]),
         inc=ld, k=16)
    for wS, wn in ((wqS, "fwq"), (wkS, "fwk"), (wvS, "fwv")):
        em.q("gpsimd", lambda g, wS=wS, wn=wn: g.dma_start(
            out=wS[:].rearrange("p (a b c) -> p a b c", a=KT, b=2),
            in_=io[wn]), inc=gw, k=16)
    em.wait("tensor", gw)
    em.wait("tensor", ld)

    wq3 = wqS[:].rearrange("p (a b c) -> p a b c", a=KT, b=2)
    wk3 = wkS[:].rearrange("p (a b c) -> p a b c", a=KT, b=2)
    wv3 = wvS[:].rearrange("p (a b c) -> p a b c", a=KT, b=2)
    for piece in range(NCORE):
        em.wait("sync", mm)
        for kt in range(KT):
            em.q("sync", lambda s, piece=piece, kt=kt: s.dma_start(
                out=xnp[:, kt * 1024:(kt + 1) * 1024],
                in_=io["cc_xn_out"][piece * D + kt * 128:
                                    piece * D + (kt + 1) * 128, :]),
                inc=ld, k=16)
        em.wait("tensor", ld)
        for w3, stn in ((wq3, "st_ffq"), (wk3, "st_ffk"), (wv3, "st_ffv")):
            for cb in range(2):
                for q2, ps in ((0, p_q0), (1, p_q1)):
                    for kt in range(KT):
                        em.q("tensor", lambda t, kt=kt, cb=cb, q2=q2, ps=ps,
                             w3=w3: t.matmul(
                                 ps[:], w3[:, kt, cb, :],
                                 xnp[:, kt * 1024 + q2 * 512:
                                     kt * 1024 + (q2 + 1) * 512],
                                 start=(kt == 0), stop=(kt == KT - 1)),
                             inc=(mm if kt == KT - 1 else None))
                em.wait("vector", mm)
                em.q("vector", lambda v: v.tensor_copy(evq[:, 0:512], p_q0[:]))
                em.q("vector", lambda v: v.tensor_copy(evq[:, 512:1024], p_q1[:]),
                     inc=ev)
                em.wait("sync", ev)
                em.q("sync", lambda s, stn=stn, cb=cb, piece=piece: s.dma_start(
                    out=io[stn][cb][:, piece * 1024:(piece + 1) * 1024],
                    in_=evq[:]), inc=sg, k=16)
                em.wait("vector", sg)
                em.wait("tensor", sg)
    em.barrier("endF1proj")
    f1a.close()
    f1b = ExitStack()
    fsbb = lambda n, s, d=F32: f1b.enter_context(nc.sbuf_tensor(n, list(s), d))
    QS = fsbb("QS", (128, T), BF16)
    KS = fsbb("KS", (128, T), BF16)
    VS = fsbb("VS", (128, T), BF16)
    VtS = fsbb("VtS", (128, T), BF16)
    Pt16 = fsbb("Pt16", (128, 2 * 512), BF16)
    O16 = fsbb("O16", (128, 512), BF16)
    for hh in range(2):
        em.wait("sync", sg)
        em.wait("sync", mm)
        em.wait("sync", sc)
        em.wait("sync", ev)
        em.q("sync", lambda s, hh=hh: s.dma_start(out=QS[:], in_=io["st_ffq"][hh]),
             inc=ld, k=16)
        em.q("sync", lambda s, hh=hh: s.dma_start(out=KS[:], in_=io["st_ffk"][hh]),
             inc=ld, k=16)
        em.q("sync", lambda s, hh=hh: s.dma_start(out=VS[:], in_=io["st_ffv"][hh]),
             inc=ld, k=16)
        em.wait("tensor", ld)
        for b8 in range(8):
            for k in range(8):
                kv = b8 * 8 + k
                em.q("tensor", lambda t, kv=kv, k=k: t.transpose(
                    p_tv[:, k * 128:(k + 1) * 128],
                    VS[:, kv * 128:(kv + 1) * 128], G["idS"][:]),
                    inc=(mm if k == 7 else None))
            em.wait("vector", mm)
            em.q("vector", lambda v, b8=b8: v.tensor_copy(
                VtS[:, b8 * 1024:(b8 + 1) * 1024],
                p_tv[:].rearrange("p (a w) -> p a w", a=8).rearrange(
                    "p a w -> p (a w)")), inc=ev)
            em.wait("tensor", ev)
        for qb in range(16):
            nkv = 4 * qb + 4
            pr = G["pr"]
            for kv in range(nkv):
                ps = p_S0 if kv % 2 == 0 else p_S1
                pb = 0 if kv % 2 == 0 else 1
                em.q("tensor", lambda t, kv=kv, qb=qb, ps=ps: t.matmul(
                    ps[:], KS[:, kv * 128:(kv + 1) * 128],
                    QS[:, qb * 512:(qb + 1) * 512], start=True, stop=True),
                    inc=mm)
                em.wait("scalar", mm)
                if kv >= 2:
                    em.wait("scalar", pr, pr.n - 1)
                em.q("scalar", lambda s, ps=ps, pb=pb: s.activation(
                    Pt16[:, pb * 512:(pb + 1) * 512], ps[:], AF.Exp,
                    scale=SCALE, bias=-C0F), inc=sc)
                em.wait("tensor", sc)
                if kv >= 4 * qb:
                    dj = kv - 4 * qb
                    em.wait("vector", sc)
                    em.q("vector", lambda v, pb=pb, dj=dj: v.tensor_mul(
                        Pt16[:, pb * 512:(pb + 1) * 512],
                        Pt16[:, pb * 512:(pb + 1) * 512],
                        dmS[:, dj * 512:(dj + 1) * 512]), inc=ev)
                    em.wait("tensor", ev)
                em.q("tensor", lambda t, kv=kv, pb=pb, nkv=nkv: t.matmul(
                    p_O[:], VtS[:, kv * 128:(kv + 1) * 128],
                    Pt16[:, pb * 512:(pb + 1) * 512],
                    start=(kv == 0), stop=(kv == nkv - 1)))
                em.q("tensor", lambda t, kv=kv, pb=pb, nkv=nkv: t.matmul(
                    p_Z[:], G["onesC16"][:],
                    Pt16[:, pb * 512:(pb + 1) * 512],
                    start=(kv == 0), stop=(kv == nkv - 1)),
                    inc=[(mm, 1)] if kv == nkv - 1 else [(pr, 1)])
            em.wait("vector", mm)
            em.q("vector", lambda v: v.reciprocal(rowD[0:1, 0:512], p_Z[:]),
                 inc=ev)
            em.wait("vector", ev)
            em.q("vector", lambda v: v.tensor_copy(rz16[:], rowD[0:1, 0:512]),
                 inc=ev)
            em.wait("tensor", ev)
            em.q("tensor", lambda t: t.matmul(
                p_bc[:], G["onesR16"][:], rz16[:], start=True, stop=True),
                inc=mm)
            em.q("vector", lambda v: v.tensor_copy(P1[:, 0:512], p_O[:]), inc=ev)
            em.wait("vector", mm)
            em.wait("vector", ev)
            em.q("vector", lambda v: v.tensor_mul(O16[:], P1[:, 0:512], p_bc[:]),
                 inc=ev)
            em.wait("sync", ev)
            qb2 = qb // 2
            em.q("sync", lambda s, hh=hh, qb=qb, qb2=qb2: s.dma_start(
                out=io["cc_ctx_in"][qb2 * 256 + hh * 128:
                                    qb2 * 256 + (hh + 1) * 128,
                                    (qb % 2) * 512:(qb % 2) * 512 + 512],
                in_=O16[:]), inc=sg, k=16)
            em.wait("vector", sg)
            em.wait("tensor", sg)
    em.wait("gpsimd", sg)
    em.q("gpsimd", lambda g: g.collective_compute(
        "AllGather", ALU.bypass, replica_groups=[list(range(NCORE))],
        ins=[io["cc_ctx_in"].opt()], outs=[io["cc_ctx_out"].opt()]), inc=cc, k=1)
    em.wait("gpsimd", cc)
    em.barrier("endF1")
    f1b.close()
    fctx.close()



def phaseF2(G):
    nc, em, io = G["nc"], G["em"], G["io"]
    ld, gw, mm, ev, sc, sg = (G[k] for k in ("ld", "gw", "mm", "ev", "sc", "sg"))
    P1, P2, P3, P4 = G["P1"], G["P2"], G["P3"], G["P4"]
    B1, B2, B3 = G["B1"], G["B2"], G["B3"]
    rowA, colB = G["rowA"], G["colB"]
    colA = G["colA"]
    rowC, rowD = G["rowC"], G["rowD"]

    fctx = ExitStack()
    fsb = lambda n, s, d=F32: fctx.enter_context(nc.sbuf_tensor(n, list(s), d))
    fps = lambda n, s, d=F32: fctx.enter_context(nc.psum_tensor(n, list(s), d))
    l2sS = fsb("l2sS", (128, KT)); l2bS = fsb("l2bS", (128, KT))
    b0S = fsb("b0S", (128, KT))
    D0bc = fsb("D0bc", (128, 1024)); D1bc = fsb("D1bc", (128, 1024))
    cidxS = fsb("cidxS", (128, KT), U32)
    f2a = ExitStack()
    ctxS = f2a.enter_context(nc.sbuf_tensor("ctxS", [128, KT * 1024], BF16))
    woSl = f2a.enter_context(nc.sbuf_tensor("woSl", [128, 2 * 2048], BF16))

    p_w0 = fps("pH_w0", (128, 512)); p_w1 = fps("pH_w1", (128, 512))
    p_stA = fps("pH_stA", (128, 512))
    p_stB = fps("pH_stB", (128, 512))
    p_sa = p_stA[0:1, :]; p_sb = p_stA[64:65, :]
    p_sc2 = p_stB[0:1, :]; p_sd = p_stB[64:65, :]
    p_gb = fps("pH_gb", (128, 512))
    p_tr = fps("pH_tr", (128, 128))
    p_trb = fps("pH_trb", (128, 1024), BF16)

    em.q("sync", lambda s: s.dma_start(out=l2sS[:], in_=io["l2s"]), inc=ld, k=16)
    em.q("sync", lambda s: s.dma_start(out=l2bS[:], in_=io["l2b"]), inc=ld, k=16)
    em.q("sync", lambda s: s.dma_start(out=b0S[:], in_=io["b0"]), inc=ld, k=16)
    em.q("sync", lambda s: s.dma_start(out=cidxS[:], in_=io["ctxidx"]),
         inc=ld, k=16)
    em.wait("gpsimd", ld)
    for kt in range(KT):
        em.q("gpsimd", lambda g, kt=kt: g.indirect_dma_start(
            out=ctxS[:, kt * 1024:(kt + 1) * 1024], out_offset=None,
            in_=io["cc_ctx_out"],
            in_offset=IndirectOffsetOnAxis(cidxS[:, kt:kt + 1], 0)),
            inc=gw, k=16)
    em.wait("tensor", gw)
    em.wait("tensor", ld)
    wo_marks = []
    for dt in range(KT):
        if dt >= 2:
            em.wait("gpsimd", mm, wo_marks[dt - 2])
        em.q("gpsimd", lambda g, dt=dt: g.dma_start(
            out=woSl[:, (dt % 2) * 2048:(dt % 2) * 2048 + 2048].rearrange(
                "p (a c) -> p a c", a=KT),
            in_=io["fwo"][dt]), inc=gw, k=16)
        em.wait("sync", ev)
        em.wait("sync", sc)
        em.q("sync", lambda s, dt=dt: s.dma_start(
            out=B1[:, 0:1024], in_=io["sp_hs"][dt]), inc=ld, k=16)
        em.wait("tensor", gw)
        em.wait("tensor", ld)
        wsl = woSl[:, (dt % 2) * 2048:(dt % 2) * 2048 + 2048]
        for q2, ps in ((0, p_w0), (1, p_w1)):
            for kt in range(KT):
                em.q("tensor", lambda t, kt=kt, q2=q2, ps=ps, wsl=wsl: t.matmul(
                    ps[:], wsl[:, kt * 128:(kt + 1) * 128],
                    ctxS[:, kt * 1024 + q2 * 512:kt * 1024 + (q2 + 1) * 512],
                    start=(kt == 0), stop=(kt == KT - 1)),
                    inc=(mm if kt == KT - 1 else None))
        em.wait("vector", mm)
        em.wait("vector", G["pr"])
        em.q("vector", lambda v: v.tensor_add(P1[:, 0:512], B1[:, 0:512],
                                              p_w0[:]))
        em.q("vector", lambda v: v.tensor_add(P1[:, 512:1024], B1[:, 512:1024],
                                              p_w1[:]), inc=ev)
        em.wait("tensor", ev)
        em.wait("scalar", ev)
        em.q("scalar", lambda s: s.activation(P2[:, 0:1024], P1[:, 0:1024],
                                              AF.Square), inc=sc)
        em.q("vector", lambda v: v.tensor_copy(B2[:, 0:1024], P1[:, 0:1024]),
             inc=ev)
        em.wait("sync", ev)
        em.q("sync", lambda s, dt=dt: s.dma_start(
            out=io["sp_xt"][dt], in_=B2[:, 0:1024]), inc=sg, k=16)
        em.wait("tensor", sc)
        for q2, ps in ((0, p_sa), (1, p_sb)):
            em.q("tensor", lambda t, q2=q2, ps=ps, dt=dt: t.matmul(
                ps[:], G["onesC32"][:], P1[:, q2 * 512:(q2 + 1) * 512],
                start=(dt == 0), stop=(dt == KT - 1)))
        for q2, ps in ((0, p_sc2), (1, p_sd)):
            em.q("tensor", lambda t, q2=q2, ps=ps, dt=dt: t.matmul(
                ps[:], G["onesC32"][:], P2[:, q2 * 512:(q2 + 1) * 512],
                start=(dt == 0), stop=(dt == KT - 1)),
                inc=(mm if (dt == KT - 1 and q2 == 1) else None))
        wo_marks.append(mm.n)
        em.wait("vector", sg)
    em.wait("vector", mm)
    em.q("vector", lambda v: v.tensor_scalar(
        rowA[0:1, 0:512], p_sa, 1.0 / D, None, op0=ALU.mult))
    em.q("vector", lambda v: v.tensor_scalar(
        rowA[0:1, 512:1024], p_sb, 1.0 / D, None, op0=ALU.mult))
    em.q("vector", lambda v: v.tensor_scalar(
        rowC[0:1, 0:512], p_sc2, 1.0 / D, None, op0=ALU.mult))
    em.q("vector", lambda v: v.tensor_scalar(
        rowC[0:1, 512:1024], p_sd, 1.0 / D, None, op0=ALU.mult), inc=ev)
    em.wait("vector", ev)
    em.q("vector", lambda v: v.tensor_mul(rowD[0:1, 0:1024],
                                          rowA[0:1, 0:1024], rowA[0:1, 0:1024]),
         inc=ev)
    em.wait("vector", ev)
    em.q("vector", lambda v: v.tensor_sub(rowC[0:1, 0:1024], rowC[0:1, 0:1024],
                                          rowD[0:1, 0:1024]), inc=ev)
    em.wait("scalar", ev)
    em.q("scalar", lambda s: s.activation(rowC[0:1, 0:1024], rowC[0:1, 0:1024],
                                          AF.Sqrt, bias=1e-5), inc=sc)
    em.wait("vector", sc)
    em.q("vector", lambda v: v.reciprocal(rowC[0:1, 0:1024], rowC[0:1, 0:1024]),
         inc=ev)
    em.wait("tensor", ev)
    for q2 in range(2):
        em.q("tensor", lambda t, q2=q2: t.matmul(
            p_gb[:], G["onesR"][:], rowA[0:1, q2 * 512:(q2 + 1) * 512],
            start=True, stop=True), inc=mm)
        em.wait("vector", mm)
        em.q("vector", lambda v, q2=q2: v.tensor_copy(
            P3[:, q2 * 512:(q2 + 1) * 512], p_gb[:]), inc=ev)
        em.wait("tensor", ev)
        em.q("tensor", lambda t, q2=q2: t.matmul(
            p_gb[:], G["onesR"][:], rowC[0:1, q2 * 512:(q2 + 1) * 512],
            start=True, stop=True), inc=mm)
        em.wait("vector", mm)
        em.q("vector", lambda v, q2=q2: v.tensor_copy(
            P4[:, q2 * 512:(q2 + 1) * 512], p_gb[:]), inc=ev)
        em.wait("tensor", ev)
    em.barrier("endF2wo")
    f2a.close()
    f2b = ExitStack()
    xn2S = f2b.enter_context(nc.sbuf_tensor("xn2S", [128, KT * 1024], BF16))
    aS = f2b.enter_context(nc.sbuf_tensor("aS", [128, KT * 1024], BF16))
    ACC = f2b.enter_context(nc.sbuf_tensor("ACC", [128, KT * 1024], BF16))
    f1Sl = f2b.enter_context(nc.sbuf_tensor("f1Sl", [128, 2 * 2048], BF16))
    f2Sl = f2b.enter_context(nc.sbuf_tensor("f2Sl", [128, 2 * 2048], BF16))
    for kt in range(KT):
        em.wait("sync", ev)
        em.q("sync", lambda s, kt=kt: s.dma_start(
            out=B1[:, 0:1024], in_=io["sp_xt"][kt]), inc=ld, k=16)
        em.wait("vector", ld)
        em.q("vector", lambda v: v.tensor_sub(P1[:, 0:1024], B1[:, 0:1024],
                                              P3[:, 0:1024]), inc=ev)
        em.wait("vector", ev)
        em.q("vector", lambda v: v.tensor_mul(P1[:, 0:1024], P1[:, 0:1024],
                                              P4[:, 0:1024]), inc=ev)
        em.wait("vector", ev)
        em.q("vector", lambda v, kt=kt: v.tensor_scalar(
            xn2S[:, kt * 1024:(kt + 1) * 1024], P1[:, 0:1024],
            l2sS[:, kt:kt + 1], l2bS[:, kt:kt + 1],
            op0=ALU.mult, op1=ALU.add), inc=ev)
    em.wait("tensor", ev)
    for tt in range(8):
        em.q("tensor", lambda t, tt=tt: t.transpose(
            p_tr[0:1, :], colB[:, 2 * tt:2 * tt + 1], G["idS32"][:]), inc=mm)
        em.wait("vector", mm)
        em.q("vector", lambda v: v.tensor_copy(rowA[0:1, 1024:1152],
                                               p_tr[0:1, 0:128]), inc=ev)
        em.wait("tensor", ev)
        em.q("tensor", lambda t, tt=tt: t.transpose(
            p_tr[0:1, :], colB[:, 2 * tt + 1:2 * tt + 2], G["idS32"][:]), inc=mm)
        em.wait("vector", mm)
        em.q("vector", lambda v: v.tensor_copy(rowC[0:1, 1024:1152],
                                               p_tr[0:1, 0:128]), inc=ev)
        em.wait("tensor", ev)
        em.q("tensor", lambda t, tt=tt: t.matmul(
            p_gb[:, 0:128], G["onesR"][:], rowA[0:1, 1024:1152],
            start=True, stop=True), inc=mm)
        em.wait("vector", mm)
        em.q("vector", lambda v, tt=tt: v.tensor_copy(
            D0bc[:, tt * 128:(tt + 1) * 128], p_gb[:, 0:128]), inc=ev)
        em.wait("tensor", ev)
        em.q("tensor", lambda t, tt=tt: t.matmul(
            p_gb[:, 0:128], G["onesR"][:], rowC[0:1, 1024:1152],
            start=True, stop=True), inc=mm)
        em.wait("vector", mm)
        em.q("vector", lambda v, tt=tt: v.tensor_copy(
            D1bc[:, tt * 128:(tt + 1) * 128], p_gb[:, 0:128]), inc=ev)
        em.wait("tensor", ev)
    f1_marks = []
    for fb in range(4):
        for ft in range(16):
            gft = fb * 16 + ft
            if len(f1_marks) >= 2:
                em.wait("gpsimd", mm, f1_marks[-2])
            em.q("gpsimd", lambda g, gft=gft, ft=ft: g.dma_start(
                out=f1Sl[:, (ft % 2) * 2048:(ft % 2) * 2048 + 2048].rearrange(
                    "p (a c) -> p a c", a=KT),
                in_=io["ff1"][gft]), inc=gw, k=16)
            em.wait("tensor", gw)
            fsl = f1Sl[:, (ft % 2) * 2048:(ft % 2) * 2048 + 2048]
            for q2, ps in ((0, p_w0), (1, p_w1)):
                for kt in range(KT):
                    em.q("tensor", lambda t, kt=kt, q2=q2, ps=ps, fsl=fsl:
                         t.matmul(
                             ps[:], fsl[:, kt * 128:(kt + 1) * 128],
                             xn2S[:, kt * 1024 + q2 * 512:
                                  kt * 1024 + (q2 + 1) * 512],
                             start=(kt == 0), stop=(kt == KT - 1)),
                         inc=(mm if kt == KT - 1 else None))
                em.wait("scalar", mm)
                em.q("scalar", lambda s, ps=ps: s.activation(
                    P1[:, 0:512], ps[:], AF.Square), inc=sc)
                em.wait("vector", sc)
                em.q("vector", lambda v: v.tensor_scalar(
                    P1[:, 0:512], P1[:, 0:512], GB, GA,
                    op0=ALU.mult, op1=ALU.add), inc=ev)
                em.wait("vector", ev)
                em.q("vector", lambda v, ps=ps: v.tensor_mul(
                    P2[:, 0:512], P1[:, 0:512], ps[:]), inc=ev)
                em.wait("scalar", ev)
                em.q("scalar", lambda s: s.activation(
                    P2[:, 0:512], P2[:, 0:512], AF.Sigmoid), inc=sc)
                em.wait("vector", sc)
                em.q("vector", lambda v, ps=ps, ft=ft, q2=q2: v.tensor_mul(
                    aS[:, ft * 1024 + q2 * 512:ft * 1024 + (q2 + 1) * 512],
                    P2[:, 0:512], ps[:]), inc=ev)
                em.wait("tensor", ev)
            f1_marks.append(mm.n)
        for dt in range(KT):
            em.wait("gpsimd", mm)
            em.q("gpsimd", lambda g, dt=dt, fb=fb: g.dma_start(
                out=f2Sl[:, (dt % 2) * 2048:(dt % 2) * 2048 + 2048].rearrange(
                    "p (a c) -> p a c", a=KT),
                in_=io["ff2"][dt, fb]), inc=gw, k=16)
            em.wait("tensor", gw)
            fsl = f2Sl[:, (dt % 2) * 2048:(dt % 2) * 2048 + 2048]
            for q2, ps in ((0, p_w0), (1, p_w1)):
                for fk in range(16):
                    em.q("tensor", lambda t, fk=fk, q2=q2, ps=ps, fsl=fsl:
                         t.matmul(
                             ps[:], fsl[:, fk * 128:(fk + 1) * 128],
                             aS[:, fk * 1024 + q2 * 512:
                                fk * 1024 + (q2 + 1) * 512],
                             start=(fk == 0), stop=(fk == KT - 1)),
                         inc=(mm if fk == KT - 1 else None))
            em.wait("vector", mm)
            if fb == 0:
                em.q("vector", lambda v, dt=dt: v.tensor_copy(
                    ACC[:, dt * 1024:dt * 1024 + 512], p_w0[:]))
                em.q("vector", lambda v, dt=dt: v.tensor_copy(
                    ACC[:, dt * 1024 + 512:dt * 1024 + 1024], p_w1[:]), inc=ev)
            else:
                em.q("vector", lambda v, dt=dt: v.tensor_add(
                    ACC[:, dt * 1024:dt * 1024 + 512],
                    ACC[:, dt * 1024:dt * 1024 + 512], p_w0[:]))
                em.q("vector", lambda v, dt=dt: v.tensor_add(
                    ACC[:, dt * 1024 + 512:dt * 1024 + 1024],
                    ACC[:, dt * 1024 + 512:dt * 1024 + 1024], p_w1[:]), inc=ev)
            em.wait("tensor", ev)
    if DBGRAW:
        em.wait("sync", ev)
        em.q("sync", lambda s: s.dma_start(
            out=io["st_raw"][:, 0:KT * 1024], in_=ACC[:]), inc=sg, k=16)
        em.q("sync", lambda s: s.dma_start(
            out=io["st_dbg3"].rearrange("a p c -> p (a c)")
            if False else io["st_dbg3"].rearrange("a p c -> p a c"),
            in_=xn2S[:].rearrange("p (a c) -> p a c", a=NCH * 2)
            if False else xn2S[:].rearrange("p (a c) -> p a c", a=NCH)),
            inc=sg, k=16)
        em.q("sync", lambda s: s.dma_start(
            out=io["st_dbg4"].rearrange("a p c -> p a c"),
            in_=aS[:].rearrange("p (a c) -> p a c", a=NCH)), inc=sg, k=16)
        em.q("sync", lambda s: s.dma_start(out=io["st_dbg2"][:, 0:32],
                                           in_=colA[:, 0:32]), inc=sg, k=16)
        em.q("sync", lambda s: s.dma_start(out=io["st_dbg2"][:, 32:64],
                                           in_=colB[:, 0:32]), inc=sg, k=16)
    for dt in range(KT):
        em.wait("sync", ev)
        em.q("sync", lambda s, dt=dt: s.dma_start(
            out=B1[:, 0:1024], in_=io["sp_xt"][dt]), inc=ld, k=16)
        em.q("sync", lambda s, dt=dt: s.dma_start(
            out=B2[:, 0:1024], in_=io["sp_hs"][dt]), inc=ld, k=16)
        em.q("sync", lambda s, dt=dt: s.dma_start(
            out=B3[:, 0:1024], in_=io["sp_lr"][dt]), inc=ld, k=16)
        em.q("sync", lambda s, dt=dt: s.dma_start(
            out=P4[:, 0:1024].rearrange("p (a w) -> p a w", a=NCH),
            in_=io["sp_x1"][:, dt].rearrange("a p w -> p a w")), inc=ld, k=16)
        em.wait("vector", ld)
        em.q("vector", lambda v, dt=dt: v.tensor_add(
            P1[:, 0:1024], B1[:, 0:1024], ACC[:, dt * 1024:(dt + 1) * 1024]),
            inc=ev)
        em.wait("vector", ev)
        em.q("vector", lambda v, dt=dt: v.scalar_tensor_tensor(
            P2[:, 0:1024], B2[:, 0:1024], b0S[:, dt:dt + 1], P1[:, 0:1024],
            op0=ALU.add, op1=ALU.subtract), inc=ev)
        em.wait("vector", ev)
        em.q("vector", lambda v: v.tensor_mul(P2[:, 0:1024], P2[:, 0:1024],
                                              D0bc[:, 0:1024]), inc=ev)
        em.wait("vector", ev)
        em.q("vector", lambda v: v.tensor_add(P1[:, 0:1024], P1[:, 0:1024],
                                              P2[:, 0:1024]), inc=ev)
        em.wait("vector", ev)
        em.q("vector", lambda v: v.tensor_add(P2[:, 0:1024], B2[:, 0:1024],
                                              B3[:, 0:1024]), inc=ev)
        em.wait("vector", ev)
        em.q("vector", lambda v, dt=dt: v.tensor_sub(
            P2[:, 0:1024], P2[:, 0:1024], B1[:, 0:1024]), inc=ev)
        em.wait("vector", ev)
        em.q("vector", lambda v, dt=dt: v.scalar_tensor_tensor(
            P2[:, 0:1024], ACC[:, dt * 1024:(dt + 1) * 1024], -1.0,
            P2[:, 0:1024], op0=ALU.mult, op1=ALU.add), inc=ev)
        em.wait("vector", ev)
        em.q("vector", lambda v: v.tensor_mul(P2[:, 0:1024], P2[:, 0:1024],
                                              D1bc[:, 0:1024]), inc=ev)
        em.wait("vector", ev)
        em.q("vector", lambda v: v.tensor_add(P1[:, 0:1024], P1[:, 0:1024],
                                              P2[:, 0:1024]), inc=ev)
        em.wait("vector", ev)
        em.q("vector", lambda v: v.tensor_add(P1[:, 0:1024], P1[:, 0:1024],
                                              P4[:, 0:1024]), inc=ev)
        em.wait("sync", ev)
        em.q("sync", lambda s, dt=dt: s.dma_start(
            out=io["sp_x2"][dt], in_=P1[:, 0:1024]), inc=sg, k=16)
        em.wait("vector", sg)
    fin = G["fin"]
    for j in range(NCH):
        em.wait("sync", ev)
        em.q("sync", lambda s, j=j: s.dma_start(
            out=P1[:, 0:2048].rearrange("p (a w) -> p a w", a=KT),
            in_=io["sp_x2"][:, :, j * 128:(j + 1) * 128].rearrange(
                "a p w -> p a w")), inc=ld, k=16)
        em.wait("vector", ld)
        em.q("vector", lambda v: v.tensor_copy(B1[:], P1[:]), inc=ev)
        em.wait("tensor", ev)
        for b in range(2):
            for k in range(8):
                kt = b * 8 + k
                em.q("tensor", lambda t, kt=kt, k=k: t.transpose(
                    p_trb[:, k * 128:(k + 1) * 128],
                    B1[:, kt * 128:(kt + 1) * 128], G["idS"][:]),
                    inc=(mm if k == 7 else None))
            em.wait("vector", mm)
            em.q("vector", lambda v, b=b: v.tensor_copy(
                P2[:].rearrange("p (a w) -> p a w", a=KT)[:, b * 8:(b + 1) * 8, :],
                p_trb[:].rearrange("p (a w) -> p a w", a=8)), inc=ev)
            em.wait("tensor", ev)
        em.wait("sync", ev)
        em.q("sync", lambda s, j=j: s.dma_start(
            out=io["y_strip"][j * 128:(j + 1) * 128, :], in_=P2[:]),
            inc=fin, k=16)
        em.wait("vector", fin)
    em.wait("sync", fin)
    f2b.close()
    fctx.close()



def kernel(**inputs):
    maps = prep_inputs(inputs)
    nc = build_program(taps=TAPS)
    for k, shp in EXPECTED_IN_SHAPES.items():
        assert tuple(maps[0][k].shape) == shp, (k, maps[0][k].shape, shp)
    res = run_bass_kernel_spmd(nc, maps, core_ids=list(range(NCORE)))
    out = np.concatenate([res.results[c]["y_strip"] for c in range(NCORE)], 0)
    return out.reshape(1, T, D)


# version 04
# speedup vs baseline: 1.0039x; 1.0039x over previous
import numpy as np
import ml_dtypes
import concourse.bass as bass
import concourse.mybir as mybir
from concourse.bass_utils import run_bass_kernel_spmd
from concourse.bass_core_rust import IndirectOffsetOnAxis
from contextlib import ExitStack

F32 = mybir.dt.float32
BF16 = mybir.dt.bfloat16
U32 = mybir.dt.uint32
AF = mybir.ActivationFunctionType
ALU = mybir.AluOpType
AX = mybir.AxisListType
BF16NP = ml_dtypes.bfloat16

T, D, H = 8192, 2048, 16
NCORE = 8
STRIP = T // NCORE
CHK = 128
NCH = STRIP // CHK
KT = D // 128
REFI = 63
SCALE = 128.0 ** -0.5
C0F = 20.0
GA = 1.5957691216057308
GB = 0.07135481283462803

ENGINES = ("sync", "tensor", "vector", "scalar", "gpsimd")
_t = np.ascontiguousarray

PHASES = dict(A=True, X1=True, C=True, F0=True, F1=True, F2=True)
DBGRAW = False
TAPS = []



def prep_inputs(inp):
    x0 = np.asarray(inp["L0_attn_hidden_states"], np.float32).reshape(T, D)
    g = lambda k: np.asarray(inp[k], np.float32)

    sh = {}
    for nm, key in (("Wq", "L0_attn_Wq"), ("Wk", "L0_attn_Wk"),
                    ("Wv", "L0_attn_Wv"), ("Ww", "L0_attn_Ww")):
        w = g(key)
        sh[nm] = _t(w.reshape(KT, 128, KT, 128).transpose(2, 1, 0, 3))
    for nm, key in (("cq", "L0_attn_cq"), ("ck", "L0_attn_ck"),
                    ("cv", "L0_attn_cv")):
        sh[nm] = _t(g(key).reshape(KT, 128, 4).transpose(1, 0, 2).reshape(128, KT * 4))
    sh["Wf1"] = _t(g("L0_attn_Wf1").reshape(KT, 128, 128).transpose(1, 0, 2))
    sh["Wf2"] = g("L0_attn_Wf2")
    sh["Wb"] = _t(g("L0_attn_Wb").reshape(KT, 128, H).transpose(1, 0, 2))
    sh["Wg1"] = _t(g("L0_attn_Wg1").reshape(KT, 128, 128).transpose(1, 0, 2))
    sh["Wg2"] = g("L0_attn_Wg2")
    sh["bg2"] = _t(g("L0_attn_bg2").reshape(1, D))
    sh["nw"] = _t(g("L0_attn_norm_w").reshape(128, 1))
    sh["Wo"] = _t(g("L0_attn_Wo").reshape(KT, 128, KT, 128).transpose(1, 0, 2, 3))
    sh["rtr"] = _t(g("L0_ffn_w_router").T.reshape(KT, 128, 3).transpose(1, 0, 2))
    sh["b0"] = _t(g("L0_ffn_simple_bias").reshape(KT, 128).T)
    sh["wdn"] = _t(g("L0_ffn_w_down").T.reshape(KT, 128, 32).transpose(1, 0, 2))
    sh["wup"] = _t(g("L0_ffn_w_up").T.reshape(32, KT, 128))
    for nm, key in (("l1s", "L0_ffn_ln1_scale"), ("l1b", "L0_ffn_ln1_bias"),
                    ("l2s", "L0_ffn_ln2_scale"), ("l2b", "L0_ffn_ln2_bias")):
        sh[nm] = _t(g(key).reshape(KT, 128).T)
    wo = g("L0_ffn_wo")
    sh["fwo"] = _t(wo.reshape(KT, 128, KT, 128).transpose(0, 3, 2, 1))
    sh["ff1"] = _t(g("L0_ffn_w_ff1").reshape(64, 128, KT, 128).transpose(0, 3, 2, 1))
    sh["ff2"] = _t(g("L0_ffn_w_ff2").reshape(KT, 128, 4, 16, 128).transpose(
        0, 2, 4, 3, 1))

    wq, wk, wv = g("L0_ffn_wq"), g("L0_ffn_wk"), g("L0_ffn_wv")
    maps = []
    for c in range(NCORE):
        m = dict(sh)
        m["x_strip"] = _t(x0[c * STRIP:(c + 1) * STRIP])
        halo = np.zeros((128, D), np.float32)
        if c > 0:
            halo[125:128] = x0[c * STRIP - 3:c * STRIP]
        m["x_halo"] = halo
        mu = np.zeros((128, NCORE), np.float32)
        mu[:, :c] = 1.0
        m["mu"] = mu
        ci = np.zeros((128, KT), np.uint32)
        for kt in range(KT):
            ci[:, kt] = (kt // 2) * D + c * 256 + (kt % 2) * 128 + np.arange(128)
        m["ctxidx"] = ci
        for nm, w in (("fwq", wq), ("fwk", wk), ("fwv", wv)):
            ws = w[256 * c:256 * (c + 1)]
            m[nm] = _t(ws.reshape(2, 128, KT, 128).transpose(3, 2, 0, 1))
        maps.append(m)
    return maps



class Sem:
    __slots__ = ("h", "n", "name")

    def __init__(self, ctx, nc, name):
        self.h = ctx.enter_context(nc.semaphore(name))
        self.n = 0
        self.name = name


class Em:
    def __init__(self, nc, ctx):
        self.nc = nc
        self.ctx = ctx
        self.ops = {e: [] for e in ENGINES}

    def sem(self, name):
        return Sem(self.ctx, self.nc, name)

    def q(self, eng, fn, inc=None, k=1):
        incs = []
        if inc is not None:
            incs = [(inc, k)] if isinstance(inc, Sem) else list(inc)
        for s, kk in incs:
            s.n += kk

        def thunk(e, fn=fn, incs=tuple(incs)):
            inst = fn(e)
            for s, kk in incs:
                inst.then_inc(s.h, kk)

        self.ops[eng].append(thunk)

    def wait(self, eng, s, val=None):
        v = s.n if val is None else val
        if v > 0:
            self.ops[eng].append(lambda e, h=s.h, v=v: e.wait_ge(h, v))

    def barrier(self, name, engines=ENGINES):
        prog = getattr(self, "progress_sems", [])
        for e in engines:
            for s in prog:
                self.wait(e, s)
        b = self.sem("bar_" + name)
        for e in engines:
            self.ops[e].append(lambda en, h=b.h: en.nop().then_inc(h, 1))
            b.n += 1
        tgt = b.n
        for e in engines:
            self.wait(e, b, tgt)



EXPECTED_IN_SHAPES = {}


def build_program(taps=()):
    nc = bass.Bass("TRN2", target_bir_lowering=False)
    gctx = ExitStack()
    io = {}

    def din(name, shape, dt=F32):
        EXPECTED_IN_SHAPES[name] = tuple(shape)
        io[name] = nc.dram_tensor(name, list(shape), dt, kind="ExternalInput").ap()

    def dout(name, shape, dt=F32):
        io[name] = nc.dram_tensor(name, list(shape), dt, kind="ExternalOutput").ap()

    def dint(name, shape, dt=F32):
        io[name] = nc.dram_tensor(name, list(shape), dt, kind="Internal").ap()

    din("x_strip", (STRIP, D)); din("x_halo", (128, D)); din("mu", (128, NCORE))
    din("ctxidx", (128, KT), U32)
    for nm in ("Wq", "Wk", "Wv", "Ww"):
        din(nm, (KT, 128, KT, 128))
    din("Wo", (128, KT, KT, 128))
    for nm in ("cq", "ck", "cv"):
        din(nm, (128, KT * 4))
    din("Wf1", (128, KT, 128)); din("Wf2", (128, D)); din("Wb", (128, KT, H))
    din("Wg1", (128, KT, 128)); din("Wg2", (128, D)); din("bg2", (1, D))
    din("nw", (128, 1)); din("rtr", (128, KT, 3)); din("b0", (128, KT))
    din("wdn", (128, KT, 32)); din("wup", (32, KT, 128))
    for nm in ("l1s", "l1b", "l2s", "l2b"):
        din(nm, (128, KT))
    din("fwo", (KT, 128, KT, 128))
    din("ff1", (64, 128, KT, 128)); din("ff2", (KT, 4, 128, 16, 128))
    for nm in ("fwq", "fwk", "fwv"):
        din(nm, (128, KT, 2, 128))

    dout("y_strip", (STRIP, D))

    for nm in ("st_q", "st_k", "st_Kt", "st_Vt", "st_U", "st_A", "st_E", "st_SIG"):
        dint(nm, (NCH, 128, D), BF16)
    dint("st_rows", (NCH, 2, D))
    dint("st_raw", (128, KT * 1027), BF16)
    dint("st_hln", (128, KT * (STRIP + 128)), BF16)
    dint("st_dbg2", (128, 64))
    dint("st_dbg3", (NCH, 128, D), BF16)
    dint("st_dbg4", (NCH, 128, D), BF16)
    dint("cc_in", (258, D)); dint("cc_out", (NCORE * 258, D))
    dint("cc_xn_in", (D, STRIP), BF16); dint("cc_xn_out", (NCORE * D, STRIP), BF16)
    dint("cc_ctx_in", (D, STRIP), BF16)
    dint("cc_ctx_out", (NCORE * D, STRIP), BF16)
    dint("sp_x1", (NCH, KT, 128, 128))
    dint("sp_hs", (KT, 128, STRIP), BF16)
    dint("sp_xt", (KT, 128, STRIP), BF16)
    dint("sp_lr", (KT, 128, STRIP), BF16)
    dint("sp_x2", (KT, 128, STRIP))
    dint("st_ffq", (2, 128, T), BF16)
    dint("st_ffk", (2, 128, T), BF16)
    dint("st_ffv", (2, 128, T), BF16)
    dint("st_a", (64, 128, STRIP), BF16)

    cst = {}
    cst["ident"] = nc.inline_tensor(np.eye(128, dtype=BF16NP), name="c_id").ap()
    cst["triu"] = nc.inline_tensor(
        np.triu(np.ones((128, 128), np.float32)), name="c_triu").ap()
    s_ = np.arange(128)[:, None]
    t_ = np.arange(128)[None, :]
    lneg = (s_ <= REFI).astype(np.float32) - (s_ <= t_).astype(np.float32)
    cst["lneg"] = nc.inline_tensor(_t(lneg), name="c_lneg").ap()
    ext = np.zeros((128, 65), np.float32)
    ext[:, 0] = -(np.arange(128) > REFI).astype(np.float32)
    ext[:, 64] = -(np.arange(128) <= REFI).astype(np.float32)
    cst["gext"] = nc.inline_tensor(ext, name="c_gext").ap()
    cst["ones_r"] = nc.inline_tensor(np.ones((1, 128), np.float32), name="c_or").ap()
    o65 = np.zeros((65, 128), np.float32); o65[0] = 1.0; o65[64] = 1.0
    cst["ones_r65"] = nc.inline_tensor(o65, name="c_or65").ap()
    cst["ones_r16"] = nc.inline_tensor(np.ones((1, 128), BF16NP), name="c_or16").ap()
    cst["ones_c16"] = nc.inline_tensor(np.ones((128, 1), BF16NP), name="c_oc16").ap()
    dm = np.zeros((128, 4 * 512), BF16NP)
    kv_ = np.arange(128)[:, None]
    ql_ = np.arange(512)[None, :]
    for j in range(4):
        dm[:, j * 512:(j + 1) * 512] = (ql_ >= 128 * j + kv_).astype(BF16NP)
    cst["dmask"] = nc.inline_tensor(dm, name="c_dmask").ap()
    cst["ones_c32"] = nc.inline_tensor(np.ones((128, 1), np.float32),
                                       name="c_oc32").ap()
    cst["ident32"] = nc.inline_tensor(np.eye(128, dtype=np.float32),
                                      name="c_id32").ap()

    em = Em(nc, gctx)

    gsb = lambda n, s, d=F32: gctx.enter_context(nc.sbuf_tensor(n, list(s), d))
    P1 = gsb("P1", (128, 2048)); P2 = gsb("P2", (128, 2048))
    P3 = gsb("P3", (128, 2048)); P4 = gsb("P4", (128, 2048))
    B1 = gsb("B1", (128, 2048), BF16); B2 = gsb("B2", (128, 2048), BF16)
    B3 = gsb("B3", (128, 2048), BF16); B4 = gsb("B4", (128, 2048), BF16)
    B5 = gsb("B5", (128, 2048), BF16)
    rowA = gsb("rowA", (65, 2048)); rowB = rowA
    rowC = gsb("rowC", (1, 2048)); rowD = gsb("rowD", (65, 2048))
    colA = gsb("colA", (128, 64)); colB = gsb("colB", (128, 64))
    beta = gsb("beta", (128, NCH * H))
    onesR = gsb("onesR", (1, 128)); onesR16 = gsb("onesR16", (1, 128), BF16)
    onesR65 = gsb("onesR65", (65, 128))
    onesC16 = gsb("onesC16", (128, 1), BF16)
    idS = gsb("idS", (128, 128), BF16)
    triuS = gsb("triuS", (128, 128))
    lnegS = gsb("lnegS", (128, 128)); gextS = gsb("gextS", (128, 65))
    t1S = gsb("t1S", (128, 128), BF16); t2S = gsb("t2S", (128, 128), BF16)
    onesC32 = gsb("onesC32", (128, 1)); idS32 = gsb("idS32", (128, 128))

    G = dict(nc=nc, em=em, io=io, cst=cst, gctx=gctx,
             P1=P1, P2=P2, P3=P3, P4=P4, B1=B1, B2=B2, B3=B3, B4=B4, B5=B5,
             rowA=rowA, rowB=rowB, rowC=rowC, rowD=rowD,
             colA=colA, colB=colB, beta=beta, onesR=onesR, onesR16=onesR16,
             onesC16=onesC16, idS=idS, triuS=triuS, lnegS=lnegS, gextS=gextS,
             t1S=t1S, t2S=t2S, onesC32=onesC32, idS32=idS32, onesR65=onesR65)

    for s in ("ld", "gw", "mm", "ev", "sc", "sg", "pr", "cc", "fin"):
        G[s] = em.sem(s)
    em.progress_sems = [G[s] for s in ("ld", "gw", "mm", "ev", "sc", "sg", "cc")]

    colZ = gsb("colZ", (128, 4))
    nc.const_aps.register(F32, 1e-5, colZ[:, 0:1])
    nc.const_aps.register(F32, 1e-6, colZ[:, 1:2])
    nc.const_aps.register(F32, -C0F, colZ[:, 2:3])
    czv = np.zeros((128, 4), np.float32)
    czv[:, 0] = 1e-5; czv[:, 1] = 1e-6; czv[:, 2] = -C0F
    cst["colz"] = nc.inline_tensor(czv, name="c_colz").ap()
    em.q("sync", lambda s: s.dma_start(out=colZ[:], in_=cst["colz"]),
         inc=G["ld"], k=16)
    em.wait("scalar", G["ld"])
    em.wait("vector", G["ld"])

    ld = G["ld"]
    for dst, src in ((onesR, "ones_r"), (onesR16, "ones_r16"),
                     (onesC16, "ones_c16"), (idS, "ident"), (triuS, "triu"),
                     (lnegS, "lneg"), (gextS, "gext"), (onesC32, "ones_c32"),
                     (idS32, "ident32"), (onesR65, "ones_r65")):
        em.q("sync", lambda s, d=dst, c=cst[src]: s.dma_start(out=d[:], in_=c),
             inc=ld, k=16)

    if PHASES["A"]:
        phaseA(G)
    if PHASES["X1"]:
        phaseX1(G)
    if PHASES["C"]:
        phaseC(G)
    if PHASES["F0"]:
        phaseF0(G)
    if PHASES["F1"]:
        phaseF1(G)
    if PHASES["F2"]:
        phaseF2(G)

    if taps:
        em.barrier("taps")
        ts = em.sem("tapsem")
        for nm in taps:
            ap = io[nm]
            tp = nc.dram_tensor("tap_" + nm, list(ap.shape), ap.dtype,
                                kind="ExternalOutput").ap()
            em.q("gpsimd", lambda g, o=tp, i=ap: g.dma_start(out=o, in_=i),
                 inc=ts, k=16)
        em.wait("gpsimd", ts)

    blk = gctx.enter_context(nc.Block())
    for eng in ENGINES:
        ops = em.ops[eng]
        getattr(blk, eng)(lambda e, ops=ops: [op(e) for op in ops])
    gctx.close()
    return nc



def phaseA(G):
    nc, em, io, cst = G["nc"], G["em"], G["io"], G["cst"]
    ld, gw, mm, ev, sc, sg, pr = (G[k] for k in
                                  ("ld", "gw", "mm", "ev", "sc", "sg", "pr"))
    P1, P2, P3, P4 = G["P1"], G["P2"], G["P3"], G["P4"]
    B1, B2, B3 = G["B1"], G["B2"], G["B3"]
    rowA, rowC, rowD, colA = G["rowA"], G["rowC"], G["rowD"], G["colA"]
    beta = G["beta"]

    actx = ExitStack()
    asb = lambda n, s, d=F32: actx.enter_context(nc.sbuf_tensor(n, list(s), d))
    aps = lambda n, s, d=F32: actx.enter_context(nc.psum_tensor(n, list(s), d))

    HW = STRIP + 128
    hlnT = asb("hlnT", (128, KT * HW), BF16)
    SkR = asb("SkR", (128, 2048))
    SvR = asb("SvR", (128, 2048))
    G["SkR"], G["SvR"] = SkR, SvR

    em.q("vector", lambda v: v.memset(SkR[:], 0.0), inc=ev)
    em.q("vector", lambda v: v.memset(SvR[:], 0.0), inc=ev)

    for nm in ("Wf1", "Wg1"):
        em.q("gpsimd", lambda g, n=nm: g.dma_start(
            out=smallw[n][:].rearrange("p (a c) -> p a c", a=KT),
            in_=io[n]), inc=gw, k=16)
    for nm in ("Wf2", "Wg2", "bg2"):
        em.q("gpsimd", lambda g, n=nm: g.dma_start(out=smallw[n][:], in_=io[n]),
             inc=gw, k=16)
    em.q("gpsimd", lambda g: g.dma_start(
        out=smallw["Wb"][:].rearrange("p (a c) -> p a c", a=KT),
        in_=io["Wb"]), inc=gw, k=16)
    for nm in ("cq", "ck", "cv"):
        em.q("sync", lambda s, n=nm: s.dma_start(out=convw[n][:], in_=io[n]),
             inc=ld, k=16)

    convw = {}
    for nm in ("cq", "ck", "cv"):
        convw[nm] = asb("sw_" + nm, (128, KT * 4))
    a1ctx = ExitStack()
    a1ps = lambda n, s, d=F32: a1ctx.enter_context(nc.psum_tensor(n, list(s), d))
    a1sb = lambda n, s, d=F32: a1ctx.enter_context(nc.sbuf_tensor(n, list(s), d))
    smallw = {}
    for nm, shape in (("Wf1", (128, KT * 128)), ("Wf2", (128, D)),
                      ("Wb", (128, KT * H)), ("Wg1", (128, KT * 128)),
                      ("Wg2", (128, D)), ("bg2", (1, D))):
        smallw[nm] = a1sb("sw_" + nm, shape, BF16)
    p_tr = a1ps("pA_tr", (128, 1024), BF16)
    p_g1 = a1ps("pA_g1", (128, 128))
    p_gg = a1ps("pA_gg", (128, 512))
    p_cum = a1ps("pA_cum", (128, 1024))
    p_row = a1ps("pA_row", (65, 512))
    p_bet = a1ps("pA_bet", (128, H))

    xrow, xnt = P1, P2
    gsp, gtmp = P3, P4
    xn16, At16, SIG16 = B1, B2, B3
    mvc = colA

    def ln_tile(j, is_halo):
        em.wait("scalar", sg)
        em.wait("vector", sg)
        em.wait("sync", ev)
        em.wait("sync", sc)
        src = (io["x_halo"] if is_halo
               else io["x_strip"][j * 128:(j + 1) * 128, :])
        em.q("sync", lambda s, src=src: s.dma_start(out=xrow[:], in_=src),
             inc=ld, k=16)
        em.wait("vector", ld)
        em.wait("scalar", ld)
        em.q("vector", lambda v: v.reduce_sum(mvc[:, 2:3], xrow[:], axis=AX.X),
             inc=ev)
        em.q("scalar", lambda s: s.activation(xnt[:], xrow[:], AF.Square), inc=sc)
        em.wait("vector", sc)
        em.q("vector", lambda v: v.reduce_sum(mvc[:, 3:4], xnt[:], axis=AX.X),
             inc=ev)
        em.wait("vector", ev)
        em.q("vector", lambda v: v.tensor_scalar(
            mvc[:, 0:1], mvc[:, 2:3], 1.0 / D, None, op0=ALU.mult), inc=ev)
        em.wait("vector", ev)
        em.q("vector", lambda v: v.tensor_mul(mvc[:, 4:5], mvc[:, 0:1], mvc[:, 0:1]),
             inc=ev)
        em.q("vector", lambda v: v.tensor_scalar(
            mvc[:, 5:6], mvc[:, 3:4], 1.0 / D, None, op0=ALU.mult), inc=ev)
        em.wait("vector", ev)
        em.q("vector", lambda v: v.tensor_sub(mvc[:, 1:2], mvc[:, 5:6], mvc[:, 4:5]),
             inc=ev)
        em.wait("scalar", ev)
        em.q("scalar", lambda s: s.activation(mvc[:, 1:2], mvc[:, 1:2], AF.Sqrt,
                                              bias=1e-5), inc=sc)
        em.wait("vector", sc)
        em.q("vector", lambda v: v.reciprocal(mvc[:, 1:2], mvc[:, 1:2]), inc=ev)
        em.wait("vector", ev)
        em.q("vector", lambda v: v.tensor_scalar(
            xnt[:], xrow[:], mvc[:, 0:1], mvc[:, 1:2],
            op0=ALU.subtract, op1=ALU.mult), inc=ev)
        em.wait("vector", ev)
        em.q("vector", lambda v: v.tensor_copy(xn16[:], xnt[:]), inc=ev)
        em.wait("tensor", ev)
        col0 = 0 if is_halo else 128 + j * 128
        h3 = hlnT[:].rearrange("p (a w) -> p a w", a=KT)
        for b in range(2):
            for k in range(8):
                kt = b * 8 + k
                em.q("tensor", lambda t, kt=kt, k=k: t.transpose(
                    p_tr[:, k * 128:(k + 1) * 128],
                    xn16[:, kt * 128:(kt + 1) * 128], G["idS"][:]),
                    inc=(mm if k == 7 else None))
            em.wait("vector", mm)
            em.q("vector", lambda v, b=b, col0=col0: v.tensor_copy(
                h3[:, b * 8:(b + 1) * 8, col0:col0 + 128],
                p_tr[:].rearrange("p (a w) -> p a w", a=8)), inc=ev)
            em.wait("tensor", ev)

    def gpath_tile(j):
        col0 = 128 + j * 128
        h3 = hlnT[:].rearrange("p (a w) -> p a w", a=KT)
        hj = h3[:, :, col0:col0 + 128]
        em.wait("tensor", G["gw"])
        for kt in range(KT):
            em.q("tensor", lambda t, kt=kt: t.matmul(
                p_g1[:], smallw["Wf1"][:, kt * 128:(kt + 1) * 128], hj[:, kt, :],
                start=(kt == 0), stop=(kt == KT - 1)),
                inc=(mm if kt == KT - 1 else None))
        em.wait("vector", mm)
        em.q("vector", lambda v: v.tensor_copy(G["t1S"][:], p_g1[:]), inc=ev)
        em.wait("tensor", ev)
        for kt in range(KT):
            em.q("tensor", lambda t, kt=kt: t.matmul(
                p_bet[:], hj[:, kt, :],
                smallw["Wb"][:, kt * H:(kt + 1) * H],
                start=(kt == 0), stop=(kt == KT - 1)),
                inc=(mm if kt == KT - 1 else None))
        em.wait("scalar", mm)
        em.q("scalar", lambda s, j=j: s.activation(
            beta[:, j * H:(j + 1) * H], p_bet[:], AF.Sigmoid), inc=sc)
        em.wait("tensor", sc)
        for kt in range(KT):
            em.q("tensor", lambda t, kt=kt: t.matmul(
                p_g1[:], smallw["Wg1"][:, kt * 128:(kt + 1) * 128], hj[:, kt, :],
                start=(kt == 0), stop=(kt == KT - 1)),
                inc=(mm if kt == KT - 1 else None))
        em.wait("vector", mm)
        em.q("vector", lambda v: v.tensor_copy(G["t2S"][:], p_g1[:]), inc=ev)
        for q4 in range(4):
            em.q("tensor", lambda t, q4=q4: t.matmul(
                p_gg[:], G["t1S"][:], smallw["Wf2"][:, q4 * 512:(q4 + 1) * 512],
                start=True, stop=True), inc=mm)
            em.wait("scalar", mm)
            em.q("scalar", lambda s, q4=q4: s.activation(
                gtmp[:, q4 * 512:(q4 + 1) * 512], p_gg[:], AF.Exp, scale=-1.0),
                inc=sc)
            em.wait("tensor", sc)
        em.q("scalar", lambda s: s.activation(gsp[:], gtmp[:], AF.Ln, bias=1.0),
             inc=sc)
        em.wait("tensor", ev)
        for q4 in range(4):
            em.q("tensor", lambda t, q4=q4: t.matmul(
                p_gg[:], G["t2S"][:], smallw["Wg2"][:, q4 * 512:(q4 + 1) * 512],
                start=True, stop=False))
            em.q("tensor", lambda t, q4=q4: t.matmul(
                p_gg[:], G["onesR16"][:], smallw["bg2"][:, q4 * 512:(q4 + 1) * 512],
                start=False, stop=True), inc=mm)
            em.wait("scalar", mm)
            em.q("scalar", lambda s, q4=q4: s.activation(
                SIG16[:, q4 * 512:(q4 + 1) * 512], p_gg[:], AF.Sigmoid), inc=sc)
            em.wait("tensor", sc)
        em.wait("sync", sc)
        em.q("sync", lambda s, j=j: s.dma_start(out=io["st_SIG"][j], in_=SIG16[:]),
             inc=sg, k=16)
        em.wait("tensor", sc)
        for half in range(2):
            for q2 in range(2):
                em.q("tensor", lambda t, half=half, q2=q2: t.matmul(
                    p_cum[:, q2 * 512:(q2 + 1) * 512], G["lnegS"][:],
                    gsp[:, half * 1024 + q2 * 512:half * 1024 + (q2 + 1) * 512],
                    start=True, stop=True), inc=(mm if q2 == 1 else None))
            em.wait("scalar", mm)
            em.q("scalar", lambda s, half=half: s.activation(
                At16[:, half * 1024:(half + 1) * 1024], p_cum[:], AF.Exp), inc=sc)
            em.q("scalar", lambda s, half=half: s.activation(
                xn16[:, half * 1024:(half + 1) * 1024], p_cum[:], AF.Exp,
                scale=-1.0), inc=sc)
            em.wait("tensor", sc)
        em.wait("sync", sc)
        em.q("sync", lambda s, j=j: s.dma_start(out=io["st_A"][j], in_=At16[:]),
             inc=sg, k=16)
        em.q("sync", lambda s, j=j: s.dma_start(out=io["st_E"][j], in_=xn16[:]),
             inc=sg, k=16)
        for q4 in range(4):
            em.q("tensor", lambda t, q4=q4: t.matmul(
                p_row[:], G["gextS"][:], gsp[:, q4 * 512:(q4 + 1) * 512],
                start=True, stop=True), inc=mm)
            em.wait("scalar", mm)
            em.q("scalar", lambda s, q4=q4: s.activation(
                rowA[:, q4 * 512:(q4 + 1) * 512], p_row[:], AF.Exp), inc=sc)
            em.wait("tensor", sc)
        em.wait("sync", sc)
        em.q("sync", lambda s, j=j: s.dma_start(out=io["st_rows"][j][0:1],
                                                in_=rowA[0:1, :]), inc=sg, k=16)
        em.q("sync", lambda s, j=j: s.dma_start(out=io["st_rows"][j][1:2],
                                                in_=rowA[64:65, :]), inc=sg, k=16)
        em.wait("vector", sc)
        em.q("vector", lambda v: v.tensor_copy(rowD[0:1, :], rowA[64:65, :]),
             inc=ev)
        em.wait("vector", ev)
        if j == 0:
            em.q("vector", lambda v: v.tensor_mul(rowC[:], rowA[0:1, :],
                                                  rowD[0:1, :]), inc=ev)
        else:
            em.q("vector", lambda v: v.tensor_mul(gtmp[0:1, :], rowA[0:1, :],
                                                  rowD[0:1, :]), inc=ev)
            em.wait("vector", ev)
            em.q("vector", lambda v: v.tensor_mul(rowC[:], rowC[:], gtmp[0:1, :]),
                 inc=ev)

    ln_tile(0, True)
    for j in range(NCH):
        ln_tile(j, False)
        gpath_tile(j)
    if DBGRAW:
        em.wait("sync", G["ev"])
        em.wait("sync", G["sc"])
        em.q("sync", lambda s: s.dma_start(out=io["st_hln"], in_=hlnT[:]),
             inc=G["sg"], k=16)
    em.barrier("endA1")
    a1ctx.close()

    phaseA2(G, actx, asb, aps, hlnT, convw)
    gw, sg2 = G["gw"], G["sg"]
    em.wait("gpsimd", G["ev"])
    em.wait("gpsimd", G["sg"])
    em.q("gpsimd", lambda g: g.dma_start(out=io["cc_in"][0:128, :], in_=SkR[:]),
         inc=gw, k=16)
    em.q("gpsimd", lambda g: g.dma_start(out=io["cc_in"][128:256, :], in_=SvR[:]),
         inc=gw, k=16)
    em.q("gpsimd", lambda g: g.dma_start(out=io["cc_in"][256:257, :],
                                         in_=G["rowC"][:]), inc=gw, k=16)
    em.wait("gpsimd", gw)
    em.barrier("endA")
    actx.close()


def phaseA2(G, actx, asb, aps, hlnT, convw):
    nc, em, io = G["nc"], G["em"], G["io"]
    ld, gw, mm, ev, sc, sg, pr = (G[k] for k in
                                  ("ld", "gw", "mm", "ev", "sc", "sg", "pr"))
    P1, P2, P3, P4 = G["P1"], G["P2"], G["P3"], G["P4"]
    B1, B2, B3, B4, B5 = G["B1"], G["B2"], G["B3"], G["B4"], G["B5"]
    rowB, colA = G["rowB"], G["colA"]
    beta, SkR, SvR = G["beta"], G["SkR"], G["SvR"]

    raw = asb("rawA", (128, KT * 1027), BF16)
    gcolsA = asb("gcolsA", (128, 32))
    cvo = asb("cvoA", (128, KT * 1024), BF16)
    wsl = asb("wslA", (128, 2 * 2048), BF16)
    p_h = aps("pA_h", (128, 128))
    p_s0 = aps("pA_s0", (128, 512))
    p_s1 = aps("pA_s1", (128, 512))
    p_tp = aps("pA_tp", (128, 1024), BF16)
    p_c1 = aps("pA_c1", (128, 512))
    p_c2 = aps("pA_c2", (128, 512))
    p_gb = aps("pA_gb", (128, 512))
    prj = em.sem("a_prj")

    h3 = hlnT[:].rearrange("p (a w) -> p a w", a=KT)
    raw3 = raw[:].rearrange("p (a w) -> p a w", a=KT)
    cvo3 = cvo[:].rearrange("p (a w) -> p a w", a=KT)

    def proj_conv(wname, cname):
        em.wait("vector", sg)
        for ot in range(KT):
            half = ot % 2
            dst = wsl[:, half * 2048:(half + 1) * 2048]
            if prj.n >= 2:
                em.wait("gpsimd", prj, prj.n - 1)
            em.q("gpsimd", lambda g, ot=ot, dst=dst, w=wname: g.dma_start(
                out=dst.rearrange("p (a c) -> p a c", a=KT),
                in_=io[w][ot]), inc=gw, k=16)
            em.wait("tensor", gw)
            for kt in range(KT):
                em.q("tensor", lambda t, kt=kt, dst=dst: t.matmul(
                    p_h[:], dst[:, kt * 128:(kt + 1) * 128], h3[:, kt, 0:128],
                    start=(kt == 0), stop=(kt == KT - 1)),
                    inc=(mm if kt == KT - 1 else None))
            for th, ps in ((0, p_s0), (1, p_s1)):
                for kt in range(KT):
                    inc = None
                    if kt == KT - 1:
                        inc = prj if th == 1 else mm
                    em.q("tensor", lambda t, kt=kt, th=th, ps=ps, dst=dst: t.matmul(
                        ps[:], dst[:, kt * 128:(kt + 1) * 128],
                        h3[:, kt, 128 + th * 512:128 + (th + 1) * 512],
                        start=(kt == 0), stop=(kt == KT - 1)),
                        inc=inc)
            em.wait("vector", mm)
            em.wait("vector", prj)
            em.q("vector", lambda v, ot=ot: v.tensor_copy(
                raw3[:, ot, 0:3], p_h[:, 125:128]), inc=ev)
            em.q("vector", lambda v, ot=ot: v.tensor_copy(
                raw3[:, ot, 3:515], p_s0[:]), inc=ev)
            em.q("vector", lambda v, ot=ot: v.tensor_copy(
                raw3[:, ot, 515:1027], p_s1[:]), inc=ev)
            em.wait("tensor", ev)
        if DBGRAW and wname == "Wq":
            em.wait("sync", ev)
            em.q("sync", lambda s: s.dma_start(out=io["st_raw"], in_=raw[:]),
                 inc=sg, k=16)
        cw = convw[cname]
        em.wait("vector", sg)
        em.wait("scalar", sg)
        em.wait("vector", ev)
        for ot in range(KT):
            em.q("vector", lambda v, ot=ot: v.tensor_scalar(
                P1[:, 0:1024], raw3[:, ot, 0:1024], cw[:, ot * 4:ot * 4 + 1],
                None, op0=ALU.mult))
            for tap in range(1, 4):
                em.q("vector", lambda v, ot=ot, tap=tap: v.scalar_tensor_tensor(
                    P1[:, 0:1024], raw3[:, ot, tap:tap + 1024],
                    cw[:, ot * 4 + tap:ot * 4 + tap + 1], P1[:, 0:1024],
                    op0=ALU.mult, op1=ALU.add), inc=(ev if tap == 3 else None))
            em.wait("scalar", ev)
            em.q("scalar", lambda s, ot=ot: s.activation(
                cvo3[:, ot, :], P1[:, 0:1024], AF.Silu), inc=sc)
            em.wait("vector", sc)

    def stage_chan(stname):
        em.wait("sync", G["sc"])
        for j in range(NCH):
            em.q("sync", lambda s, j=j, st=stname: s.dma_start(
                out=io[st][j].rearrange("p (a w) -> p a w", a=KT),
                in_=cvo[:].rearrange("p (a c w) -> p a c w", a=KT, c=NCH)[
                    :, :, j, :]), inc=sg, k=16)

    def transpose_chunk(j, dst16):
        cvo4 = cvo[:].rearrange("p (a c w) -> p a c w", a=KT, c=NCH)
        em.wait("tensor", sc)
        for b in range(2):
            for k in range(8):
                kt = b * 8 + k
                em.q("tensor", lambda t, kt=kt, k=k, j=j: t.transpose(
                    p_tp[:, k * 128:(k + 1) * 128], cvo4[:, kt, j, :],
                    G["idS"][:]), inc=(mm if k == 7 else None))
            em.wait("vector", mm)
            em.q("vector", lambda v, b=b: v.tensor_copy(
                dst16[:].rearrange("p (a w) -> p a w", a=KT)[:, b * 8:(b + 1) * 8, :],
                p_tp[:].rearrange("p (a w) -> p a w", a=8)), inc=ev)
            em.wait("tensor", ev)

    proj_conv("Wq", "cq")
    stage_chan("st_q")
    proj_conv("Wk", "ck")
    stage_chan("st_k")
    for j in range(NCH):
        transpose_chunk(j, B1)
        em.wait("sync", ev)
        em.q("sync", lambda s, j=j: s.dma_start(out=io["st_Kt"][j], in_=B1[:]),
             inc=sg, k=16)
        em.wait("vector", sg)
    proj_conv("Wv", "cv")
    for j in range(NCH):
        transpose_chunk(j, B1)
        em.wait("sync", ev)
        em.q("sync", lambda s, j=j: s.dma_start(out=io["st_Vt"][j], in_=B1[:]),
             inc=sg, k=16)
        em.wait("vector", sg)
    proj_conv("Ww", "cv")
    for j in range(NCH):
        transpose_chunk(j, B1)
        em.wait("sync", mm)
        em.wait("sync", ev)
        em.wait("sync", sc)
        em.q("sync", lambda s, j=j: s.dma_start(out=B3[:], in_=io["st_E"][j]),
             inc=ld, k=16)
        em.q("sync", lambda s, j=j: s.dma_start(out=B4[:], in_=io["st_Kt"][j]),
             inc=ld, k=16)
        em.q("sync", lambda s, j=j: s.dma_start(out=B5[:], in_=io["st_Vt"][j]),
             inc=ld, k=16)
        em.q("sync", lambda s, j=j: s.dma_start(out=rowB[0:1, :],
                                                in_=io["st_rows"][j][0:1]),
             inc=ld, k=16)
        em.q("sync", lambda s, j=j: s.dma_start(out=rowB[64:65, :],
                                                in_=io["st_rows"][j][1:2]),
             inc=ld, k=16)
        em.wait("scalar", ev)
        em.q("scalar", lambda s: s.activation(P2[:], B1[:], AF.Square), inc=sc)
        em.wait("vector", sc)
        em.q("vector", lambda v: v.reduce_sum(
            colA[:, 0:16].rearrange("p (h o) -> p h o", o=1),
            P2[:].rearrange("p (h m) -> p h m", m=128), axis=AX.X), inc=ev)
        em.wait("scalar", ev)
        em.q("scalar", lambda s: s.activation(colA[:, 0:16], colA[:, 0:16],
                                              AF.Sqrt, bias=1e-6), inc=sc)
        em.wait("vector", sc)
        em.q("vector", lambda v: v.reciprocal(colA[:, 0:16], colA[:, 0:16]),
             inc=ev)
        em.wait("vector", ev)
        em.q("vector", lambda v, j=j: v.tensor_mul(
            colA[:, 0:16], colA[:, 0:16], beta[:, j * H:(j + 1) * H]), inc=ev)
        em.wait("vector", ld)
        em.wait("vector", ev)
        for h in range(H):
            em.q("vector", lambda v, h=h: v.tensor_scalar(
                P2[:, h * 128:(h + 1) * 128], B1[:, h * 128:(h + 1) * 128],
                colA[:, h:h + 1], None, op0=ALU.mult),
                inc=(ev if h == H - 1 else None))
        em.wait("vector", ev)
        em.q("vector", lambda v: v.tensor_mul(B2[:], P2[:], B3[:]), inc=ev)
        em.wait("sync", ev)
        em.q("sync", lambda s, j=j: s.dma_start(out=io["st_U"][j], in_=B2[:]),
             inc=sg, k=16)
        em.wait("tensor", ld)
        for q4 in range(4):
            em.q("tensor", lambda t, q4=q4: t.matmul(
                p_gb[:], G["onesR"][:], rowB[0:1, q4 * 512:(q4 + 1) * 512],
                start=True, stop=True), inc=mm)
            em.wait("vector", mm)
            em.q("vector", lambda v, q4=q4: v.tensor_copy(
                P3[:, q4 * 512:(q4 + 1) * 512], p_gb[:]), inc=ev)
            em.wait("tensor", ev)
            em.q("tensor", lambda t, q4=q4: t.matmul(
                p_gb[:], G["onesR65"][64:65, :], rowB[64:65, q4 * 512:(q4 + 1) * 512],
                start=True, stop=True), inc=mm)
            em.wait("vector", mm)
            em.q("vector", lambda v, q4=q4: v.tensor_copy(
                P4[:, q4 * 512:(q4 + 1) * 512], p_gb[:]), inc=ev)
            em.wait("tensor", ev)
        for h in range(H):
            em.q("tensor", lambda t, h=h: t.transpose(
                p_gb[:, 2 * h:2 * h + 1], rowB[0:1, h * 128:(h + 1) * 128],
                G["idS32"][0:1, 0:1]))
            em.q("tensor", lambda t, h=h: t.transpose(
                p_gb[:, 2 * h + 1:2 * h + 2], rowB[64:65, h * 128:(h + 1) * 128],
                G["idS32"][64:65, 64:65]), inc=(mm if h == H - 1 else None))
        em.wait("vector", mm)
        em.q("vector", lambda v: v.tensor_copy(gcolsA[:], p_gb[:, 0:32]), inc=ev)
        em.wait("tensor", ev)
        for hg in range(4):
            for hh in range(4):
                h = hg * 4 + hh
                em.q("tensor", lambda t, h=h, hh=hh: t.matmul(
                    p_c1[:, hh * 128:(hh + 1) * 128],
                    B4[:, h * 128:(h + 1) * 128], B2[:, h * 128:(h + 1) * 128],
                    start=True, stop=True), inc=(mm if hh == 3 else None))
            for hh in range(4):
                h = hg * 4 + hh
                em.q("tensor", lambda t, h=h, hh=hh: t.matmul(
                    p_c2[:, hh * 128:(hh + 1) * 128],
                    B2[:, h * 128:(h + 1) * 128], B5[:, h * 128:(h + 1) * 128],
                    start=True, stop=True), inc=(mm if hh == 3 else None))
            em.wait("vector", mm)
            sl = slice(hg * 512, (hg + 1) * 512)
            em.q("vector", lambda v, sl=sl: v.tensor_mul(
                SkR[:, sl], SkR[:, sl], P4[:, sl]))
            em.q("vector", lambda v, sl=sl: v.tensor_add(
                SkR[:, sl], SkR[:, sl], p_c1[:]))
            em.q("vector", lambda v, sl=sl: v.tensor_mul(
                SkR[:, sl], SkR[:, sl], P3[:, sl]))
            for hh in range(4):
                h = hg * 4 + hh
                em.q("vector", lambda v, h=h: v.tensor_scalar(
                    SvR[:, h * 128:(h + 1) * 128], SvR[:, h * 128:(h + 1) * 128],
                    gcolsA[:, 2 * h + 1:2 * h + 2], None, op0=ALU.mult))
            em.q("vector", lambda v, sl=sl: v.tensor_add(
                SvR[:, sl], SvR[:, sl], p_c2[:]))
            for hh in range(4):
                h = hg * 4 + hh
                em.q("vector", lambda v, h=h: v.tensor_scalar(
                    SvR[:, h * 128:(h + 1) * 128], SvR[:, h * 128:(h + 1) * 128],
                    gcolsA[:, 2 * h:2 * h + 1], None, op0=ALU.mult),
                    inc=(ev if hh == 3 else None))
            em.wait("tensor", ev)



def phaseX1(G):
    nc, em, io = G["nc"], G["em"], G["io"]
    ld, gw, mm, ev, sc, sg, cc = (G[k] for k in
                                  ("ld", "gw", "mm", "ev", "sc", "sg", "cc"))
    rowC, rowD = G["rowC"], G["rowD"]
    P1, P2, P3 = G["P1"], G["P2"], G["P3"]

    xctx = ExitStack()
    muS = xctx.enter_context(nc.sbuf_tensor("muS", [128, NCORE], F32))
    SkI = xctx.enter_context(nc.sbuf_tensor("SkI", [128, 2048], F32))
    SvI = xctx.enter_context(nc.sbuf_tensor("SvI", [128, 2048], F32))
    G["SkI"], G["SvI"] = SkI, SvI
    G["xctx"] = xctx
    x1ps = ExitStack()
    p_gb = x1ps.enter_context(nc.psum_tensor("pX_gb", [128, 512], F32))

    em.q("sync", lambda s: s.dma_start(out=muS[:], in_=io["mu"]), inc=ld, k=16)
    em.q("gpsimd", lambda g: g.collective_compute(
        "AllGather", ALU.bypass, replica_groups=[list(range(NCORE))],
        ins=[io["cc_in"].opt()], outs=[io["cc_out"].opt()]), inc=cc, k=1)
    em.wait("gpsimd", cc)
    em.q("vector", lambda v: v.memset(SkI[:], 0.0), inc=ev)
    em.q("vector", lambda v: v.memset(SvI[:], 0.0), inc=ev)
    for j in range(NCORE):
        base = j * 258
        em.q("gpsimd", lambda g, base=base: g.dma_start(
            out=P1[:], in_=io["cc_out"][base:base + 128, :]), inc=gw, k=16)
        em.q("gpsimd", lambda g, base=base: g.dma_start(
            out=P2[:], in_=io["cc_out"][base + 128:base + 256, :]), inc=gw, k=16)
        em.q("gpsimd", lambda g, base=base: g.dma_start(
            out=rowD[0:1, :], in_=io["cc_out"][base + 256:base + 257, :]),
            inc=gw, k=16)
        em.wait("vector", gw)
        em.wait("vector", ld)
        em.q("vector", lambda v, j=j: v.tensor_scalar(
            rowD[64:65, :], rowD[0:1, :], 1.0, muS[0:1, j:j + 1],
            op0=ALU.subtract, op1=ALU.mult), inc=ev)
        em.wait("vector", ev)
        em.q("vector", lambda v: v.tensor_scalar(
            rowD[64:65, :], rowD[64:65, :], 1.0, None, op0=ALU.add), inc=ev)
        em.wait("tensor", ev)
        for q4 in range(4):
            em.q("tensor", lambda t, q4=q4: t.matmul(
                p_gb[:], G["onesR65"][64:65, :], rowD[64:65, q4 * 512:(q4 + 1) * 512],
                start=True, stop=True), inc=mm)
            em.wait("vector", mm)
            em.q("vector", lambda v, q4=q4: v.tensor_copy(
                P3[:, q4 * 512:(q4 + 1) * 512], p_gb[:]), inc=ev)
            em.wait("tensor", ev)
        em.q("vector", lambda v: v.tensor_mul(SkI[:], SkI[:], P3[:]))
        em.q("vector", lambda v, j=j: v.scalar_tensor_tensor(
            P1[:], P1[:], muS[:, j:j + 1], SkI[:], op0=ALU.mult, op1=ALU.add))
        em.q("vector", lambda v: v.tensor_copy(SkI[:], P1[:]))
        em.q("vector", lambda v: v.tensor_mul(SvI[:], SvI[:], P3[:]))
        em.q("vector", lambda v, j=j: v.scalar_tensor_tensor(
            P2[:], P2[:], muS[:, j:j + 1], SvI[:], op0=ALU.mult, op1=ALU.add))
        em.q("vector", lambda v: v.tensor_copy(SvI[:], P2[:]), inc=ev)
        em.wait("gpsimd", ev)
    em.barrier("endX1")
    x1ps.close()



def phaseC(G):
    nc, em, io = G["nc"], G["em"], G["io"]
    ld, gw, mm, ev, sc, sg = (G[k] for k in ("ld", "gw", "mm", "ev", "sc", "sg"))
    P1, P2, P3, P4 = G["P1"], G["P2"], G["P3"], G["P4"]
    B1, B2, B3, B4, B5 = (G[k] for k in ("B1", "B2", "B3", "B4", "B5"))
    rowB, colA, colB = G["rowB"], G["colA"], G["colB"]
    SkI, SvI = G["SkI"], G["SvI"]

    cctx = ExitStack()
    csb = lambda n, s, d=F32: cctx.enter_context(nc.sbuf_tensor(n, list(s), d))
    B6 = csb("B6c", (128, 2048), BF16)
    B7 = csb("B7c", (128, 2048), BF16)
    cps = lambda n, s, d=F32: cctx.enter_context(nc.psum_tensor(n, list(s), d))
    WoS = csb("WoS", (128, KT * KT * 128), BF16)
    attnC = csb("attnC", (128, 2048), BF16)
    SkP16 = csb("SkP16", (128, 2048), BF16)
    SvP16 = csb("SvP16", (128, 2048), BF16)
    esx = csb("esC", (128, 128))
    aA16 = csb("aA16", (128, 128), BF16)
    aAT16 = csb("aAT16", (128, 128), BF16)
    UT16 = csb("UT16", (128, 128), BF16)
    Pm16 = csb("Pm16", (128, 128), BF16)
    W2m16 = csb("W2m16", (128, 128), BF16)
    gcols = csb("gcolsC", (128, 32))
    nwS = csb("nwS", (128, 1))

    p_P = cps("pC_P", (128, 128))
    p_s = cps("pC_s", (128, 128))
    p_o = cps("pC_o", (128, 128))
    p_t16 = cps("pC_t16", (128, 256), BF16)
    p_to = cps("pC_to", (128, 1024), BF16)
    p_c1 = cps("pC_c1", (128, 512))
    p_c2 = cps("pC_c2", (128, 512))
    p_gb = cps("pC_gb", (128, 512))

    em.q("gpsimd", lambda g: g.dma_start(
        out=WoS[:].rearrange("p (a b c) -> p a b c", a=KT, b=KT),
        in_=io["Wo"]), inc=gw, k=16)
    em.q("sync", lambda s: s.dma_start(out=nwS[:], in_=io["nw"]), inc=ld, k=16)
    em.wait("vector", gw)
    em.wait("vector", ld)
    em.q("vector", lambda v: v.tensor_scalar(
        WoS[:], WoS[:], nwS[:, 0:1], None, op0=ALU.mult), inc=ev)

    for j in range(NCH):
        em.wait("sync", mm)
        em.wait("sync", ev)
        em.wait("sync", sc)
        em.q("sync", lambda s, j=j: s.dma_start(out=B1[:], in_=io["st_q"][j]),
             inc=ld, k=16)
        em.q("sync", lambda s, j=j: s.dma_start(out=B2[:], in_=io["st_k"][j]),
             inc=ld, k=16)
        em.q("sync", lambda s, j=j: s.dma_start(out=B3[:], in_=io["st_Kt"][j]),
             inc=ld, k=16)
        em.q("sync", lambda s, j=j: s.dma_start(out=B4[:], in_=io["st_Vt"][j]),
             inc=ld, k=16)
        em.q("sync", lambda s, j=j: s.dma_start(out=B5[:], in_=io["st_U"][j]),
             inc=ld, k=16)
        em.q("sync", lambda s, j=j: s.dma_start(out=B6[:], in_=io["st_A"][j]),
             inc=ld, k=16)
        em.q("sync", lambda s, j=j: s.dma_start(out=rowB[0:1, :],
                                                in_=io["st_rows"][j][0:1]),
             inc=ld, k=16)
        em.q("sync", lambda s, j=j: s.dma_start(out=rowB[64:65, :],
                                                in_=io["st_rows"][j][1:2]),
             inc=ld, k=16)
        em.q("sync", lambda s, j=j: s.dma_start(out=B7[:], in_=io["st_SIG"][j]),
             inc=ld, k=16)
        em.wait("tensor", ld)
        em.wait("vector", ld)
        for q4 in range(4):
            em.q("tensor", lambda t, q4=q4: t.matmul(
                p_gb[:], G["onesR"][:], rowB[0:1, q4 * 512:(q4 + 1) * 512],
                start=True, stop=True), inc=mm)
            em.wait("vector", mm)
            em.q("vector", lambda v, q4=q4: v.tensor_copy(
                P3[:, q4 * 512:(q4 + 1) * 512], p_gb[:]), inc=ev)
            em.wait("tensor", ev)
            em.q("tensor", lambda t, q4=q4: t.matmul(
                p_gb[:], G["onesR65"][64:65, :], rowB[64:65, q4 * 512:(q4 + 1) * 512],
                start=True, stop=True), inc=mm)
            em.wait("vector", mm)
            em.q("vector", lambda v, q4=q4: v.tensor_copy(
                P4[:, q4 * 512:(q4 + 1) * 512], p_gb[:]), inc=ev)
            em.wait("tensor", ev)
        for h in range(H):
            em.q("tensor", lambda t, h=h: t.transpose(
                p_s[:, 2 * h:2 * h + 1], rowB[0:1, h * 128:(h + 1) * 128],
                G["idS32"][0:1, 0:1]))
            em.q("tensor", lambda t, h=h: t.transpose(
                p_s[:, 2 * h + 1:2 * h + 2], rowB[64:65, h * 128:(h + 1) * 128],
                G["idS32"][64:65, 64:65]), inc=(mm if h == H - 1 else None))
        em.wait("vector", mm)
        em.q("vector", lambda v: v.tensor_copy(gcols[:], p_s[:, 0:32]), inc=ev)
        em.wait("tensor", ev)
        em.q("vector", lambda v: v.tensor_mul(SkI[:], SkI[:], P4[:]), inc=ev)
        for h in range(H):
            em.q("vector", lambda v, h=h: v.tensor_scalar(
                SvI[:, h * 128:(h + 1) * 128], SvI[:, h * 128:(h + 1) * 128],
                gcols[:, 2 * h + 1:2 * h + 2], None, op0=ALU.mult),
                inc=(ev if h == H - 1 else None))
        em.wait("vector", ev)
        em.q("vector", lambda v: v.tensor_copy(SkP16[:], SkI[:]), inc=ev)
        em.q("vector", lambda v: v.tensor_copy(SvP16[:], SvI[:]), inc=ev)
        em.wait("tensor", ev)
        for h in range(H):
            hs = slice(h * 128, (h + 1) * 128)
            em.q("tensor", lambda t, hs=hs: t.matmul(
                p_P[:], B2[:, hs], B1[:, hs], start=True, stop=True), inc=mm)
            em.wait("vector", mm)
            em.q("vector", lambda v: v.tensor_mul(Pm16[:], p_P[:], G["triuS"][:]),
                 inc=ev)
            em.wait("tensor", ev)
            em.q("tensor", lambda t, hs=hs: t.matmul(
                p_s[:], Pm16[:], B5[:, hs], start=True, stop=False))
            em.q("tensor", lambda t, hs=hs: t.matmul(
                p_s[:], B1[:, hs], SkP16[:, hs], start=False, stop=True), inc=mm)
            em.wait("vector", mm)
            em.q("vector", lambda v, hs=hs: v.tensor_mul(esx[:], p_s[:], B6[:, hs]),
                 inc=ev)
            em.wait("scalar", ev)
            em.q("scalar", lambda s: s.activation(
                esx[:], esx[:], AF.Exp, scale=SCALE, accum_out=colB[:, 32:33]),
                inc=sc)
            em.wait("vector", sc)
            em.q("vector", lambda v: v.reciprocal(colB[:, 33:34], colB[:, 32:33]),
                 inc=ev)
            em.wait("vector", ev)
            em.q("vector", lambda v, hs=hs: v.scalar_tensor_tensor(
                aA16[:], esx[:], colB[:, 33:34], B6[:, hs],
                op0=ALU.mult, op1=ALU.mult), inc=ev)
            em.wait("tensor", ev)
            em.q("tensor", lambda t: t.transpose(
                p_t16[:, 0:128], aA16[:], G["idS"][:]))
            em.q("tensor", lambda t, hs=hs: t.transpose(
                p_t16[:, 128:256], B5[:, hs], G["idS"][:]), inc=mm)
            em.wait("vector", mm)
            em.q("vector", lambda v: v.tensor_copy(aAT16[:], p_t16[:, 0:128]))
            em.q("vector", lambda v: v.tensor_copy(UT16[:], p_t16[:, 128:256]),
                 inc=ev)
            em.wait("tensor", ev)
            em.q("tensor", lambda t: t.matmul(
                p_P[:], UT16[:], aAT16[:], start=True, stop=True), inc=mm)
            em.wait("vector", mm)
            em.q("vector", lambda v: v.tensor_mul(W2m16[:], p_P[:], G["triuS"][:]),
                 inc=ev)
            em.wait("tensor", ev)
            em.q("tensor", lambda t, hs=hs: t.matmul(
                p_o[:], W2m16[:], B4[:, hs], start=True, stop=False))
            em.q("tensor", lambda t, hs=hs: t.matmul(
                p_o[:], aAT16[:], SvP16[:, hs], start=False, stop=True), inc=mm)
            em.wait("vector", mm)
            em.q("vector", lambda v, hs=hs: v.tensor_copy(P1[:, hs], p_o[:]),
                 inc=ev)
            em.wait("tensor", ev)
            hh = h % 4
            em.q("tensor", lambda t, hs=hs, hh=hh: t.matmul(
                p_c1[:, hh * 128:(hh + 1) * 128], B3[:, hs], B5[:, hs],
                start=True, stop=True))
            em.q("tensor", lambda t, hs=hs, hh=hh: t.matmul(
                p_c2[:, hh * 128:(hh + 1) * 128], B5[:, hs], B4[:, hs],
                start=True, stop=True), inc=mm)
            if hh == 3:
                hg = h // 4
                sl = slice(hg * 512, (hg + 1) * 512)
                em.wait("vector", mm)
                em.q("vector", lambda v, sl=sl: v.tensor_add(
                    SkI[:, sl], SkI[:, sl], p_c1[:]))
                em.q("vector", lambda v, sl=sl: v.tensor_mul(
                    SkI[:, sl], SkI[:, sl], P3[:, sl]))
                em.q("vector", lambda v, sl=sl: v.tensor_add(
                    SvI[:, sl], SvI[:, sl], p_c2[:]), inc=ev)
                for h2 in range(hg * 4, hg * 4 + 4):
                    em.q("vector", lambda v, h2=h2: v.tensor_scalar(
                        SvI[:, h2 * 128:(h2 + 1) * 128],
                        SvI[:, h2 * 128:(h2 + 1) * 128],
                        gcols[:, 2 * h2:2 * h2 + 1], None, op0=ALU.mult),
                        inc=(ev if h2 % 4 == 3 else None))
                em.wait("tensor", ev)
        em.wait("scalar", ev)
        em.q("scalar", lambda s: s.activation(P2[:], P1[:], AF.Square), inc=sc)
        em.wait("vector", sc)
        em.q("vector", lambda v: v.reduce_sum(
            colA[:, 0:16].rearrange("p (h o) -> p h o", o=1),
            P2[:].rearrange("p (h m) -> p h m", m=128), axis=AX.X), inc=ev)
        em.wait("scalar", ev)
        em.q("scalar", lambda s: s.activation(colA[:, 0:16], colA[:, 0:16],
                                              AF.Sqrt, scale=1.0 / 128,
                                              bias=1e-5), inc=sc)
        em.wait("vector", sc)
        em.q("vector", lambda v: v.reciprocal(colA[:, 0:16], colA[:, 0:16]),
             inc=ev)
        em.wait("vector", ev)
        for h in range(H):
            em.q("vector", lambda v, h=h: v.tensor_scalar(
                P1[:, h * 128:(h + 1) * 128], P1[:, h * 128:(h + 1) * 128],
                colA[:, h:h + 1], None, op0=ALU.mult),
                inc=(ev if h == H - 1 else None))
        em.wait("vector", ev)
        em.q("vector", lambda v: v.tensor_mul(B1[:], P1[:], B7[:]), inc=ev)
        em.wait("tensor", ev)
        for b in range(2):
            for k in range(8):
                kt = b * 8 + k
                em.q("tensor", lambda t, kt=kt, k=k: t.transpose(
                    p_to[:, k * 128:(k + 1) * 128],
                    B1[:, kt * 128:(kt + 1) * 128], G["idS"][:]),
                    inc=(mm if k == 7 else None))
            em.wait("vector", mm)
            em.q("vector", lambda v, b=b: v.tensor_copy(
                B2[:].rearrange("p (a w) -> p a w", a=KT)[:, b * 8:(b + 1) * 8, :],
                p_to[:].rearrange("p (a w) -> p a w", a=8)), inc=ev)
            em.wait("tensor", ev)
        Wo3 = WoS[:].rearrange("p (a b c) -> p a b c", a=KT, b=KT)
        for dt in range(KT):
            for kt in range(KT):
                em.q("tensor", lambda t, dt=dt, kt=kt: t.matmul(
                    p_o[:], Wo3[:, kt, dt, :], B2[:, kt * 128:(kt + 1) * 128],
                    start=(kt == 0), stop=(kt == KT - 1)),
                    inc=(mm if kt == KT - 1 else None))
            em.wait("vector", mm)
            em.q("vector", lambda v, dt=dt: v.tensor_copy(
                attnC[:, dt * 128:(dt + 1) * 128], p_o[:]), inc=ev)
            em.wait("tensor", ev)
        em.wait("sync", ev)
        em.wait("sync", sc)
        em.q("sync", lambda s, j=j: s.dma_start(
            out=P1[:, 0:2048], in_=io["x_strip"][j * 128:(j + 1) * 128, :]),
            inc=ld, k=16)
        em.wait("vector", ld)
        em.q("vector", lambda v: v.tensor_copy(B3[:], P1[:]), inc=ev)
        em.wait("tensor", ev)
        for b in range(2):
            for k in range(8):
                kt = b * 8 + k
                em.q("tensor", lambda t, kt=kt, k=k: t.transpose(
                    p_to[:, k * 128:(k + 1) * 128],
                    B3[:, kt * 128:(kt + 1) * 128], G["idS"][:]),
                    inc=(mm if k == 7 else None))
            em.wait("vector", mm)
            em.q("vector", lambda v, b=b: v.tensor_copy(
                B4[:].rearrange("p (a w) -> p a w", a=KT)[:, b * 8:(b + 1) * 8, :],
                p_to[:].rearrange("p (a w) -> p a w", a=8)), inc=ev)
            em.wait("tensor", ev)
        em.q("vector", lambda v: v.tensor_add(P2[:], B4[:], attnC[:]), inc=ev)
        em.wait("sync", ev)
        em.q("sync", lambda s, j=j: s.dma_start(
            out=io["sp_x1"][j].rearrange("a p w -> p a w"),
            in_=P2[:].rearrange("p (a w) -> p a w", a=KT)), inc=sg, k=16)
        em.wait("vector", sg)
    em.barrier("endC")
    cctx.close()
    G["xctx"].close()



def phaseF0(G):
    nc, em, io = G["nc"], G["em"], G["io"]
    ld, gw, mm, ev, sc, sg, cc = (G[k] for k in
                                  ("ld", "gw", "mm", "ev", "sc", "sg", "cc"))
    P1, P2, P3, P4 = G["P1"], G["P2"], G["P3"], G["P4"]
    B1, B2 = G["B1"], G["B2"]
    rowA, colA, colB = G["rowA"], G["colA"], G["colB"]
    rowC, rowD = G["rowC"], G["rowD"]

    fctx = ExitStack()
    fsb = lambda n, s, d=F32: fctx.enter_context(nc.sbuf_tensor(n, list(s), d))
    fps = lambda n, s, d=F32: fctx.enter_context(nc.psum_tensor(n, list(s), d))
    rtrS = fsb("rtrS", (128, KT * 3))
    wdnS = fsb("wdnS", (128, KT * 32))
    wupS = fsb("wupS", (32, KT * 128))
    l1sS = fsb("l1sS", (128, KT)); l1bS = fsb("l1bS", (128, KT))
    tdn = fsb("tdn", (32, 1024))
    p_stA = fps("pF_stA", (128, 512))
    p_stB = fps("pF_stB", (128, 512))
    p_sa = p_stA[0:1, :]; p_sb = p_stA[64:65, :]
    p_sc = p_stB[0:1, :]; p_sd = p_stB[64:65, :]
    p_r = fps("pF_r", (128, 24))
    p_dn = fps("pF_dn", (32, 1024))
    p_gb = fps("pF_gb", (128, 512))

    em.q("sync", lambda s: s.dma_start(
        out=rtrS[:].rearrange("p (a c) -> p a c", a=KT), in_=io["rtr"]),
        inc=ld, k=16)
    em.q("sync", lambda s: s.dma_start(
        out=wdnS[:].rearrange("p (a c) -> p a c", a=KT), in_=io["wdn"]),
        inc=ld, k=16)
    em.q("sync", lambda s: s.dma_start(
        out=wupS[:].rearrange("p (a c) -> p a c", a=KT), in_=io["wup"]),
        inc=ld, k=16)
    em.q("sync", lambda s: s.dma_start(out=l1sS[:], in_=io["l1s"]), inc=ld, k=16)
    em.q("sync", lambda s: s.dma_start(out=l1bS[:], in_=io["l1b"]), inc=ld, k=16)
    em.wait("tensor", ld)

    pr = G["pr"]
    B1F, B2F = G["B1"], G["B2"]
    for kt in range(KT):
        em.wait("sync", pr)
        em.wait("sync", sc)
        em.wait("sync", ev)
        em.q("sync", lambda s, kt=kt: s.dma_start(
            out=P1[:, 0:1024].rearrange("p (a w) -> p a w", a=NCH),
            in_=io["sp_x1"][:, kt].rearrange("a p w -> p a w")), inc=ld, k=16)
        em.wait("scalar", ld)
        em.wait("vector", ld)
        em.q("scalar", lambda s: s.activation(B2F[:, 0:1024], P1[:, 0:1024],
                                              AF.Square), inc=sc)
        em.q("vector", lambda v: v.tensor_copy(B1F[:, 0:1024], P1[:, 0:1024]),
             inc=ev)
        em.wait("tensor", sc)
        em.wait("tensor", ev)
        for q2, ps in ((0, p_sa), (1, p_sb)):
            em.q("tensor", lambda t, q2=q2, ps=ps, kt=kt: t.matmul(
                ps, G["onesC16"][:], B1F[:, q2 * 512:(q2 + 1) * 512],
                start=(kt == 0), stop=(kt == KT - 1)))
        for q2, ps in ((0, p_sc), (1, p_sd)):
            em.q("tensor", lambda t, q2=q2, ps=ps, kt=kt: t.matmul(
                ps, G["onesC16"][:], B2F[:, q2 * 512:(q2 + 1) * 512],
                start=(kt == 0), stop=(kt == KT - 1)),
                inc=((pr if kt < KT - 1 else mm) if q2 == 1 else None))
    em.wait("vector", mm)
    em.q("vector", lambda v: v.tensor_scalar(
        rowA[0:1, 0:512], p_sa, 1.0 / D, None, op0=ALU.mult))
    em.q("vector", lambda v: v.tensor_scalar(
        rowA[0:1, 512:1024], p_sb, 1.0 / D, None, op0=ALU.mult))
    em.q("vector", lambda v: v.tensor_scalar(
        rowC[0:1, 0:512], p_sc, 1.0 / D, None, op0=ALU.mult))
    em.q("vector", lambda v: v.tensor_scalar(
        rowC[0:1, 512:1024], p_sd, 1.0 / D, None, op0=ALU.mult), inc=ev)
    em.wait("vector", ev)
    em.q("vector", lambda v: v.tensor_mul(rowD[0:1, 0:1024],
                                          rowA[0:1, 0:1024], rowA[0:1, 0:1024]),
         inc=ev)
    em.wait("vector", ev)
    em.q("vector", lambda v: v.tensor_sub(rowC[0:1, 0:1024], rowC[0:1, 0:1024],
                                          rowD[0:1, 0:1024]), inc=ev)
    em.wait("scalar", ev)
    em.q("scalar", lambda s: s.activation(rowC[0:1, 0:1024], rowC[0:1, 0:1024],
                                          AF.Sqrt, bias=1e-5), inc=sc)
    em.wait("vector", sc)
    em.q("vector", lambda v: v.reciprocal(rowC[0:1, 0:1024], rowC[0:1, 0:1024]),
         inc=ev)
    em.wait("tensor", ev)
    for q2 in range(2):
        em.q("tensor", lambda t, q2=q2: t.matmul(
            p_gb[:], G["onesR"][:], rowA[0:1, q2 * 512:(q2 + 1) * 512],
            start=True, stop=True), inc=mm)
        em.wait("vector", mm)
        em.q("vector", lambda v, q2=q2: v.tensor_copy(
            P3[:, q2 * 512:(q2 + 1) * 512], p_gb[:]), inc=ev)
        em.wait("tensor", ev)
        em.q("tensor", lambda t, q2=q2: t.matmul(
            p_gb[:], G["onesR"][:], rowC[0:1, q2 * 512:(q2 + 1) * 512],
            start=True, stop=True), inc=mm)
        em.wait("vector", mm)
        em.q("vector", lambda v, q2=q2: v.tensor_copy(
            P4[:, q2 * 512:(q2 + 1) * 512], p_gb[:]), inc=ev)
        em.wait("tensor", ev)
    for kt in range(KT):
        em.wait("sync", ev)
        em.q("sync", lambda s, kt=kt: s.dma_start(
            out=P1[:, 0:1024].rearrange("p (a w) -> p a w", a=NCH),
            in_=io["sp_x1"][:, kt].rearrange("a p w -> p a w")), inc=ld, k=16)
        em.wait("vector", ld)
        em.wait("vector", pr)
        em.q("vector", lambda v: v.tensor_sub(P2[:, 0:1024], P1[:, 0:1024],
                                              P3[:, 0:1024]), inc=ev)
        em.wait("vector", ev)
        em.q("vector", lambda v: v.tensor_mul(P2[:, 0:1024], P2[:, 0:1024],
                                              P4[:, 0:1024]), inc=ev)
        em.wait("vector", ev)
        em.q("vector", lambda v: v.tensor_copy(B1[:, 0:1024], P2[:, 0:1024]),
             inc=ev)
        em.wait("vector", ev)
        em.q("vector", lambda v, kt=kt: v.tensor_scalar(
            B2[:, 0:1024], P2[:, 0:1024], l1sS[:, kt:kt + 1],
            l1bS[:, kt:kt + 1], op0=ALU.mult, op1=ALU.add), inc=ev)
        em.wait("tensor", ev)
        for tt in range(8):
            em.q("tensor", lambda t, tt=tt, kt=kt: t.matmul(
                p_r[:, tt * 3:(tt + 1) * 3], P2[:, tt * 128:(tt + 1) * 128],
                rtrS[:, kt * 3:(kt + 1) * 3],
                start=(kt == 0 and tt == 0),
                stop=(kt == KT - 1 and tt == 7), skip_group_check=True))
        for q2 in range(2):
            em.q("tensor", lambda t, q2=q2, kt=kt: t.matmul(
                p_dn[:, q2 * 512:(q2 + 1) * 512],
                wdnS[:, kt * 32:(kt + 1) * 32], P2[:, q2 * 512:(q2 + 1) * 512],
                start=(kt == 0), stop=(kt == KT - 1)),
                inc=((pr if kt < KT - 1 else mm) if q2 == 1 else None))
        em.wait("sync", ev)
        em.q("sync", lambda s, kt=kt: s.dma_start(
            out=io["sp_hs"][kt], in_=B1[:, 0:1024]), inc=sg, k=16)
        em.q("sync", lambda s, kt=kt: s.dma_start(
            out=io["cc_xn_in"][kt * 128:(kt + 1) * 128, :], in_=B2[:, 0:1024]),
            inc=sg, k=16)
        em.wait("vector", sg)
    em.wait("vector", mm)
    em.q("vector", lambda v: v.tensor_copy(colA[:, 0:24], p_r[:]), inc=ev)
    em.wait("vector", ev)
    lA3 = colA[:, 0:24].rearrange("p (a c) -> p a c", a=8)
    l0s, l1s, l2s = lA3[:, :, 0], lA3[:, :, 1], lA3[:, :, 2]
    ge01, ge02, ge12 = colA[:, 32:40], colA[:, 40:48], colA[:, 48:56]
    nd0 = colA[:, 56:64]
    cB3 = colB[:, 0:16].rearrange("p (a c) -> p a c", a=8)
    d0s, d1s = cB3[:, :, 0], cB3[:, :, 1]
    em.q("vector", lambda v: v.tensor_tensor(ge01, l0s, l1s, op=ALU.is_ge))
    em.q("vector", lambda v: v.tensor_tensor(ge02, l0s, l2s, op=ALU.is_ge))
    em.q("vector", lambda v: v.tensor_tensor(ge12, l1s, l2s, op=ALU.is_ge),
         inc=ev)
    em.wait("vector", ev)
    em.q("vector", lambda v: v.tensor_mul(d0s, ge01, ge02), inc=ev)
    em.wait("vector", ev)
    em.q("vector", lambda v: v.tensor_scalar(
        nd0, d0s, -1.0, 1.0, op0=ALU.mult, op1=ALU.add), inc=ev)
    em.wait("vector", ev)
    em.q("vector", lambda v: v.tensor_mul(d1s, ge12, nd0), inc=ev)
    em.wait("vector", ev)
    em.wait("vector", mm)
    em.q("vector", lambda v: v.tensor_copy(tdn[:], p_dn[:]), inc=ev)
    em.wait("tensor", ev)
    for dt in range(KT):
        for q2 in range(2):
            em.q("tensor", lambda t, dt=dt, q2=q2: t.matmul(
                p_gb[:], wupS[:, dt * 128:(dt + 1) * 128],
                tdn[:, q2 * 512:(q2 + 1) * 512],
                start=True, stop=True), inc=mm)
            em.wait("vector", mm)
            em.q("vector", lambda v, q2=q2: v.tensor_copy(
                B1[:, q2 * 512:(q2 + 1) * 512], p_gb[:]), inc=ev)
            em.wait("tensor", ev)
        em.wait("sync", ev)
        em.q("sync", lambda s, dt=dt: s.dma_start(
            out=io["sp_lr"][dt], in_=B1[:, 0:1024]), inc=sg, k=16)
        em.wait("vector", sg)
    em.wait("gpsimd", sg)
    em.q("gpsimd", lambda g: g.collective_compute(
        "AllGather", ALU.bypass, replica_groups=[list(range(NCORE))],
        ins=[io["cc_xn_in"].opt()], outs=[io["cc_xn_out"].opt()]), inc=cc, k=1)
    em.wait("gpsimd", cc)
    em.barrier("endF0")
    fctx.close()



def phaseF1(G):
    nc, em, io = G["nc"], G["em"], G["io"]
    ld, gw, mm, ev, sc, sg, cc = (G[k] for k in
                                  ("ld", "gw", "mm", "ev", "sc", "sg", "cc"))
    P1 = G["P1"]
    rowD = G["rowD"]

    fctx = ExitStack()
    fsb = lambda n, s, d=F32: fctx.enter_context(nc.sbuf_tensor(n, list(s), d))
    fps = lambda n, s, d=F32: fctx.enter_context(nc.psum_tensor(n, list(s), d))
    rz16 = fsb("rz16", (1, 512), BF16)
    dmS = fsb("dmS", (128, 2048), BF16)
    f1a = ExitStack()
    fsa = lambda n, s, d=F32: f1a.enter_context(nc.sbuf_tensor(n, list(s), d))
    xnp = fsa("xnp", (128, KT * 1024), BF16)
    wqS = fsa("wqS", (128, KT * 256), BF16)
    wkS = fsa("wkS", (128, KT * 256), BF16)
    wvS = fsa("wvS", (128, KT * 256), BF16)
    evq = fsa("evq", (128, 1024), BF16)

    p_q0 = fps("pG_q0", (128, 512)); p_q1 = fps("pG_q1", (128, 512))
    p_S0 = fps("pG_S0", (128, 512)); p_S1 = fps("pG_S1", (128, 512))
    p_O = fps("pG_O", (128, 512))
    p_Z = fps("pG_Z", (1, 512))
    p_bc = fps("pG_bc", (128, 512))
    p_tv = fps("pG_tv", (128, 1024), BF16)

    em.q("sync", lambda s: s.dma_start(out=dmS[:], in_=G["cst"]["dmask"]),
         inc=ld, k=16)
    for wS, wn in ((wqS, "fwq"), (wkS, "fwk"), (wvS, "fwv")):
        em.q("gpsimd", lambda g, wS=wS, wn=wn: g.dma_start(
            out=wS[:].rearrange("p (a b c) -> p a b c", a=KT, b=2),
            in_=io[wn]), inc=gw, k=16)
    em.wait("tensor", gw)
    em.wait("tensor", ld)

    wq3 = wqS[:].rearrange("p (a b c) -> p a b c", a=KT, b=2)
    wk3 = wkS[:].rearrange("p (a b c) -> p a b c", a=KT, b=2)
    wv3 = wvS[:].rearrange("p (a b c) -> p a b c", a=KT, b=2)
    for piece in range(NCORE):
        em.wait("sync", mm)
        for kt in range(KT):
            em.q("sync", lambda s, piece=piece, kt=kt: s.dma_start(
                out=xnp[:, kt * 1024:(kt + 1) * 1024],
                in_=io["cc_xn_out"][piece * D + kt * 128:
                                    piece * D + (kt + 1) * 128, :]),
                inc=ld, k=16)
        em.wait("tensor", ld)
        for w3, stn in ((wq3, "st_ffq"), (wk3, "st_ffk"), (wv3, "st_ffv")):
            for cb in range(2):
                for q2, ps in ((0, p_q0), (1, p_q1)):
                    for kt in range(KT):
                        em.q("tensor", lambda t, kt=kt, cb=cb, q2=q2, ps=ps,
                             w3=w3: t.matmul(
                                 ps[:], w3[:, kt, cb, :],
                                 xnp[:, kt * 1024 + q2 * 512:
                                     kt * 1024 + (q2 + 1) * 512],
                                 start=(kt == 0), stop=(kt == KT - 1)),
                             inc=(mm if kt == KT - 1 else None))
                em.wait("vector", mm)
                em.q("vector", lambda v: v.tensor_copy(evq[:, 0:512], p_q0[:]))
                em.q("vector", lambda v: v.tensor_copy(evq[:, 512:1024], p_q1[:]),
                     inc=ev)
                em.wait("sync", ev)
                em.q("sync", lambda s, stn=stn, cb=cb, piece=piece: s.dma_start(
                    out=io[stn][cb][:, piece * 1024:(piece + 1) * 1024],
                    in_=evq[:]), inc=sg, k=16)
                em.wait("vector", sg)
                em.wait("tensor", sg)
    em.barrier("endF1proj")
    f1a.close()
    f1b = ExitStack()
    fsbb = lambda n, s, d=F32: f1b.enter_context(nc.sbuf_tensor(n, list(s), d))
    QS = fsbb("QS", (128, T), BF16)
    KS = fsbb("KS", (128, T), BF16)
    VS = fsbb("VS", (128, T), BF16)
    VtS = fsbb("VtS", (128, T), BF16)
    Pt16 = fsbb("Pt16", (128, 2 * 512), BF16)
    O16 = fsbb("O16", (128, 512), BF16)
    for hh in range(2):
        em.wait("sync", sg)
        em.wait("sync", mm)
        em.wait("sync", sc)
        em.wait("sync", ev)
        em.q("sync", lambda s, hh=hh: s.dma_start(out=QS[:], in_=io["st_ffq"][hh]),
             inc=ld, k=16)
        em.q("sync", lambda s, hh=hh: s.dma_start(out=KS[:], in_=io["st_ffk"][hh]),
             inc=ld, k=16)
        em.q("sync", lambda s, hh=hh: s.dma_start(out=VS[:], in_=io["st_ffv"][hh]),
             inc=ld, k=16)
        em.wait("tensor", ld)
        for b8 in range(8):
            for k in range(8):
                kv = b8 * 8 + k
                em.q("tensor", lambda t, kv=kv, k=k: t.transpose(
                    p_tv[:, k * 128:(k + 1) * 128],
                    VS[:, kv * 128:(kv + 1) * 128], G["idS"][:]),
                    inc=(mm if k == 7 else None))
            em.wait("vector", mm)
            em.q("vector", lambda v, b8=b8: v.tensor_copy(
                VtS[:, b8 * 1024:(b8 + 1) * 1024],
                p_tv[:].rearrange("p (a w) -> p a w", a=8).rearrange(
                    "p a w -> p (a w)")), inc=ev)
            em.wait("tensor", ev)
        for qb in range(16):
            nkv = 4 * qb + 4
            pr = G["pr"]
            for kv in range(nkv):
                ps = p_S0 if kv % 2 == 0 else p_S1
                pb = 0 if kv % 2 == 0 else 1
                em.q("tensor", lambda t, kv=kv, qb=qb, ps=ps: t.matmul(
                    ps[:], KS[:, kv * 128:(kv + 1) * 128],
                    QS[:, qb * 512:(qb + 1) * 512], start=True, stop=True),
                    inc=mm)
                em.wait("scalar", mm)
                if kv >= 2:
                    em.wait("scalar", pr, pr.n - 1)
                em.q("scalar", lambda s, ps=ps, pb=pb: s.activation(
                    Pt16[:, pb * 512:(pb + 1) * 512], ps[:], AF.Exp,
                    scale=SCALE, bias=-C0F), inc=sc)
                em.wait("tensor", sc)
                if kv >= 4 * qb:
                    dj = kv - 4 * qb
                    em.wait("vector", sc)
                    em.q("vector", lambda v, pb=pb, dj=dj: v.tensor_mul(
                        Pt16[:, pb * 512:(pb + 1) * 512],
                        Pt16[:, pb * 512:(pb + 1) * 512],
                        dmS[:, dj * 512:(dj + 1) * 512]), inc=ev)
                    em.wait("tensor", ev)
                em.q("tensor", lambda t, kv=kv, pb=pb, nkv=nkv: t.matmul(
                    p_O[:], VtS[:, kv * 128:(kv + 1) * 128],
                    Pt16[:, pb * 512:(pb + 1) * 512],
                    start=(kv == 0), stop=(kv == nkv - 1)))
                em.q("tensor", lambda t, kv=kv, pb=pb, nkv=nkv: t.matmul(
                    p_Z[:], G["onesC16"][:],
                    Pt16[:, pb * 512:(pb + 1) * 512],
                    start=(kv == 0), stop=(kv == nkv - 1)),
                    inc=[(mm, 1)] if kv == nkv - 1 else [(pr, 1)])
            em.wait("vector", mm)
            em.q("vector", lambda v: v.reciprocal(rowD[0:1, 0:512], p_Z[:]),
                 inc=ev)
            em.wait("vector", ev)
            em.q("vector", lambda v: v.tensor_copy(rz16[:], rowD[0:1, 0:512]),
                 inc=ev)
            em.wait("tensor", ev)
            em.q("tensor", lambda t: t.matmul(
                p_bc[:], G["onesR16"][:], rz16[:], start=True, stop=True),
                inc=mm)
            em.q("vector", lambda v: v.tensor_copy(P1[:, 0:512], p_O[:]), inc=ev)
            em.wait("vector", mm)
            em.wait("vector", ev)
            em.q("vector", lambda v: v.tensor_mul(O16[:], P1[:, 0:512], p_bc[:]),
                 inc=ev)
            em.wait("sync", ev)
            qb2 = qb // 2
            em.q("sync", lambda s, hh=hh, qb=qb, qb2=qb2: s.dma_start(
                out=io["cc_ctx_in"][qb2 * 256 + hh * 128:
                                    qb2 * 256 + (hh + 1) * 128,
                                    (qb % 2) * 512:(qb % 2) * 512 + 512],
                in_=O16[:]), inc=sg, k=16)
            em.wait("vector", sg)
            em.wait("tensor", sg)
    em.wait("gpsimd", sg)
    em.q("gpsimd", lambda g: g.collective_compute(
        "AllGather", ALU.bypass, replica_groups=[list(range(NCORE))],
        ins=[io["cc_ctx_in"].opt()], outs=[io["cc_ctx_out"].opt()]), inc=cc, k=1)
    em.wait("gpsimd", cc)
    em.barrier("endF1")
    f1b.close()
    fctx.close()



def phaseF2(G):
    nc, em, io = G["nc"], G["em"], G["io"]
    ld, gw, mm, ev, sc, sg = (G[k] for k in ("ld", "gw", "mm", "ev", "sc", "sg"))
    P1, P2, P3, P4 = G["P1"], G["P2"], G["P3"], G["P4"]
    B1, B2, B3 = G["B1"], G["B2"], G["B3"]
    rowA, colB = G["rowA"], G["colB"]
    colA = G["colA"]
    rowC, rowD = G["rowC"], G["rowD"]

    fctx = ExitStack()
    fsb = lambda n, s, d=F32: fctx.enter_context(nc.sbuf_tensor(n, list(s), d))
    fps = lambda n, s, d=F32: fctx.enter_context(nc.psum_tensor(n, list(s), d))
    l2sS = fsb("l2sS", (128, KT)); l2bS = fsb("l2bS", (128, KT))
    b0S = fsb("b0S", (128, KT))
    D0bc = fsb("D0bc", (128, 1024)); D1bc = fsb("D1bc", (128, 1024))
    cidxS = fsb("cidxS", (128, KT), U32)
    f2a = ExitStack()
    ctxS = f2a.enter_context(nc.sbuf_tensor("ctxS", [128, KT * 1024], BF16))
    woSl = f2a.enter_context(nc.sbuf_tensor("woSl", [128, 2 * 2048], BF16))

    p_w0 = fps("pH_w0", (128, 512)); p_w1 = fps("pH_w1", (128, 512))
    p_stA = fps("pH_stA", (128, 512))
    p_stB = fps("pH_stB", (128, 512))
    p_sa = p_stA[0:1, :]; p_sb = p_stA[64:65, :]
    p_sc2 = p_stB[0:1, :]; p_sd = p_stB[64:65, :]
    p_gb = fps("pH_gb", (128, 512))
    p_tr = fps("pH_tr", (128, 128))
    p_trb = fps("pH_trb", (128, 1024), BF16)

    em.q("sync", lambda s: s.dma_start(out=l2sS[:], in_=io["l2s"]), inc=ld, k=16)
    em.q("sync", lambda s: s.dma_start(out=l2bS[:], in_=io["l2b"]), inc=ld, k=16)
    em.q("sync", lambda s: s.dma_start(out=b0S[:], in_=io["b0"]), inc=ld, k=16)
    em.q("sync", lambda s: s.dma_start(out=cidxS[:], in_=io["ctxidx"]),
         inc=ld, k=16)
    em.wait("gpsimd", ld)
    for kt in range(KT):
        em.q("gpsimd", lambda g, kt=kt: g.indirect_dma_start(
            out=ctxS[:, kt * 1024:(kt + 1) * 1024], out_offset=None,
            in_=io["cc_ctx_out"],
            in_offset=IndirectOffsetOnAxis(cidxS[:, kt:kt + 1], 0)),
            inc=gw, k=16)
    em.wait("tensor", gw)
    em.wait("tensor", ld)
    wo_marks = []
    for dt in range(KT):
        if dt >= 2:
            em.wait("gpsimd", mm, wo_marks[dt - 2])
        em.q("gpsimd", lambda g, dt=dt: g.dma_start(
            out=woSl[:, (dt % 2) * 2048:(dt % 2) * 2048 + 2048].rearrange(
                "p (a c) -> p a c", a=KT),
            in_=io["fwo"][dt]), inc=gw, k=16)
        em.wait("sync", ev)
        em.wait("sync", sc)
        em.q("sync", lambda s, dt=dt: s.dma_start(
            out=B1[:, 0:1024], in_=io["sp_hs"][dt]), inc=ld, k=16)
        em.wait("tensor", gw)
        em.wait("tensor", ld)
        wsl = woSl[:, (dt % 2) * 2048:(dt % 2) * 2048 + 2048]
        for q2, ps in ((0, p_w0), (1, p_w1)):
            for kt in range(KT):
                em.q("tensor", lambda t, kt=kt, q2=q2, ps=ps, wsl=wsl: t.matmul(
                    ps[:], wsl[:, kt * 128:(kt + 1) * 128],
                    ctxS[:, kt * 1024 + q2 * 512:kt * 1024 + (q2 + 1) * 512],
                    start=(kt == 0), stop=(kt == KT - 1)),
                    inc=(mm if kt == KT - 1 else None))
        em.wait("vector", mm)
        em.wait("vector", G["pr"])
        em.q("vector", lambda v: v.tensor_add(P1[:, 0:512], B1[:, 0:512],
                                              p_w0[:]))
        em.q("vector", lambda v: v.tensor_add(P1[:, 512:1024], B1[:, 512:1024],
                                              p_w1[:]), inc=ev)
        em.wait("tensor", ev)
        em.wait("scalar", ev)
        em.q("scalar", lambda s: s.activation(B3[:, 0:1024], P1[:, 0:1024],
                                              AF.Square), inc=sc)
        em.q("vector", lambda v: v.tensor_copy(B2[:, 0:1024], P1[:, 0:1024]),
             inc=ev)
        em.wait("sync", ev)
        em.q("sync", lambda s, dt=dt: s.dma_start(
            out=io["sp_xt"][dt], in_=B2[:, 0:1024]), inc=sg, k=16)
        em.wait("tensor", sc)
        em.wait("tensor", ev)
        for q2, ps in ((0, p_sa), (1, p_sb)):
            em.q("tensor", lambda t, q2=q2, ps=ps, dt=dt: t.matmul(
                ps[:], G["onesC16"][:], B2[:, q2 * 512:(q2 + 1) * 512],
                start=(dt == 0), stop=(dt == KT - 1)))
        for q2, ps in ((0, p_sc2), (1, p_sd)):
            em.q("tensor", lambda t, q2=q2, ps=ps, dt=dt: t.matmul(
                ps[:], G["onesC16"][:], B3[:, q2 * 512:(q2 + 1) * 512],
                start=(dt == 0), stop=(dt == KT - 1)),
                inc=(mm if (dt == KT - 1 and q2 == 1) else None))
        wo_marks.append(mm.n)
        em.wait("vector", sg)
    em.wait("vector", mm)
    em.q("vector", lambda v: v.tensor_scalar(
        rowA[0:1, 0:512], p_sa, 1.0 / D, None, op0=ALU.mult))
    em.q("vector", lambda v: v.tensor_scalar(
        rowA[0:1, 512:1024], p_sb, 1.0 / D, None, op0=ALU.mult))
    em.q("vector", lambda v: v.tensor_scalar(
        rowC[0:1, 0:512], p_sc2, 1.0 / D, None, op0=ALU.mult))
    em.q("vector", lambda v: v.tensor_scalar(
        rowC[0:1, 512:1024], p_sd, 1.0 / D, None, op0=ALU.mult), inc=ev)
    em.wait("vector", ev)
    em.q("vector", lambda v: v.tensor_mul(rowD[0:1, 0:1024],
                                          rowA[0:1, 0:1024], rowA[0:1, 0:1024]),
         inc=ev)
    em.wait("vector", ev)
    em.q("vector", lambda v: v.tensor_sub(rowC[0:1, 0:1024], rowC[0:1, 0:1024],
                                          rowD[0:1, 0:1024]), inc=ev)
    em.wait("scalar", ev)
    em.q("scalar", lambda s: s.activation(rowC[0:1, 0:1024], rowC[0:1, 0:1024],
                                          AF.Sqrt, bias=1e-5), inc=sc)
    em.wait("vector", sc)
    em.q("vector", lambda v: v.reciprocal(rowC[0:1, 0:1024], rowC[0:1, 0:1024]),
         inc=ev)
    em.wait("tensor", ev)
    for q2 in range(2):
        em.q("tensor", lambda t, q2=q2: t.matmul(
            p_gb[:], G["onesR"][:], rowA[0:1, q2 * 512:(q2 + 1) * 512],
            start=True, stop=True), inc=mm)
        em.wait("vector", mm)
        em.q("vector", lambda v, q2=q2: v.tensor_copy(
            P3[:, q2 * 512:(q2 + 1) * 512], p_gb[:]), inc=ev)
        em.wait("tensor", ev)
        em.q("tensor", lambda t, q2=q2: t.matmul(
            p_gb[:], G["onesR"][:], rowC[0:1, q2 * 512:(q2 + 1) * 512],
            start=True, stop=True), inc=mm)
        em.wait("vector", mm)
        em.q("vector", lambda v, q2=q2: v.tensor_copy(
            P4[:, q2 * 512:(q2 + 1) * 512], p_gb[:]), inc=ev)
        em.wait("tensor", ev)
    em.barrier("endF2wo")
    f2a.close()
    f2b = ExitStack()
    xn2S = f2b.enter_context(nc.sbuf_tensor("xn2S", [128, KT * 1024], BF16))
    aS = f2b.enter_context(nc.sbuf_tensor("aS", [128, KT * 1024], BF16))
    ACC = f2b.enter_context(nc.sbuf_tensor("ACC", [128, KT * 1024], BF16))
    f1Sl = f2b.enter_context(nc.sbuf_tensor("f1Sl", [128, 2 * 2048], BF16))
    f2Sl = f2b.enter_context(nc.sbuf_tensor("f2Sl", [128, 2 * 2048], BF16))
    for kt in range(KT):
        em.wait("sync", ev)
        em.q("sync", lambda s, kt=kt: s.dma_start(
            out=B1[:, 0:1024], in_=io["sp_xt"][kt]), inc=ld, k=16)
        em.wait("vector", ld)
        em.q("vector", lambda v: v.tensor_sub(P1[:, 0:1024], B1[:, 0:1024],
                                              P3[:, 0:1024]), inc=ev)
        em.wait("vector", ev)
        em.q("vector", lambda v: v.tensor_mul(P1[:, 0:1024], P1[:, 0:1024],
                                              P4[:, 0:1024]), inc=ev)
        em.wait("vector", ev)
        em.q("vector", lambda v, kt=kt: v.tensor_scalar(
            xn2S[:, kt * 1024:(kt + 1) * 1024], P1[:, 0:1024],
            l2sS[:, kt:kt + 1], l2bS[:, kt:kt + 1],
            op0=ALU.mult, op1=ALU.add), inc=ev)
    em.wait("tensor", ev)
    for tt in range(8):
        em.q("tensor", lambda t, tt=tt: t.transpose(
            p_tr[0:1, :], colB[:, 2 * tt:2 * tt + 1], G["idS32"][:]), inc=mm)
        em.wait("vector", mm)
        em.q("vector", lambda v: v.tensor_copy(rowA[0:1, 1024:1152],
                                               p_tr[0:1, 0:128]), inc=ev)
        em.wait("tensor", ev)
        em.q("tensor", lambda t, tt=tt: t.transpose(
            p_tr[0:1, :], colB[:, 2 * tt + 1:2 * tt + 2], G["idS32"][:]), inc=mm)
        em.wait("vector", mm)
        em.q("vector", lambda v: v.tensor_copy(rowC[0:1, 1024:1152],
                                               p_tr[0:1, 0:128]), inc=ev)
        em.wait("tensor", ev)
        em.q("tensor", lambda t, tt=tt: t.matmul(
            p_gb[:, 0:128], G["onesR"][:], rowA[0:1, 1024:1152],
            start=True, stop=True), inc=mm)
        em.wait("vector", mm)
        em.q("vector", lambda v, tt=tt: v.tensor_copy(
            D0bc[:, tt * 128:(tt + 1) * 128], p_gb[:, 0:128]), inc=ev)
        em.wait("tensor", ev)
        em.q("tensor", lambda t, tt=tt: t.matmul(
            p_gb[:, 0:128], G["onesR"][:], rowC[0:1, 1024:1152],
            start=True, stop=True), inc=mm)
        em.wait("vector", mm)
        em.q("vector", lambda v, tt=tt: v.tensor_copy(
            D1bc[:, tt * 128:(tt + 1) * 128], p_gb[:, 0:128]), inc=ev)
        em.wait("tensor", ev)
    f1_marks = []
    for fb in range(4):
        for ft in range(16):
            gft = fb * 16 + ft
            if len(f1_marks) >= 2:
                em.wait("gpsimd", mm, f1_marks[-2])
            em.q("gpsimd", lambda g, gft=gft, ft=ft: g.dma_start(
                out=f1Sl[:, (ft % 2) * 2048:(ft % 2) * 2048 + 2048].rearrange(
                    "p (a c) -> p a c", a=KT),
                in_=io["ff1"][gft]), inc=gw, k=16)
            em.wait("tensor", gw)
            fsl = f1Sl[:, (ft % 2) * 2048:(ft % 2) * 2048 + 2048]
            for q2, ps in ((0, p_w0), (1, p_w1)):
                for kt in range(KT):
                    em.q("tensor", lambda t, kt=kt, q2=q2, ps=ps, fsl=fsl:
                         t.matmul(
                             ps[:], fsl[:, kt * 128:(kt + 1) * 128],
                             xn2S[:, kt * 1024 + q2 * 512:
                                  kt * 1024 + (q2 + 1) * 512],
                             start=(kt == 0), stop=(kt == KT - 1)),
                         inc=(mm if kt == KT - 1 else None))
                em.wait("scalar", mm)
                em.q("scalar", lambda s, ps=ps: s.activation(
                    P1[:, 0:512], ps[:], AF.Square), inc=sc)
                em.wait("vector", sc)
                em.q("vector", lambda v: v.tensor_scalar(
                    P1[:, 0:512], P1[:, 0:512], GB, GA,
                    op0=ALU.mult, op1=ALU.add), inc=ev)
                em.wait("vector", ev)
                em.q("vector", lambda v, ps=ps: v.tensor_mul(
                    P2[:, 0:512], P1[:, 0:512], ps[:]), inc=ev)
                em.wait("scalar", ev)
                em.q("scalar", lambda s: s.activation(
                    P2[:, 0:512], P2[:, 0:512], AF.Sigmoid), inc=sc)
                em.wait("vector", sc)
                em.q("vector", lambda v, ps=ps, ft=ft, q2=q2: v.tensor_mul(
                    aS[:, ft * 1024 + q2 * 512:ft * 1024 + (q2 + 1) * 512],
                    P2[:, 0:512], ps[:]), inc=ev)
                em.wait("tensor", ev)
            f1_marks.append(mm.n)
        for dt in range(KT):
            em.wait("gpsimd", mm)
            em.q("gpsimd", lambda g, dt=dt, fb=fb: g.dma_start(
                out=f2Sl[:, (dt % 2) * 2048:(dt % 2) * 2048 + 2048].rearrange(
                    "p (a c) -> p a c", a=KT),
                in_=io["ff2"][dt, fb]), inc=gw, k=16)
            em.wait("tensor", gw)
            fsl = f2Sl[:, (dt % 2) * 2048:(dt % 2) * 2048 + 2048]
            for q2, ps in ((0, p_w0), (1, p_w1)):
                for fk in range(16):
                    em.q("tensor", lambda t, fk=fk, q2=q2, ps=ps, fsl=fsl:
                         t.matmul(
                             ps[:], fsl[:, fk * 128:(fk + 1) * 128],
                             aS[:, fk * 1024 + q2 * 512:
                                fk * 1024 + (q2 + 1) * 512],
                             start=(fk == 0), stop=(fk == KT - 1)),
                         inc=(mm if fk == KT - 1 else None))
            em.wait("vector", mm)
            if fb == 0:
                em.q("vector", lambda v, dt=dt: v.tensor_copy(
                    ACC[:, dt * 1024:dt * 1024 + 512], p_w0[:]))
                em.q("vector", lambda v, dt=dt: v.tensor_copy(
                    ACC[:, dt * 1024 + 512:dt * 1024 + 1024], p_w1[:]), inc=ev)
            else:
                em.q("vector", lambda v, dt=dt: v.tensor_add(
                    ACC[:, dt * 1024:dt * 1024 + 512],
                    ACC[:, dt * 1024:dt * 1024 + 512], p_w0[:]))
                em.q("vector", lambda v, dt=dt: v.tensor_add(
                    ACC[:, dt * 1024 + 512:dt * 1024 + 1024],
                    ACC[:, dt * 1024 + 512:dt * 1024 + 1024], p_w1[:]), inc=ev)
            em.wait("tensor", ev)
    if DBGRAW:
        em.wait("sync", ev)
        em.q("sync", lambda s: s.dma_start(
            out=io["st_raw"][:, 0:KT * 1024], in_=ACC[:]), inc=sg, k=16)
        em.q("sync", lambda s: s.dma_start(
            out=io["st_dbg3"].rearrange("a p c -> p (a c)")
            if False else io["st_dbg3"].rearrange("a p c -> p a c"),
            in_=xn2S[:].rearrange("p (a c) -> p a c", a=NCH * 2)
            if False else xn2S[:].rearrange("p (a c) -> p a c", a=NCH)),
            inc=sg, k=16)
        em.q("sync", lambda s: s.dma_start(
            out=io["st_dbg4"].rearrange("a p c -> p a c"),
            in_=aS[:].rearrange("p (a c) -> p a c", a=NCH)), inc=sg, k=16)
        em.q("sync", lambda s: s.dma_start(out=io["st_dbg2"][:, 0:32],
                                           in_=colA[:, 0:32]), inc=sg, k=16)
        em.q("sync", lambda s: s.dma_start(out=io["st_dbg2"][:, 32:64],
                                           in_=colB[:, 0:32]), inc=sg, k=16)
    for dt in range(KT):
        em.wait("sync", ev)
        em.q("sync", lambda s, dt=dt: s.dma_start(
            out=B1[:, 0:1024], in_=io["sp_xt"][dt]), inc=ld, k=16)
        em.q("sync", lambda s, dt=dt: s.dma_start(
            out=B2[:, 0:1024], in_=io["sp_hs"][dt]), inc=ld, k=16)
        em.q("sync", lambda s, dt=dt: s.dma_start(
            out=B3[:, 0:1024], in_=io["sp_lr"][dt]), inc=ld, k=16)
        em.q("sync", lambda s, dt=dt: s.dma_start(
            out=P4[:, 0:1024].rearrange("p (a w) -> p a w", a=NCH),
            in_=io["sp_x1"][:, dt].rearrange("a p w -> p a w")), inc=ld, k=16)
        em.wait("vector", ld)
        em.q("vector", lambda v, dt=dt: v.tensor_add(
            P1[:, 0:1024], B1[:, 0:1024], ACC[:, dt * 1024:(dt + 1) * 1024]),
            inc=ev)
        em.wait("vector", ev)
        em.q("vector", lambda v, dt=dt: v.scalar_tensor_tensor(
            P2[:, 0:1024], B2[:, 0:1024], b0S[:, dt:dt + 1], P1[:, 0:1024],
            op0=ALU.add, op1=ALU.subtract), inc=ev)
        em.wait("vector", ev)
        em.q("vector", lambda v: v.tensor_mul(P2[:, 0:1024], P2[:, 0:1024],
                                              D0bc[:, 0:1024]), inc=ev)
        em.wait("vector", ev)
        em.q("vector", lambda v: v.tensor_add(P1[:, 0:1024], P1[:, 0:1024],
                                              P2[:, 0:1024]), inc=ev)
        em.wait("vector", ev)
        em.q("vector", lambda v: v.tensor_add(P2[:, 0:1024], B2[:, 0:1024],
                                              B3[:, 0:1024]), inc=ev)
        em.wait("vector", ev)
        em.q("vector", lambda v, dt=dt: v.tensor_sub(
            P2[:, 0:1024], P2[:, 0:1024], B1[:, 0:1024]), inc=ev)
        em.wait("vector", ev)
        em.q("vector", lambda v, dt=dt: v.scalar_tensor_tensor(
            P2[:, 0:1024], ACC[:, dt * 1024:(dt + 1) * 1024], -1.0,
            P2[:, 0:1024], op0=ALU.mult, op1=ALU.add), inc=ev)
        em.wait("vector", ev)
        em.q("vector", lambda v: v.tensor_mul(P2[:, 0:1024], P2[:, 0:1024],
                                              D1bc[:, 0:1024]), inc=ev)
        em.wait("vector", ev)
        em.q("vector", lambda v: v.tensor_add(P1[:, 0:1024], P1[:, 0:1024],
                                              P2[:, 0:1024]), inc=ev)
        em.wait("vector", ev)
        em.q("vector", lambda v: v.tensor_add(P1[:, 0:1024], P1[:, 0:1024],
                                              P4[:, 0:1024]), inc=ev)
        em.wait("sync", ev)
        em.q("sync", lambda s, dt=dt: s.dma_start(
            out=io["sp_x2"][dt], in_=P1[:, 0:1024]), inc=sg, k=16)
        em.wait("vector", sg)
    fin = G["fin"]
    for j in range(NCH):
        em.wait("sync", ev)
        em.q("sync", lambda s, j=j: s.dma_start(
            out=P1[:, 0:2048].rearrange("p (a w) -> p a w", a=KT),
            in_=io["sp_x2"][:, :, j * 128:(j + 1) * 128].rearrange(
                "a p w -> p a w")), inc=ld, k=16)
        em.wait("vector", ld)
        em.q("vector", lambda v: v.tensor_copy(B1[:], P1[:]), inc=ev)
        em.wait("tensor", ev)
        for b in range(2):
            for k in range(8):
                kt = b * 8 + k
                em.q("tensor", lambda t, kt=kt, k=k: t.transpose(
                    p_trb[:, k * 128:(k + 1) * 128],
                    B1[:, kt * 128:(kt + 1) * 128], G["idS"][:]),
                    inc=(mm if k == 7 else None))
            em.wait("vector", mm)
            em.q("vector", lambda v, b=b: v.tensor_copy(
                P2[:].rearrange("p (a w) -> p a w", a=KT)[:, b * 8:(b + 1) * 8, :],
                p_trb[:].rearrange("p (a w) -> p a w", a=8)), inc=ev)
            em.wait("tensor", ev)
        em.wait("sync", ev)
        em.q("sync", lambda s, j=j: s.dma_start(
            out=io["y_strip"][j * 128:(j + 1) * 128, :], in_=P2[:]),
            inc=fin, k=16)
        em.wait("vector", fin)
    em.wait("sync", fin)
    f2b.close()
    fctx.close()



def kernel(**inputs):
    maps = prep_inputs(inputs)
    nc = build_program(taps=TAPS)
    for k, shp in EXPECTED_IN_SHAPES.items():
        assert tuple(maps[0][k].shape) == shp, (k, maps[0][k].shape, shp)
    res = run_bass_kernel_spmd(nc, maps, core_ids=list(range(NCORE)))
    out = np.concatenate([res.results[c]["y_strip"] for c in range(NCORE)], 0)
    return out.reshape(1, T, D)


# version 05 (final)
# speedup vs baseline: 1.0076x; 1.0037x over previous
import numpy as np
import ml_dtypes
import concourse.bass as bass
import concourse.mybir as mybir
from concourse.bass_utils import run_bass_kernel_spmd
from concourse.bass_core_rust import IndirectOffsetOnAxis
from contextlib import ExitStack

F32 = mybir.dt.float32
BF16 = mybir.dt.bfloat16
U32 = mybir.dt.uint32
AF = mybir.ActivationFunctionType
ALU = mybir.AluOpType
AX = mybir.AxisListType
BF16NP = ml_dtypes.bfloat16

T, D, H = 8192, 2048, 16
NCORE = 8
STRIP = T // NCORE
CHK = 128
NCH = STRIP // CHK
KT = D // 128
REFI = 63
SCALE = 128.0 ** -0.5
C0F = 20.0
GA = 1.5957691216057308
GB = 0.07135481283462803

ENGINES = ("sync", "tensor", "vector", "scalar", "gpsimd")
_t = np.ascontiguousarray

PHASES = dict(A=True, X1=True, C=True, F0=True, F1=True, F2=True)
DBGRAW = False
TAPS = []



def prep_inputs(inp):
    x0 = np.asarray(inp["L0_attn_hidden_states"], np.float32).reshape(T, D)
    g = lambda k: np.asarray(inp[k], np.float32)

    sh = {}
    for nm, key in (("Wq", "L0_attn_Wq"), ("Wk", "L0_attn_Wk"),
                    ("Wv", "L0_attn_Wv"), ("Ww", "L0_attn_Ww")):
        w = g(key)
        sh[nm] = _t(w.reshape(KT, 128, KT, 128).transpose(2, 1, 0, 3))
    for nm, key in (("cq", "L0_attn_cq"), ("ck", "L0_attn_ck"),
                    ("cv", "L0_attn_cv")):
        sh[nm] = _t(g(key).reshape(KT, 128, 4).transpose(1, 0, 2).reshape(128, KT * 4))
    sh["Wf1"] = _t(g("L0_attn_Wf1").reshape(KT, 128, 128).transpose(1, 0, 2))
    sh["Wf2"] = g("L0_attn_Wf2")
    sh["Wb"] = _t(g("L0_attn_Wb").reshape(KT, 128, H).transpose(1, 0, 2))
    sh["Wg1"] = _t(g("L0_attn_Wg1").reshape(KT, 128, 128).transpose(1, 0, 2))
    sh["Wg2"] = g("L0_attn_Wg2")
    sh["bg2"] = _t(g("L0_attn_bg2").reshape(1, D))
    sh["nw"] = _t(g("L0_attn_norm_w").reshape(128, 1))
    sh["Wo"] = _t(g("L0_attn_Wo").reshape(KT, 128, KT, 128).transpose(1, 0, 2, 3))
    sh["rtr"] = _t(g("L0_ffn_w_router").T.reshape(KT, 128, 3).transpose(1, 0, 2))
    sh["b0"] = _t(g("L0_ffn_simple_bias").reshape(KT, 128).T)
    sh["wdn"] = _t(g("L0_ffn_w_down").T.reshape(KT, 128, 32).transpose(1, 0, 2))
    sh["wup"] = _t(g("L0_ffn_w_up").T.reshape(32, KT, 128))
    for nm, key in (("l1s", "L0_ffn_ln1_scale"), ("l1b", "L0_ffn_ln1_bias"),
                    ("l2s", "L0_ffn_ln2_scale"), ("l2b", "L0_ffn_ln2_bias")):
        sh[nm] = _t(g(key).reshape(KT, 128).T)
    wo = g("L0_ffn_wo")
    sh["fwo"] = _t(wo.reshape(KT, 128, KT, 128).transpose(0, 3, 2, 1))
    sh["ff1"] = _t(g("L0_ffn_w_ff1").reshape(64, 128, KT, 128).transpose(0, 3, 2, 1))
    sh["ff2"] = _t(g("L0_ffn_w_ff2").reshape(KT, 128, 4, 16, 128).transpose(
        0, 2, 4, 3, 1))

    wq, wk, wv = g("L0_ffn_wq"), g("L0_ffn_wk"), g("L0_ffn_wv")
    maps = []
    for c in range(NCORE):
        m = dict(sh)
        m["x_strip"] = _t(x0[c * STRIP:(c + 1) * STRIP])
        halo = np.zeros((128, D), np.float32)
        if c > 0:
            halo[125:128] = x0[c * STRIP - 3:c * STRIP]
        m["x_halo"] = halo
        mu = np.zeros((128, NCORE), np.float32)
        mu[:, :c] = 1.0
        m["mu"] = mu
        ci = np.zeros((128, KT), np.uint32)
        for kt in range(KT):
            ci[:, kt] = (kt // 2) * D + c * 256 + (kt % 2) * 128 + np.arange(128)
        m["ctxidx"] = ci
        for nm, w in (("fwq", wq), ("fwk", wk), ("fwv", wv)):
            ws = w[256 * c:256 * (c + 1)]
            m[nm] = _t(ws.reshape(2, 128, KT, 128).transpose(3, 2, 0, 1))
        maps.append(m)
    return maps



class Sem:
    __slots__ = ("h", "n", "name")

    def __init__(self, ctx, nc, name):
        self.h = ctx.enter_context(nc.semaphore(name))
        self.n = 0
        self.name = name


class Em:
    def __init__(self, nc, ctx):
        self.nc = nc
        self.ctx = ctx
        self.ops = {e: [] for e in ENGINES}

    def sem(self, name):
        return Sem(self.ctx, self.nc, name)

    def q(self, eng, fn, inc=None, k=1):
        incs = []
        if inc is not None:
            incs = [(inc, k)] if isinstance(inc, Sem) else list(inc)
        for s, kk in incs:
            s.n += kk

        def thunk(e, fn=fn, incs=tuple(incs)):
            inst = fn(e)
            for s, kk in incs:
                inst.then_inc(s.h, kk)

        self.ops[eng].append(thunk)

    def wait(self, eng, s, val=None):
        v = s.n if val is None else val
        if v > 0:
            self.ops[eng].append(lambda e, h=s.h, v=v: e.wait_ge(h, v))

    def barrier(self, name, engines=ENGINES):
        prog = getattr(self, "progress_sems", [])
        for e in engines:
            for s in prog:
                self.wait(e, s)
        b = self.sem("bar_" + name)
        for e in engines:
            self.ops[e].append(lambda en, h=b.h: en.nop().then_inc(h, 1))
            b.n += 1
        tgt = b.n
        for e in engines:
            self.wait(e, b, tgt)



EXPECTED_IN_SHAPES = {}


def build_program(taps=()):
    nc = bass.Bass("TRN2", target_bir_lowering=False)
    gctx = ExitStack()
    io = {}

    def din(name, shape, dt=F32):
        EXPECTED_IN_SHAPES[name] = tuple(shape)
        io[name] = nc.dram_tensor(name, list(shape), dt, kind="ExternalInput").ap()

    def dout(name, shape, dt=F32):
        io[name] = nc.dram_tensor(name, list(shape), dt, kind="ExternalOutput").ap()

    def dint(name, shape, dt=F32):
        io[name] = nc.dram_tensor(name, list(shape), dt, kind="Internal").ap()

    din("x_strip", (STRIP, D)); din("x_halo", (128, D)); din("mu", (128, NCORE))
    din("ctxidx", (128, KT), U32)
    for nm in ("Wq", "Wk", "Wv", "Ww"):
        din(nm, (KT, 128, KT, 128))
    din("Wo", (128, KT, KT, 128))
    for nm in ("cq", "ck", "cv"):
        din(nm, (128, KT * 4))
    din("Wf1", (128, KT, 128)); din("Wf2", (128, D)); din("Wb", (128, KT, H))
    din("Wg1", (128, KT, 128)); din("Wg2", (128, D)); din("bg2", (1, D))
    din("nw", (128, 1)); din("rtr", (128, KT, 3)); din("b0", (128, KT))
    din("wdn", (128, KT, 32)); din("wup", (32, KT, 128))
    for nm in ("l1s", "l1b", "l2s", "l2b"):
        din(nm, (128, KT))
    din("fwo", (KT, 128, KT, 128))
    din("ff1", (64, 128, KT, 128)); din("ff2", (KT, 4, 128, 16, 128))
    for nm in ("fwq", "fwk", "fwv"):
        din(nm, (128, KT, 2, 128))

    dout("y_strip", (STRIP, D))

    for nm in ("st_q", "st_k", "st_Kt", "st_Vt", "st_U", "st_A", "st_E", "st_SIG"):
        dint(nm, (NCH, 128, D), BF16)
    dint("st_rows", (NCH, 2, D))
    dint("st_raw", (128, KT * 1027), BF16)
    dint("st_hln", (128, KT * (STRIP + 128)), BF16)
    dint("st_dbg2", (128, 64))
    dint("st_dbg3", (NCH, 128, D), BF16)
    dint("st_dbg4", (NCH, 128, D), BF16)
    dint("cc_in", (258, D)); dint("cc_out", (NCORE * 258, D))
    dint("cc_xn_in", (D, STRIP), BF16); dint("cc_xn_out", (NCORE * D, STRIP), BF16)
    dint("cc_ctx_in", (D, STRIP), BF16)
    dint("cc_ctx_out", (NCORE * D, STRIP), BF16)
    dint("sp_x1", (NCH, KT, 128, 128))
    dint("sp_hs", (KT, 128, STRIP), BF16)
    dint("sp_xt", (KT, 128, STRIP), BF16)
    dint("sp_lr", (KT, 128, STRIP), BF16)
    dint("sp_x2", (KT, 128, STRIP))
    dint("st_ffq", (2, 128, T), BF16)
    dint("st_ffk", (2, 128, T), BF16)
    dint("st_ffv", (2, 128, T), BF16)
    dint("st_a", (64, 128, STRIP), BF16)

    cst = {}
    cst["ident"] = nc.inline_tensor(np.eye(128, dtype=BF16NP), name="c_id").ap()
    cst["triu"] = nc.inline_tensor(
        np.triu(np.ones((128, 128), np.float32)), name="c_triu").ap()
    s_ = np.arange(128)[:, None]
    t_ = np.arange(128)[None, :]
    lneg = (s_ <= REFI).astype(np.float32) - (s_ <= t_).astype(np.float32)
    cst["lneg"] = nc.inline_tensor(_t(lneg), name="c_lneg").ap()
    ext = np.zeros((128, 65), np.float32)
    ext[:, 0] = -(np.arange(128) > REFI).astype(np.float32)
    ext[:, 64] = -(np.arange(128) <= REFI).astype(np.float32)
    cst["gext"] = nc.inline_tensor(ext, name="c_gext").ap()
    cst["ones_r"] = nc.inline_tensor(np.ones((1, 128), np.float32), name="c_or").ap()
    o65 = np.zeros((65, 128), np.float32); o65[0] = 1.0; o65[64] = 1.0
    cst["ones_r65"] = nc.inline_tensor(o65, name="c_or65").ap()
    cst["ones_r16"] = nc.inline_tensor(np.ones((1, 128), BF16NP), name="c_or16").ap()
    cst["ones_c16"] = nc.inline_tensor(np.ones((128, 1), BF16NP), name="c_oc16").ap()
    dm = np.zeros((128, 4 * 512), BF16NP)
    kv_ = np.arange(128)[:, None]
    ql_ = np.arange(512)[None, :]
    for j in range(4):
        dm[:, j * 512:(j + 1) * 512] = (ql_ >= 128 * j + kv_).astype(BF16NP)
    cst["dmask"] = nc.inline_tensor(dm, name="c_dmask").ap()
    cst["ones_c32"] = nc.inline_tensor(np.ones((128, 1), np.float32),
                                       name="c_oc32").ap()
    cst["ident32"] = nc.inline_tensor(np.eye(128, dtype=np.float32),
                                      name="c_id32").ap()

    em = Em(nc, gctx)

    gsb = lambda n, s, d=F32: gctx.enter_context(nc.sbuf_tensor(n, list(s), d))
    P1 = gsb("P1", (128, 2048)); P2 = gsb("P2", (128, 2048))
    P3 = gsb("P3", (128, 2048)); P4 = gsb("P4", (128, 2048))
    B1 = gsb("B1", (128, 2048), BF16); B2 = gsb("B2", (128, 2048), BF16)
    B3 = gsb("B3", (128, 2048), BF16); B4 = gsb("B4", (128, 2048), BF16)
    B5 = gsb("B5", (128, 2048), BF16)
    rowA = gsb("rowA", (65, 2048)); rowB = rowA
    rowC = gsb("rowC", (1, 2048)); rowD = gsb("rowD", (65, 2048))
    colA = gsb("colA", (128, 64)); colB = gsb("colB", (128, 64))
    beta = gsb("beta", (128, NCH * H))
    onesR = gsb("onesR", (1, 128)); onesR16 = gsb("onesR16", (1, 128), BF16)
    onesR65 = gsb("onesR65", (65, 128))
    onesC16 = gsb("onesC16", (128, 1), BF16)
    idS = gsb("idS", (128, 128), BF16)
    triuS = gsb("triuS", (128, 128))
    lnegS = gsb("lnegS", (128, 128)); gextS = gsb("gextS", (128, 65))
    t1S = gsb("t1S", (128, 128), BF16); t2S = gsb("t2S", (128, 128), BF16)
    onesC32 = gsb("onesC32", (128, 1)); idS32 = gsb("idS32", (128, 128))

    G = dict(nc=nc, em=em, io=io, cst=cst, gctx=gctx,
             P1=P1, P2=P2, P3=P3, P4=P4, B1=B1, B2=B2, B3=B3, B4=B4, B5=B5,
             rowA=rowA, rowB=rowB, rowC=rowC, rowD=rowD,
             colA=colA, colB=colB, beta=beta, onesR=onesR, onesR16=onesR16,
             onesC16=onesC16, idS=idS, triuS=triuS, lnegS=lnegS, gextS=gextS,
             t1S=t1S, t2S=t2S, onesC32=onesC32, idS32=idS32, onesR65=onesR65)

    for s in ("ld", "gw", "mm", "ev", "sc", "sg", "pr", "cc", "fin"):
        G[s] = em.sem(s)
    em.progress_sems = [G[s] for s in ("ld", "gw", "mm", "ev", "sc", "sg", "cc")]

    colZ = gsb("colZ", (128, 4))
    nc.const_aps.register(F32, 1e-5, colZ[:, 0:1])
    nc.const_aps.register(F32, 1e-6, colZ[:, 1:2])
    nc.const_aps.register(F32, -C0F, colZ[:, 2:3])
    czv = np.zeros((128, 4), np.float32)
    czv[:, 0] = 1e-5; czv[:, 1] = 1e-6; czv[:, 2] = -C0F
    cst["colz"] = nc.inline_tensor(czv, name="c_colz").ap()
    em.q("sync", lambda s: s.dma_start(out=colZ[:], in_=cst["colz"]),
         inc=G["ld"], k=16)
    em.wait("scalar", G["ld"])
    em.wait("vector", G["ld"])

    ld = G["ld"]
    for dst, src in ((onesR, "ones_r"), (onesR16, "ones_r16"),
                     (onesC16, "ones_c16"), (idS, "ident"), (triuS, "triu"),
                     (lnegS, "lneg"), (gextS, "gext"), (onesC32, "ones_c32"),
                     (idS32, "ident32"), (onesR65, "ones_r65")):
        em.q("sync", lambda s, d=dst, c=cst[src]: s.dma_start(out=d[:], in_=c),
             inc=ld, k=16)

    if PHASES["A"]:
        phaseA(G)
    if PHASES["X1"]:
        phaseX1(G)
    if PHASES["C"]:
        phaseC(G)
    if PHASES["F0"]:
        phaseF0(G)
    if PHASES["F1"]:
        phaseF1(G)
    if PHASES["F2"]:
        phaseF2(G)

    if taps:
        em.barrier("taps")
        ts = em.sem("tapsem")
        for nm in taps:
            ap = io[nm]
            tp = nc.dram_tensor("tap_" + nm, list(ap.shape), ap.dtype,
                                kind="ExternalOutput").ap()
            em.q("gpsimd", lambda g, o=tp, i=ap: g.dma_start(out=o, in_=i),
                 inc=ts, k=16)
        em.wait("gpsimd", ts)

    blk = gctx.enter_context(nc.Block())
    for eng in ENGINES:
        ops = em.ops[eng]
        getattr(blk, eng)(lambda e, ops=ops: [op(e) for op in ops])
    gctx.close()
    return nc



def phaseA(G):
    nc, em, io, cst = G["nc"], G["em"], G["io"], G["cst"]
    ld, gw, mm, ev, sc, sg, pr = (G[k] for k in
                                  ("ld", "gw", "mm", "ev", "sc", "sg", "pr"))
    P1, P2, P3, P4 = G["P1"], G["P2"], G["P3"], G["P4"]
    B1, B2, B3 = G["B1"], G["B2"], G["B3"]
    rowA, rowC, rowD, colA = G["rowA"], G["rowC"], G["rowD"], G["colA"]
    beta = G["beta"]

    actx = ExitStack()
    asb = lambda n, s, d=F32: actx.enter_context(nc.sbuf_tensor(n, list(s), d))
    aps = lambda n, s, d=F32: actx.enter_context(nc.psum_tensor(n, list(s), d))

    HW = STRIP + 128
    hlnT = asb("hlnT", (128, KT * HW), BF16)
    SkR = asb("SkR", (128, 2048))
    SvR = asb("SvR", (128, 2048))
    G["SkR"], G["SvR"] = SkR, SvR

    em.q("vector", lambda v: v.memset(SkR[:], 0.0), inc=ev)
    em.q("vector", lambda v: v.memset(SvR[:], 0.0), inc=ev)

    for nm in ("Wf1", "Wg1"):
        em.q("gpsimd", lambda g, n=nm: g.dma_start(
            out=smallw[n][:].rearrange("p (a c) -> p a c", a=KT),
            in_=io[n]), inc=gw, k=16)
    for nm in ("Wf2", "Wg2", "bg2"):
        em.q("gpsimd", lambda g, n=nm: g.dma_start(out=smallw[n][:], in_=io[n]),
             inc=gw, k=16)
    em.q("gpsimd", lambda g: g.dma_start(
        out=smallw["Wb"][:].rearrange("p (a c) -> p a c", a=KT),
        in_=io["Wb"]), inc=gw, k=16)
    for nm in ("cq", "ck", "cv"):
        em.q("sync", lambda s, n=nm: s.dma_start(out=convw[n][:], in_=io[n]),
             inc=ld, k=16)

    convw = {}
    for nm in ("cq", "ck", "cv"):
        convw[nm] = asb("sw_" + nm, (128, KT * 4))
    a1ctx = ExitStack()
    a1ps = lambda n, s, d=F32: a1ctx.enter_context(nc.psum_tensor(n, list(s), d))
    a1sb = lambda n, s, d=F32: a1ctx.enter_context(nc.sbuf_tensor(n, list(s), d))
    smallw = {}
    for nm, shape in (("Wf1", (128, KT * 128)), ("Wf2", (128, D)),
                      ("Wb", (128, KT * H)), ("Wg1", (128, KT * 128)),
                      ("Wg2", (128, D)), ("bg2", (1, D))):
        smallw[nm] = a1sb("sw_" + nm, shape, BF16)
    p_tr = a1ps("pA_tr", (128, 1024), BF16)
    p_g1 = a1ps("pA_g1", (128, 128))
    p_gg = a1ps("pA_gg", (128, 512))
    p_cum = a1ps("pA_cum", (128, 1024))
    p_row = a1ps("pA_row", (65, 512))
    p_bet = a1ps("pA_bet", (128, H))

    xrow, xnt = P1, P2
    gsp, gtmp = P3, P4
    xn16, At16, SIG16 = B1, B2, B3
    mvc = colA

    def ln_tile(j, is_halo):
        em.wait("scalar", sg)
        em.wait("vector", sg)
        em.wait("sync", ev)
        em.wait("sync", sc)
        src = (io["x_halo"] if is_halo
               else io["x_strip"][j * 128:(j + 1) * 128, :])
        em.q("sync", lambda s, src=src: s.dma_start(out=xrow[:], in_=src),
             inc=ld, k=16)
        em.wait("vector", ld)
        em.wait("scalar", ld)
        em.q("vector", lambda v: v.reduce_sum(mvc[:, 2:3], xrow[:], axis=AX.X),
             inc=ev)
        em.q("scalar", lambda s: s.activation(xnt[:], xrow[:], AF.Square), inc=sc)
        em.wait("vector", sc)
        em.q("vector", lambda v: v.reduce_sum(mvc[:, 3:4], xnt[:], axis=AX.X),
             inc=ev)
        em.wait("vector", ev)
        em.q("vector", lambda v: v.tensor_scalar(
            mvc[:, 0:1], mvc[:, 2:3], 1.0 / D, None, op0=ALU.mult), inc=ev)
        em.wait("vector", ev)
        em.q("vector", lambda v: v.tensor_mul(mvc[:, 4:5], mvc[:, 0:1], mvc[:, 0:1]),
             inc=ev)
        em.q("vector", lambda v: v.tensor_scalar(
            mvc[:, 5:6], mvc[:, 3:4], 1.0 / D, None, op0=ALU.mult), inc=ev)
        em.wait("vector", ev)
        em.q("vector", lambda v: v.tensor_sub(mvc[:, 1:2], mvc[:, 5:6], mvc[:, 4:5]),
             inc=ev)
        em.wait("scalar", ev)
        em.q("scalar", lambda s: s.activation(mvc[:, 1:2], mvc[:, 1:2], AF.Sqrt,
                                              bias=1e-5), inc=sc)
        em.wait("vector", sc)
        em.q("vector", lambda v: v.reciprocal(mvc[:, 1:2], mvc[:, 1:2]), inc=ev)
        em.wait("vector", ev)
        em.q("vector", lambda v: v.tensor_scalar(
            xnt[:], xrow[:], mvc[:, 0:1], mvc[:, 1:2],
            op0=ALU.subtract, op1=ALU.mult), inc=ev)
        em.wait("vector", ev)
        em.q("vector", lambda v: v.tensor_copy(xn16[:], xnt[:]), inc=ev)
        em.wait("tensor", ev)
        col0 = 0 if is_halo else 128 + j * 128
        h3 = hlnT[:].rearrange("p (a w) -> p a w", a=KT)
        for b in range(2):
            for k in range(8):
                kt = b * 8 + k
                em.q("tensor", lambda t, kt=kt, k=k: t.transpose(
                    p_tr[:, k * 128:(k + 1) * 128],
                    xn16[:, kt * 128:(kt + 1) * 128], G["idS"][:]),
                    inc=(mm if k == 7 else None))
            em.wait("vector", mm)
            em.q("vector", lambda v, b=b, col0=col0: v.tensor_copy(
                h3[:, b * 8:(b + 1) * 8, col0:col0 + 128],
                p_tr[:].rearrange("p (a w) -> p a w", a=8)), inc=ev)
            em.wait("tensor", ev)

    def gpath_tile(j):
        col0 = 128 + j * 128
        h3 = hlnT[:].rearrange("p (a w) -> p a w", a=KT)
        hj = h3[:, :, col0:col0 + 128]
        em.wait("tensor", G["gw"])
        for kt in range(KT):
            em.q("tensor", lambda t, kt=kt: t.matmul(
                p_g1[:], smallw["Wf1"][:, kt * 128:(kt + 1) * 128], hj[:, kt, :],
                start=(kt == 0), stop=(kt == KT - 1)),
                inc=(mm if kt == KT - 1 else None))
        em.wait("vector", mm)
        em.q("vector", lambda v: v.tensor_copy(G["t1S"][:], p_g1[:]), inc=ev)
        em.wait("tensor", ev)
        for kt in range(KT):
            em.q("tensor", lambda t, kt=kt: t.matmul(
                p_bet[:], hj[:, kt, :],
                smallw["Wb"][:, kt * H:(kt + 1) * H],
                start=(kt == 0), stop=(kt == KT - 1)),
                inc=(mm if kt == KT - 1 else None))
        em.wait("scalar", mm)
        em.q("scalar", lambda s, j=j: s.activation(
            beta[:, j * H:(j + 1) * H], p_bet[:], AF.Sigmoid), inc=sc)
        em.wait("tensor", sc)
        for kt in range(KT):
            em.q("tensor", lambda t, kt=kt: t.matmul(
                p_g1[:], smallw["Wg1"][:, kt * 128:(kt + 1) * 128], hj[:, kt, :],
                start=(kt == 0), stop=(kt == KT - 1)),
                inc=(mm if kt == KT - 1 else None))
        em.wait("vector", mm)
        em.q("vector", lambda v: v.tensor_copy(G["t2S"][:], p_g1[:]), inc=ev)
        for q4 in range(4):
            em.q("tensor", lambda t, q4=q4: t.matmul(
                p_gg[:], G["t1S"][:], smallw["Wf2"][:, q4 * 512:(q4 + 1) * 512],
                start=True, stop=True), inc=mm)
            em.wait("scalar", mm)
            em.q("scalar", lambda s, q4=q4: s.activation(
                gtmp[:, q4 * 512:(q4 + 1) * 512], p_gg[:], AF.Exp, scale=-1.0),
                inc=sc)
            em.wait("tensor", sc)
        em.q("scalar", lambda s: s.activation(gsp[:], gtmp[:], AF.Ln, bias=1.0),
             inc=sc)
        em.wait("tensor", ev)
        for q4 in range(4):
            em.q("tensor", lambda t, q4=q4: t.matmul(
                p_gg[:], G["t2S"][:], smallw["Wg2"][:, q4 * 512:(q4 + 1) * 512],
                start=True, stop=False))
            em.q("tensor", lambda t, q4=q4: t.matmul(
                p_gg[:], G["onesR16"][:], smallw["bg2"][:, q4 * 512:(q4 + 1) * 512],
                start=False, stop=True), inc=mm)
            em.wait("scalar", mm)
            em.q("scalar", lambda s, q4=q4: s.activation(
                SIG16[:, q4 * 512:(q4 + 1) * 512], p_gg[:], AF.Sigmoid), inc=sc)
            em.wait("tensor", sc)
        em.wait("sync", sc)
        em.q("sync", lambda s, j=j: s.dma_start(out=io["st_SIG"][j], in_=SIG16[:]),
             inc=sg, k=16)
        em.wait("tensor", sc)
        for half in range(2):
            for q2 in range(2):
                em.q("tensor", lambda t, half=half, q2=q2: t.matmul(
                    p_cum[:, q2 * 512:(q2 + 1) * 512], G["lnegS"][:],
                    gsp[:, half * 1024 + q2 * 512:half * 1024 + (q2 + 1) * 512],
                    start=True, stop=True), inc=(mm if q2 == 1 else None))
            em.wait("scalar", mm)
            em.q("scalar", lambda s, half=half: s.activation(
                At16[:, half * 1024:(half + 1) * 1024], p_cum[:], AF.Exp), inc=sc)
            em.q("scalar", lambda s, half=half: s.activation(
                xn16[:, half * 1024:(half + 1) * 1024], p_cum[:], AF.Exp,
                scale=-1.0), inc=sc)
            em.wait("tensor", sc)
        em.wait("sync", sc)
        em.q("sync", lambda s, j=j: s.dma_start(out=io["st_A"][j], in_=At16[:]),
             inc=sg, k=16)
        em.q("sync", lambda s, j=j: s.dma_start(out=io["st_E"][j], in_=xn16[:]),
             inc=sg, k=16)
        for q4 in range(4):
            em.q("tensor", lambda t, q4=q4: t.matmul(
                p_row[:], G["gextS"][:], gsp[:, q4 * 512:(q4 + 1) * 512],
                start=True, stop=True), inc=mm)
            em.wait("scalar", mm)
            em.q("scalar", lambda s, q4=q4: s.activation(
                rowA[:, q4 * 512:(q4 + 1) * 512], p_row[:], AF.Exp), inc=sc)
            em.wait("tensor", sc)
        em.wait("sync", sc)
        em.q("sync", lambda s, j=j: s.dma_start(out=io["st_rows"][j][0:1],
                                                in_=rowA[0:1, :]), inc=sg, k=16)
        em.q("sync", lambda s, j=j: s.dma_start(out=io["st_rows"][j][1:2],
                                                in_=rowA[64:65, :]), inc=sg, k=16)
        em.wait("vector", sc)
        em.q("vector", lambda v: v.tensor_copy(rowD[0:1, :], rowA[64:65, :]),
             inc=ev)
        em.wait("vector", ev)
        if j == 0:
            em.q("vector", lambda v: v.tensor_mul(rowC[:], rowA[0:1, :],
                                                  rowD[0:1, :]), inc=ev)
        else:
            em.q("vector", lambda v: v.tensor_mul(gtmp[0:1, :], rowA[0:1, :],
                                                  rowD[0:1, :]), inc=ev)
            em.wait("vector", ev)
            em.q("vector", lambda v: v.tensor_mul(rowC[:], rowC[:], gtmp[0:1, :]),
                 inc=ev)

    ln_tile(0, True)
    for j in range(NCH):
        ln_tile(j, False)
        gpath_tile(j)
    if DBGRAW:
        em.wait("sync", G["ev"])
        em.wait("sync", G["sc"])
        em.q("sync", lambda s: s.dma_start(out=io["st_hln"], in_=hlnT[:]),
             inc=G["sg"], k=16)
    em.barrier("endA1")
    a1ctx.close()

    phaseA2(G, actx, asb, aps, hlnT, convw)
    gw, sg2 = G["gw"], G["sg"]
    em.wait("gpsimd", G["ev"])
    em.wait("gpsimd", G["sg"])
    em.q("gpsimd", lambda g: g.dma_start(out=io["cc_in"][0:128, :], in_=SkR[:]),
         inc=gw, k=16)
    em.q("gpsimd", lambda g: g.dma_start(out=io["cc_in"][128:256, :], in_=SvR[:]),
         inc=gw, k=16)
    em.q("gpsimd", lambda g: g.dma_start(out=io["cc_in"][256:257, :],
                                         in_=G["rowC"][:]), inc=gw, k=16)
    em.wait("gpsimd", gw)
    em.barrier("endA")
    actx.close()


def phaseA2(G, actx, asb, aps, hlnT, convw):
    nc, em, io = G["nc"], G["em"], G["io"]
    ld, gw, mm, ev, sc, sg, pr = (G[k] for k in
                                  ("ld", "gw", "mm", "ev", "sc", "sg", "pr"))
    P1, P2, P3, P4 = G["P1"], G["P2"], G["P3"], G["P4"]
    B1, B2, B3, B4, B5 = G["B1"], G["B2"], G["B3"], G["B4"], G["B5"]
    rowB, colA = G["rowB"], G["colA"]
    beta, SkR, SvR = G["beta"], G["SkR"], G["SvR"]

    raw = asb("rawA", (128, KT * 1027), BF16)
    gcolsA = asb("gcolsA", (128, 32))
    cvo = asb("cvoA", (128, KT * 1024), BF16)
    wsl = asb("wslA", (128, 2 * 2048), BF16)
    p_h = aps("pA_h", (128, 128))
    p_s0 = aps("pA_s0", (128, 512))
    p_s1 = aps("pA_s1", (128, 512))
    p_tp = aps("pA_tp", (128, 1024), BF16)
    p_c1 = aps("pA_c1", (128, 512))
    p_c2 = aps("pA_c2", (128, 512))
    p_gb = aps("pA_gb", (128, 512))
    prj = em.sem("a_prj")

    h3 = hlnT[:].rearrange("p (a w) -> p a w", a=KT)
    raw3 = raw[:].rearrange("p (a w) -> p a w", a=KT)
    cvo3 = cvo[:].rearrange("p (a w) -> p a w", a=KT)

    def proj_conv(wname, cname):
        em.wait("vector", sg)
        for ot in range(KT):
            half = ot % 2
            dst = wsl[:, half * 2048:(half + 1) * 2048]
            if prj.n >= 2:
                em.wait("gpsimd", prj, prj.n - 1)
            em.q("gpsimd", lambda g, ot=ot, dst=dst, w=wname: g.dma_start(
                out=dst.rearrange("p (a c) -> p a c", a=KT),
                in_=io[w][ot]), inc=gw, k=16)
            em.wait("tensor", gw)
            for kt in range(KT):
                em.q("tensor", lambda t, kt=kt, dst=dst: t.matmul(
                    p_h[:], dst[:, kt * 128:(kt + 1) * 128], h3[:, kt, 0:128],
                    start=(kt == 0), stop=(kt == KT - 1)),
                    inc=(mm if kt == KT - 1 else None))
            for th, ps in ((0, p_s0), (1, p_s1)):
                for kt in range(KT):
                    inc = None
                    if kt == KT - 1:
                        inc = prj if th == 1 else mm
                    em.q("tensor", lambda t, kt=kt, th=th, ps=ps, dst=dst: t.matmul(
                        ps[:], dst[:, kt * 128:(kt + 1) * 128],
                        h3[:, kt, 128 + th * 512:128 + (th + 1) * 512],
                        start=(kt == 0), stop=(kt == KT - 1)),
                        inc=inc)
            em.wait("vector", mm)
            em.wait("vector", prj)
            em.q("vector", lambda v, ot=ot: v.tensor_copy(
                raw3[:, ot, 0:3], p_h[:, 125:128]), inc=ev)
            em.q("vector", lambda v, ot=ot: v.tensor_copy(
                raw3[:, ot, 3:515], p_s0[:]), inc=ev)
            em.q("vector", lambda v, ot=ot: v.tensor_copy(
                raw3[:, ot, 515:1027], p_s1[:]), inc=ev)
            em.wait("tensor", ev)
        if DBGRAW and wname == "Wq":
            em.wait("sync", ev)
            em.q("sync", lambda s: s.dma_start(out=io["st_raw"], in_=raw[:]),
                 inc=sg, k=16)
        cw = convw[cname]
        em.wait("vector", sg)
        em.wait("scalar", sg)
        em.wait("vector", ev)
        for ot in range(KT):
            em.q("vector", lambda v, ot=ot: v.tensor_scalar(
                P1[:, 0:1024], raw3[:, ot, 0:1024], cw[:, ot * 4:ot * 4 + 1],
                None, op0=ALU.mult))
            for tap in range(1, 4):
                em.q("vector", lambda v, ot=ot, tap=tap: v.scalar_tensor_tensor(
                    P1[:, 0:1024], raw3[:, ot, tap:tap + 1024],
                    cw[:, ot * 4 + tap:ot * 4 + tap + 1], P1[:, 0:1024],
                    op0=ALU.mult, op1=ALU.add), inc=(ev if tap == 3 else None))
            em.wait("scalar", ev)
            em.q("scalar", lambda s, ot=ot: s.activation(
                cvo3[:, ot, :], P1[:, 0:1024], AF.Silu), inc=sc)
            em.wait("vector", sc)

    def stage_chan(stname):
        em.wait("sync", G["sc"])
        for j in range(NCH):
            em.q("sync", lambda s, j=j, st=stname: s.dma_start(
                out=io[st][j].rearrange("p (a w) -> p a w", a=KT),
                in_=cvo[:].rearrange("p (a c w) -> p a c w", a=KT, c=NCH)[
                    :, :, j, :]), inc=sg, k=16)

    def transpose_chunk(j, dst16):
        cvo4 = cvo[:].rearrange("p (a c w) -> p a c w", a=KT, c=NCH)
        em.wait("tensor", sc)
        for b in range(2):
            for k in range(8):
                kt = b * 8 + k
                em.q("tensor", lambda t, kt=kt, k=k, j=j: t.transpose(
                    p_tp[:, k * 128:(k + 1) * 128], cvo4[:, kt, j, :],
                    G["idS"][:]), inc=(mm if k == 7 else None))
            em.wait("vector", mm)
            em.q("vector", lambda v, b=b: v.tensor_copy(
                dst16[:].rearrange("p (a w) -> p a w", a=KT)[:, b * 8:(b + 1) * 8, :],
                p_tp[:].rearrange("p (a w) -> p a w", a=8)), inc=ev)
            em.wait("tensor", ev)

    proj_conv("Wq", "cq")
    stage_chan("st_q")
    proj_conv("Wk", "ck")
    stage_chan("st_k")
    for j in range(NCH):
        transpose_chunk(j, B1)
        em.wait("sync", ev)
        em.q("sync", lambda s, j=j: s.dma_start(out=io["st_Kt"][j], in_=B1[:]),
             inc=sg, k=16)
        em.wait("vector", sg)
    proj_conv("Wv", "cv")
    for j in range(NCH):
        transpose_chunk(j, B1)
        em.wait("sync", ev)
        em.q("sync", lambda s, j=j: s.dma_start(out=io["st_Vt"][j], in_=B1[:]),
             inc=sg, k=16)
        em.wait("vector", sg)
    proj_conv("Ww", "cv")
    for j in range(NCH):
        transpose_chunk(j, B1)
        em.wait("sync", mm)
        em.wait("sync", ev)
        em.wait("sync", sc)
        em.q("sync", lambda s, j=j: s.dma_start(out=B3[:], in_=io["st_E"][j]),
             inc=ld, k=16)
        em.q("sync", lambda s, j=j: s.dma_start(out=B4[:], in_=io["st_Kt"][j]),
             inc=ld, k=16)
        em.q("sync", lambda s, j=j: s.dma_start(out=B5[:], in_=io["st_Vt"][j]),
             inc=ld, k=16)
        em.q("sync", lambda s, j=j: s.dma_start(out=rowB[0:1, :],
                                                in_=io["st_rows"][j][0:1]),
             inc=ld, k=16)
        em.q("sync", lambda s, j=j: s.dma_start(out=rowB[64:65, :],
                                                in_=io["st_rows"][j][1:2]),
             inc=ld, k=16)
        em.wait("scalar", ev)
        em.q("scalar", lambda s: s.activation(P2[:], B1[:], AF.Square), inc=sc)
        em.wait("vector", sc)
        em.q("vector", lambda v: v.reduce_sum(
            colA[:, 0:16].rearrange("p (h o) -> p h o", o=1),
            P2[:].rearrange("p (h m) -> p h m", m=128), axis=AX.X), inc=ev)
        em.wait("scalar", ev)
        em.q("scalar", lambda s: s.activation(colA[:, 0:16], colA[:, 0:16],
                                              AF.Sqrt, bias=1e-6), inc=sc)
        em.wait("vector", sc)
        em.q("vector", lambda v: v.reciprocal(colA[:, 0:16], colA[:, 0:16]),
             inc=ev)
        em.wait("vector", ev)
        em.q("vector", lambda v, j=j: v.tensor_mul(
            colA[:, 0:16], colA[:, 0:16], beta[:, j * H:(j + 1) * H]), inc=ev)
        em.wait("vector", ld)
        em.wait("vector", ev)
        for h in range(H):
            em.q("vector", lambda v, h=h: v.tensor_scalar(
                P2[:, h * 128:(h + 1) * 128], B1[:, h * 128:(h + 1) * 128],
                colA[:, h:h + 1], None, op0=ALU.mult),
                inc=(ev if h == H - 1 else None))
        em.wait("vector", ev)
        em.q("vector", lambda v: v.tensor_mul(B2[:], P2[:], B3[:]), inc=ev)
        em.wait("sync", ev)
        em.q("sync", lambda s, j=j: s.dma_start(out=io["st_U"][j], in_=B2[:]),
             inc=sg, k=16)
        em.wait("tensor", ld)
        for q4 in range(4):
            em.q("tensor", lambda t, q4=q4: t.matmul(
                p_gb[:], G["onesR"][:], rowB[0:1, q4 * 512:(q4 + 1) * 512],
                start=True, stop=True), inc=mm)
            em.wait("vector", mm)
            em.q("vector", lambda v, q4=q4: v.tensor_copy(
                P3[:, q4 * 512:(q4 + 1) * 512], p_gb[:]), inc=ev)
            em.wait("tensor", ev)
            em.q("tensor", lambda t, q4=q4: t.matmul(
                p_gb[:], G["onesR65"][64:65, :], rowB[64:65, q4 * 512:(q4 + 1) * 512],
                start=True, stop=True), inc=mm)
            em.wait("vector", mm)
            em.q("vector", lambda v, q4=q4: v.tensor_copy(
                P4[:, q4 * 512:(q4 + 1) * 512], p_gb[:]), inc=ev)
            em.wait("tensor", ev)
        for h in range(H):
            em.q("tensor", lambda t, h=h: t.transpose(
                p_gb[:, 2 * h:2 * h + 1], rowB[0:1, h * 128:(h + 1) * 128],
                G["idS32"][0:1, 0:1]))
            em.q("tensor", lambda t, h=h: t.transpose(
                p_gb[:, 2 * h + 1:2 * h + 2], rowB[64:65, h * 128:(h + 1) * 128],
                G["idS32"][64:65, 64:65]), inc=(mm if h == H - 1 else None))
        em.wait("vector", mm)
        em.q("vector", lambda v: v.tensor_copy(gcolsA[:], p_gb[:, 0:32]), inc=ev)
        em.wait("tensor", ev)
        for hg in range(4):
            for hh in range(4):
                h = hg * 4 + hh
                em.q("tensor", lambda t, h=h, hh=hh: t.matmul(
                    p_c1[:, hh * 128:(hh + 1) * 128],
                    B4[:, h * 128:(h + 1) * 128], B2[:, h * 128:(h + 1) * 128],
                    start=True, stop=True), inc=(mm if hh == 3 else None))
            for hh in range(4):
                h = hg * 4 + hh
                em.q("tensor", lambda t, h=h, hh=hh: t.matmul(
                    p_c2[:, hh * 128:(hh + 1) * 128],
                    B2[:, h * 128:(h + 1) * 128], B5[:, h * 128:(h + 1) * 128],
                    start=True, stop=True), inc=(mm if hh == 3 else None))
            em.wait("vector", mm)
            sl = slice(hg * 512, (hg + 1) * 512)
            em.q("vector", lambda v, sl=sl: v.tensor_mul(
                SkR[:, sl], SkR[:, sl], P4[:, sl]))
            em.q("vector", lambda v, sl=sl: v.tensor_add(
                SkR[:, sl], SkR[:, sl], p_c1[:]))
            em.q("vector", lambda v, sl=sl: v.tensor_mul(
                SkR[:, sl], SkR[:, sl], P3[:, sl]))
            for hh in range(4):
                h = hg * 4 + hh
                em.q("vector", lambda v, h=h: v.tensor_scalar(
                    SvR[:, h * 128:(h + 1) * 128], SvR[:, h * 128:(h + 1) * 128],
                    gcolsA[:, 2 * h + 1:2 * h + 2], None, op0=ALU.mult))
            em.q("vector", lambda v, sl=sl: v.tensor_add(
                SvR[:, sl], SvR[:, sl], p_c2[:]))
            for hh in range(4):
                h = hg * 4 + hh
                em.q("vector", lambda v, h=h: v.tensor_scalar(
                    SvR[:, h * 128:(h + 1) * 128], SvR[:, h * 128:(h + 1) * 128],
                    gcolsA[:, 2 * h:2 * h + 1], None, op0=ALU.mult),
                    inc=(ev if hh == 3 else None))
            em.wait("tensor", ev)



def phaseX1(G):
    nc, em, io = G["nc"], G["em"], G["io"]
    ld, gw, mm, ev, sc, sg, cc = (G[k] for k in
                                  ("ld", "gw", "mm", "ev", "sc", "sg", "cc"))
    rowC, rowD = G["rowC"], G["rowD"]
    P1, P2, P3 = G["P1"], G["P2"], G["P3"]

    xctx = ExitStack()
    muS = xctx.enter_context(nc.sbuf_tensor("muS", [128, NCORE], F32))
    SkI = xctx.enter_context(nc.sbuf_tensor("SkI", [128, 2048], F32))
    SvI = xctx.enter_context(nc.sbuf_tensor("SvI", [128, 2048], F32))
    G["SkI"], G["SvI"] = SkI, SvI
    G["xctx"] = xctx
    x1ps = ExitStack()
    p_gb = x1ps.enter_context(nc.psum_tensor("pX_gb", [128, 512], F32))

    em.q("sync", lambda s: s.dma_start(out=muS[:], in_=io["mu"]), inc=ld, k=16)
    em.q("gpsimd", lambda g: g.collective_compute(
        "AllGather", ALU.bypass, replica_groups=[list(range(NCORE))],
        ins=[io["cc_in"].opt()], outs=[io["cc_out"].opt()]), inc=cc, k=1)
    em.wait("gpsimd", cc)
    em.q("vector", lambda v: v.memset(SkI[:], 0.0), inc=ev)
    em.q("vector", lambda v: v.memset(SvI[:], 0.0), inc=ev)
    for j in range(NCORE):
        base = j * 258
        em.q("gpsimd", lambda g, base=base: g.dma_start(
            out=P1[:], in_=io["cc_out"][base:base + 128, :]), inc=gw, k=16)
        em.q("gpsimd", lambda g, base=base: g.dma_start(
            out=P2[:], in_=io["cc_out"][base + 128:base + 256, :]), inc=gw, k=16)
        em.q("gpsimd", lambda g, base=base: g.dma_start(
            out=rowD[0:1, :], in_=io["cc_out"][base + 256:base + 257, :]),
            inc=gw, k=16)
        em.wait("vector", gw)
        em.wait("vector", ld)
        em.q("vector", lambda v, j=j: v.tensor_scalar(
            rowD[64:65, :], rowD[0:1, :], 1.0, muS[0:1, j:j + 1],
            op0=ALU.subtract, op1=ALU.mult), inc=ev)
        em.wait("vector", ev)
        em.q("vector", lambda v: v.tensor_scalar(
            rowD[64:65, :], rowD[64:65, :], 1.0, None, op0=ALU.add), inc=ev)
        em.wait("tensor", ev)
        for q4 in range(4):
            em.q("tensor", lambda t, q4=q4: t.matmul(
                p_gb[:], G["onesR65"][64:65, :], rowD[64:65, q4 * 512:(q4 + 1) * 512],
                start=True, stop=True), inc=mm)
            em.wait("vector", mm)
            em.q("vector", lambda v, q4=q4: v.tensor_copy(
                P3[:, q4 * 512:(q4 + 1) * 512], p_gb[:]), inc=ev)
            em.wait("tensor", ev)
        em.q("vector", lambda v: v.tensor_mul(SkI[:], SkI[:], P3[:]))
        em.q("vector", lambda v, j=j: v.scalar_tensor_tensor(
            P1[:], P1[:], muS[:, j:j + 1], SkI[:], op0=ALU.mult, op1=ALU.add))
        em.q("vector", lambda v: v.tensor_copy(SkI[:], P1[:]))
        em.q("vector", lambda v: v.tensor_mul(SvI[:], SvI[:], P3[:]))
        em.q("vector", lambda v, j=j: v.scalar_tensor_tensor(
            P2[:], P2[:], muS[:, j:j + 1], SvI[:], op0=ALU.mult, op1=ALU.add))
        em.q("vector", lambda v: v.tensor_copy(SvI[:], P2[:]), inc=ev)
        em.wait("gpsimd", ev)
    em.barrier("endX1")
    x1ps.close()



def phaseC(G):
    nc, em, io = G["nc"], G["em"], G["io"]
    ld, gw, mm, ev, sc, sg = (G[k] for k in ("ld", "gw", "mm", "ev", "sc", "sg"))
    P1, P2, P3, P4 = G["P1"], G["P2"], G["P3"], G["P4"]
    B1, B2, B3, B4, B5 = (G[k] for k in ("B1", "B2", "B3", "B4", "B5"))
    rowB, colA, colB = G["rowB"], G["colA"], G["colB"]
    SkI, SvI = G["SkI"], G["SvI"]

    cctx = ExitStack()
    csb = lambda n, s, d=F32: cctx.enter_context(nc.sbuf_tensor(n, list(s), d))
    B6 = csb("B6c", (128, 2048), BF16)
    B7 = csb("B7c", (128, 2048), BF16)
    cps = lambda n, s, d=F32: cctx.enter_context(nc.psum_tensor(n, list(s), d))
    WoS = csb("WoS", (128, KT * KT * 128), BF16)
    attnC = csb("attnC", (128, 2048), BF16)
    SkP16 = csb("SkP16", (128, 2048), BF16)
    SvP16 = csb("SvP16", (128, 2048), BF16)
    esx = csb("esC", (128, 128))
    aA16 = csb("aA16", (128, 128), BF16)
    aAT16 = csb("aAT16", (128, 128), BF16)
    UT16 = csb("UT16", (128, 128), BF16)
    Pm16 = csb("Pm16", (128, 128), BF16)
    W2m16 = csb("W2m16", (128, 128), BF16)
    gcols = csb("gcolsC", (128, 32))
    nwS = csb("nwS", (128, 1))

    p_P = cps("pC_P", (128, 128))
    p_s = cps("pC_s", (128, 128))
    p_o = cps("pC_o", (128, 128))
    p_t16 = cps("pC_t16", (128, 256), BF16)
    p_to = cps("pC_to", (128, 1024), BF16)
    p_c1 = cps("pC_c1", (128, 512))
    p_c2 = cps("pC_c2", (128, 512))
    p_gb = cps("pC_gb", (128, 512))

    em.q("gpsimd", lambda g: g.dma_start(
        out=WoS[:].rearrange("p (a b c) -> p a b c", a=KT, b=KT),
        in_=io["Wo"]), inc=gw, k=16)
    em.q("sync", lambda s: s.dma_start(out=nwS[:], in_=io["nw"]), inc=ld, k=16)
    em.wait("vector", gw)
    em.wait("vector", ld)
    em.q("vector", lambda v: v.tensor_scalar(
        WoS[:], WoS[:], nwS[:, 0:1], None, op0=ALU.mult), inc=ev)

    for j in range(NCH):
        em.wait("sync", mm)
        em.wait("sync", ev)
        em.wait("sync", sc)
        em.q("sync", lambda s, j=j: s.dma_start(out=B1[:], in_=io["st_q"][j]),
             inc=ld, k=16)
        em.q("sync", lambda s, j=j: s.dma_start(out=B2[:], in_=io["st_k"][j]),
             inc=ld, k=16)
        em.q("sync", lambda s, j=j: s.dma_start(out=B3[:], in_=io["st_Kt"][j]),
             inc=ld, k=16)
        em.q("sync", lambda s, j=j: s.dma_start(out=B4[:], in_=io["st_Vt"][j]),
             inc=ld, k=16)
        em.q("sync", lambda s, j=j: s.dma_start(out=B5[:], in_=io["st_U"][j]),
             inc=ld, k=16)
        em.q("sync", lambda s, j=j: s.dma_start(out=B6[:], in_=io["st_A"][j]),
             inc=ld, k=16)
        em.q("sync", lambda s, j=j: s.dma_start(out=rowB[0:1, :],
                                                in_=io["st_rows"][j][0:1]),
             inc=ld, k=16)
        em.q("sync", lambda s, j=j: s.dma_start(out=rowB[64:65, :],
                                                in_=io["st_rows"][j][1:2]),
             inc=ld, k=16)
        em.q("sync", lambda s, j=j: s.dma_start(out=B7[:], in_=io["st_SIG"][j]),
             inc=ld, k=16)
        em.wait("tensor", ld)
        em.wait("vector", ld)
        for q4 in range(4):
            em.q("tensor", lambda t, q4=q4: t.matmul(
                p_gb[:], G["onesR"][:], rowB[0:1, q4 * 512:(q4 + 1) * 512],
                start=True, stop=True), inc=mm)
            em.wait("vector", mm)
            em.q("vector", lambda v, q4=q4: v.tensor_copy(
                P3[:, q4 * 512:(q4 + 1) * 512], p_gb[:]), inc=ev)
            em.wait("tensor", ev)
            em.q("tensor", lambda t, q4=q4: t.matmul(
                p_gb[:], G["onesR65"][64:65, :], rowB[64:65, q4 * 512:(q4 + 1) * 512],
                start=True, stop=True), inc=mm)
            em.wait("vector", mm)
            em.q("vector", lambda v, q4=q4: v.tensor_copy(
                P4[:, q4 * 512:(q4 + 1) * 512], p_gb[:]), inc=ev)
            em.wait("tensor", ev)
        for h in range(H):
            em.q("tensor", lambda t, h=h: t.transpose(
                p_s[:, 2 * h:2 * h + 1], rowB[0:1, h * 128:(h + 1) * 128],
                G["idS32"][0:1, 0:1]))
            em.q("tensor", lambda t, h=h: t.transpose(
                p_s[:, 2 * h + 1:2 * h + 2], rowB[64:65, h * 128:(h + 1) * 128],
                G["idS32"][64:65, 64:65]), inc=(mm if h == H - 1 else None))
        em.wait("vector", mm)
        em.q("vector", lambda v: v.tensor_copy(gcols[:], p_s[:, 0:32]), inc=ev)
        em.wait("tensor", ev)
        em.q("vector", lambda v: v.tensor_mul(SkI[:], SkI[:], P4[:]), inc=ev)
        for h in range(H):
            em.q("vector", lambda v, h=h: v.tensor_scalar(
                SvI[:, h * 128:(h + 1) * 128], SvI[:, h * 128:(h + 1) * 128],
                gcols[:, 2 * h + 1:2 * h + 2], None, op0=ALU.mult),
                inc=(ev if h == H - 1 else None))
        em.wait("vector", ev)
        em.q("vector", lambda v: v.tensor_copy(SkP16[:], SkI[:]), inc=ev)
        em.q("vector", lambda v: v.tensor_copy(SvP16[:], SvI[:]), inc=ev)
        em.wait("tensor", ev)
        for h in range(H):
            hs = slice(h * 128, (h + 1) * 128)
            em.q("tensor", lambda t, hs=hs: t.matmul(
                p_P[:], B2[:, hs], B1[:, hs], start=True, stop=True), inc=mm)
            em.wait("vector", mm)
            em.q("vector", lambda v: v.tensor_mul(Pm16[:], p_P[:], G["triuS"][:]),
                 inc=ev)
            em.wait("tensor", ev)
            em.q("tensor", lambda t, hs=hs: t.matmul(
                p_s[:], Pm16[:], B5[:, hs], start=True, stop=False))
            em.q("tensor", lambda t, hs=hs: t.matmul(
                p_s[:], B1[:, hs], SkP16[:, hs], start=False, stop=True), inc=mm)
            em.wait("vector", mm)
            em.q("vector", lambda v, hs=hs: v.tensor_mul(esx[:], p_s[:], B6[:, hs]),
                 inc=ev)
            em.wait("scalar", ev)
            em.q("scalar", lambda s: s.activation(
                esx[:], esx[:], AF.Exp, scale=SCALE, accum_out=colB[:, 32:33]),
                inc=sc)
            em.wait("vector", sc)
            em.q("vector", lambda v: v.reciprocal(colB[:, 33:34], colB[:, 32:33]),
                 inc=ev)
            em.wait("vector", ev)
            em.q("vector", lambda v, hs=hs: v.scalar_tensor_tensor(
                aA16[:], esx[:], colB[:, 33:34], B6[:, hs],
                op0=ALU.mult, op1=ALU.mult), inc=ev)
            em.wait("tensor", ev)
            em.q("tensor", lambda t: t.transpose(
                p_t16[:, 0:128], aA16[:], G["idS"][:]))
            em.q("tensor", lambda t, hs=hs: t.transpose(
                p_t16[:, 128:256], B5[:, hs], G["idS"][:]), inc=mm)
            em.wait("vector", mm)
            em.q("vector", lambda v: v.tensor_copy(aAT16[:], p_t16[:, 0:128]))
            em.q("vector", lambda v: v.tensor_copy(UT16[:], p_t16[:, 128:256]),
                 inc=ev)
            em.wait("tensor", ev)
            em.q("tensor", lambda t: t.matmul(
                p_P[:], UT16[:], aAT16[:], start=True, stop=True), inc=mm)
            em.wait("vector", mm)
            em.q("vector", lambda v: v.tensor_mul(W2m16[:], p_P[:], G["triuS"][:]),
                 inc=ev)
            em.wait("tensor", ev)
            em.q("tensor", lambda t, hs=hs: t.matmul(
                p_o[:], W2m16[:], B4[:, hs], start=True, stop=False))
            em.q("tensor", lambda t, hs=hs: t.matmul(
                p_o[:], aAT16[:], SvP16[:, hs], start=False, stop=True), inc=mm)
            em.wait("vector", mm)
            em.q("vector", lambda v, hs=hs: v.tensor_copy(P1[:, hs], p_o[:]),
                 inc=ev)
            em.wait("tensor", ev)
            hh = h % 4
            em.q("tensor", lambda t, hs=hs, hh=hh: t.matmul(
                p_c1[:, hh * 128:(hh + 1) * 128], B3[:, hs], B5[:, hs],
                start=True, stop=True))
            em.q("tensor", lambda t, hs=hs, hh=hh: t.matmul(
                p_c2[:, hh * 128:(hh + 1) * 128], B5[:, hs], B4[:, hs],
                start=True, stop=True), inc=mm)
            if hh == 3:
                hg = h // 4
                sl = slice(hg * 512, (hg + 1) * 512)
                em.wait("vector", mm)
                em.q("vector", lambda v, sl=sl: v.tensor_add(
                    SkI[:, sl], SkI[:, sl], p_c1[:]))
                em.q("vector", lambda v, sl=sl: v.tensor_mul(
                    SkI[:, sl], SkI[:, sl], P3[:, sl]))
                em.q("vector", lambda v, sl=sl: v.tensor_add(
                    SvI[:, sl], SvI[:, sl], p_c2[:]), inc=ev)
                for h2 in range(hg * 4, hg * 4 + 4):
                    em.q("vector", lambda v, h2=h2: v.tensor_scalar(
                        SvI[:, h2 * 128:(h2 + 1) * 128],
                        SvI[:, h2 * 128:(h2 + 1) * 128],
                        gcols[:, 2 * h2:2 * h2 + 1], None, op0=ALU.mult),
                        inc=(ev if h2 % 4 == 3 else None))
                em.wait("tensor", ev)
        em.wait("scalar", ev)
        em.q("scalar", lambda s: s.activation(P2[:], P1[:], AF.Square), inc=sc)
        em.wait("vector", sc)
        em.q("vector", lambda v: v.reduce_sum(
            colA[:, 0:16].rearrange("p (h o) -> p h o", o=1),
            P2[:].rearrange("p (h m) -> p h m", m=128), axis=AX.X), inc=ev)
        em.wait("scalar", ev)
        em.q("scalar", lambda s: s.activation(colA[:, 0:16], colA[:, 0:16],
                                              AF.Sqrt, scale=1.0 / 128,
                                              bias=1e-5), inc=sc)
        em.wait("vector", sc)
        em.q("vector", lambda v: v.reciprocal(colA[:, 0:16], colA[:, 0:16]),
             inc=ev)
        em.wait("vector", ev)
        for h in range(H):
            em.q("vector", lambda v, h=h: v.tensor_scalar(
                P1[:, h * 128:(h + 1) * 128], P1[:, h * 128:(h + 1) * 128],
                colA[:, h:h + 1], None, op0=ALU.mult),
                inc=(ev if h == H - 1 else None))
        em.wait("vector", ev)
        em.q("vector", lambda v: v.tensor_mul(B1[:], P1[:], B7[:]), inc=ev)
        em.wait("tensor", ev)
        for b in range(2):
            for k in range(8):
                kt = b * 8 + k
                em.q("tensor", lambda t, kt=kt, k=k: t.transpose(
                    p_to[:, k * 128:(k + 1) * 128],
                    B1[:, kt * 128:(kt + 1) * 128], G["idS"][:]),
                    inc=(mm if k == 7 else None))
            em.wait("vector", mm)
            em.q("vector", lambda v, b=b: v.tensor_copy(
                B2[:].rearrange("p (a w) -> p a w", a=KT)[:, b * 8:(b + 1) * 8, :],
                p_to[:].rearrange("p (a w) -> p a w", a=8)), inc=ev)
            em.wait("tensor", ev)
        Wo3 = WoS[:].rearrange("p (a b c) -> p a b c", a=KT, b=KT)
        for dt in range(KT):
            for kt in range(KT):
                em.q("tensor", lambda t, dt=dt, kt=kt: t.matmul(
                    p_o[:], Wo3[:, kt, dt, :], B2[:, kt * 128:(kt + 1) * 128],
                    start=(kt == 0), stop=(kt == KT - 1)),
                    inc=(mm if kt == KT - 1 else None))
            em.wait("vector", mm)
            em.q("vector", lambda v, dt=dt: v.tensor_copy(
                attnC[:, dt * 128:(dt + 1) * 128], p_o[:]), inc=ev)
            em.wait("tensor", ev)
        em.wait("sync", ev)
        em.wait("sync", sc)
        em.q("sync", lambda s, j=j: s.dma_start(
            out=P1[:, 0:2048], in_=io["x_strip"][j * 128:(j + 1) * 128, :]),
            inc=ld, k=16)
        em.wait("vector", ld)
        em.q("vector", lambda v: v.tensor_copy(B3[:], P1[:]), inc=ev)
        em.wait("tensor", ev)
        for b in range(2):
            for k in range(8):
                kt = b * 8 + k
                em.q("tensor", lambda t, kt=kt, k=k: t.transpose(
                    p_to[:, k * 128:(k + 1) * 128],
                    B3[:, kt * 128:(kt + 1) * 128], G["idS"][:]),
                    inc=(mm if k == 7 else None))
            em.wait("vector", mm)
            em.q("vector", lambda v, b=b: v.tensor_copy(
                B4[:].rearrange("p (a w) -> p a w", a=KT)[:, b * 8:(b + 1) * 8, :],
                p_to[:].rearrange("p (a w) -> p a w", a=8)), inc=ev)
            em.wait("tensor", ev)
        em.q("vector", lambda v: v.tensor_add(P2[:], B4[:], attnC[:]), inc=ev)
        em.wait("sync", ev)
        em.q("sync", lambda s, j=j: s.dma_start(
            out=io["sp_x1"][j].rearrange("a p w -> p a w"),
            in_=P2[:].rearrange("p (a w) -> p a w", a=KT)), inc=sg, k=16)
        em.wait("vector", sg)
    em.barrier("endC")
    cctx.close()
    G["xctx"].close()



def phaseF0(G):
    nc, em, io = G["nc"], G["em"], G["io"]
    ld, gw, mm, ev, sc, sg, cc = (G[k] for k in
                                  ("ld", "gw", "mm", "ev", "sc", "sg", "cc"))
    P1, P2, P3, P4 = G["P1"], G["P2"], G["P3"], G["P4"]
    B1, B2 = G["B1"], G["B2"]
    rowA, colA, colB = G["rowA"], G["colA"], G["colB"]
    rowC, rowD = G["rowC"], G["rowD"]

    fctx = ExitStack()
    fsb = lambda n, s, d=F32: fctx.enter_context(nc.sbuf_tensor(n, list(s), d))
    fps = lambda n, s, d=F32: fctx.enter_context(nc.psum_tensor(n, list(s), d))
    rtrS = fsb("rtrS", (128, KT * 3))
    wdnS = fsb("wdnS", (128, KT * 32), BF16)
    wupS = fsb("wupS", (32, KT * 128), BF16)
    l1sS = fsb("l1sS", (128, KT)); l1bS = fsb("l1bS", (128, KT))
    tdn = fsb("tdn", (32, 1024), BF16)
    p_stA = fps("pF_stA", (128, 512))
    p_stB = fps("pF_stB", (128, 512))
    p_sa = p_stA[0:1, :]; p_sb = p_stA[64:65, :]
    p_sc = p_stB[0:1, :]; p_sd = p_stB[64:65, :]
    p_r = fps("pF_r", (128, 24))
    p_dn = fps("pF_dn", (32, 1024))
    p_gb = fps("pF_gb", (128, 512))

    em.q("sync", lambda s: s.dma_start(
        out=rtrS[:].rearrange("p (a c) -> p a c", a=KT), in_=io["rtr"]),
        inc=ld, k=16)
    em.q("gpsimd", lambda g: g.dma_start(
        out=wdnS[:].rearrange("p (a c) -> p a c", a=KT), in_=io["wdn"]),
        inc=gw, k=16)
    em.q("gpsimd", lambda g: g.dma_start(
        out=wupS[:].rearrange("p (a c) -> p a c", a=KT), in_=io["wup"]),
        inc=gw, k=16)
    em.q("sync", lambda s: s.dma_start(out=l1sS[:], in_=io["l1s"]), inc=ld, k=16)
    em.q("sync", lambda s: s.dma_start(out=l1bS[:], in_=io["l1b"]), inc=ld, k=16)
    em.wait("tensor", ld)
    em.wait("tensor", gw)

    pr = G["pr"]
    B1F, B2F = G["B1"], G["B2"]
    for kt in range(KT):
        em.wait("sync", pr)
        em.wait("sync", sc)
        em.wait("sync", ev)
        em.q("sync", lambda s, kt=kt: s.dma_start(
            out=P1[:, 0:1024].rearrange("p (a w) -> p a w", a=NCH),
            in_=io["sp_x1"][:, kt].rearrange("a p w -> p a w")), inc=ld, k=16)
        em.wait("scalar", ld)
        em.wait("vector", ld)
        em.q("scalar", lambda s: s.activation(B2F[:, 0:1024], P1[:, 0:1024],
                                              AF.Square), inc=sc)
        em.q("vector", lambda v: v.tensor_copy(B1F[:, 0:1024], P1[:, 0:1024]),
             inc=ev)
        em.wait("tensor", sc)
        em.wait("tensor", ev)
        for q2, ps in ((0, p_sa), (1, p_sb)):
            em.q("tensor", lambda t, q2=q2, ps=ps, kt=kt: t.matmul(
                ps, G["onesC16"][:], B1F[:, q2 * 512:(q2 + 1) * 512],
                start=(kt == 0), stop=(kt == KT - 1)))
        for q2, ps in ((0, p_sc), (1, p_sd)):
            em.q("tensor", lambda t, q2=q2, ps=ps, kt=kt: t.matmul(
                ps, G["onesC16"][:], B2F[:, q2 * 512:(q2 + 1) * 512],
                start=(kt == 0), stop=(kt == KT - 1)),
                inc=((pr if kt < KT - 1 else mm) if q2 == 1 else None))
    em.wait("vector", mm)
    em.q("vector", lambda v: v.tensor_scalar(
        rowA[0:1, 0:512], p_sa, 1.0 / D, None, op0=ALU.mult))
    em.q("vector", lambda v: v.tensor_scalar(
        rowA[0:1, 512:1024], p_sb, 1.0 / D, None, op0=ALU.mult))
    em.q("vector", lambda v: v.tensor_scalar(
        rowC[0:1, 0:512], p_sc, 1.0 / D, None, op0=ALU.mult))
    em.q("vector", lambda v: v.tensor_scalar(
        rowC[0:1, 512:1024], p_sd, 1.0 / D, None, op0=ALU.mult), inc=ev)
    em.wait("vector", ev)
    em.q("vector", lambda v: v.tensor_mul(rowD[0:1, 0:1024],
                                          rowA[0:1, 0:1024], rowA[0:1, 0:1024]),
         inc=ev)
    em.wait("vector", ev)
    em.q("vector", lambda v: v.tensor_sub(rowC[0:1, 0:1024], rowC[0:1, 0:1024],
                                          rowD[0:1, 0:1024]), inc=ev)
    em.wait("scalar", ev)
    em.q("scalar", lambda s: s.activation(rowC[0:1, 0:1024], rowC[0:1, 0:1024],
                                          AF.Sqrt, bias=1e-5), inc=sc)
    em.wait("vector", sc)
    em.q("vector", lambda v: v.reciprocal(rowC[0:1, 0:1024], rowC[0:1, 0:1024]),
         inc=ev)
    em.wait("tensor", ev)
    for q2 in range(2):
        em.q("tensor", lambda t, q2=q2: t.matmul(
            p_gb[:], G["onesR"][:], rowA[0:1, q2 * 512:(q2 + 1) * 512],
            start=True, stop=True), inc=mm)
        em.wait("vector", mm)
        em.q("vector", lambda v, q2=q2: v.tensor_copy(
            P3[:, q2 * 512:(q2 + 1) * 512], p_gb[:]), inc=ev)
        em.wait("tensor", ev)
        em.q("tensor", lambda t, q2=q2: t.matmul(
            p_gb[:], G["onesR"][:], rowC[0:1, q2 * 512:(q2 + 1) * 512],
            start=True, stop=True), inc=mm)
        em.wait("vector", mm)
        em.q("vector", lambda v, q2=q2: v.tensor_copy(
            P4[:, q2 * 512:(q2 + 1) * 512], p_gb[:]), inc=ev)
        em.wait("tensor", ev)
    for kt in range(KT):
        em.wait("sync", ev)
        em.q("sync", lambda s, kt=kt: s.dma_start(
            out=P1[:, 0:1024].rearrange("p (a w) -> p a w", a=NCH),
            in_=io["sp_x1"][:, kt].rearrange("a p w -> p a w")), inc=ld, k=16)
        em.wait("vector", ld)
        em.wait("vector", pr)
        em.q("vector", lambda v: v.tensor_sub(P2[:, 0:1024], P1[:, 0:1024],
                                              P3[:, 0:1024]), inc=ev)
        em.wait("vector", ev)
        em.q("vector", lambda v: v.tensor_mul(P2[:, 0:1024], P2[:, 0:1024],
                                              P4[:, 0:1024]), inc=ev)
        em.wait("vector", ev)
        em.q("vector", lambda v: v.tensor_copy(B1[:, 0:1024], P2[:, 0:1024]),
             inc=ev)
        em.wait("vector", ev)
        em.q("vector", lambda v, kt=kt: v.tensor_scalar(
            B2[:, 0:1024], P2[:, 0:1024], l1sS[:, kt:kt + 1],
            l1bS[:, kt:kt + 1], op0=ALU.mult, op1=ALU.add), inc=ev)
        em.wait("tensor", ev)
        for tt in range(8):
            em.q("tensor", lambda t, tt=tt, kt=kt: t.matmul(
                p_r[:, tt * 3:(tt + 1) * 3], P2[:, tt * 128:(tt + 1) * 128],
                rtrS[:, kt * 3:(kt + 1) * 3],
                start=(kt == 0 and tt == 0),
                stop=(kt == KT - 1 and tt == 7), skip_group_check=True))
        for q2 in range(2):
            em.q("tensor", lambda t, q2=q2, kt=kt: t.matmul(
                p_dn[:, q2 * 512:(q2 + 1) * 512],
                wdnS[:, kt * 32:(kt + 1) * 32], B1[:, q2 * 512:(q2 + 1) * 512],
                start=(kt == 0), stop=(kt == KT - 1)),
                inc=((pr if kt < KT - 1 else mm) if q2 == 1 else None))
        em.wait("sync", ev)
        em.q("sync", lambda s, kt=kt: s.dma_start(
            out=io["sp_hs"][kt], in_=B1[:, 0:1024]), inc=sg, k=16)
        em.q("sync", lambda s, kt=kt: s.dma_start(
            out=io["cc_xn_in"][kt * 128:(kt + 1) * 128, :], in_=B2[:, 0:1024]),
            inc=sg, k=16)
        em.wait("vector", sg)
    em.wait("vector", mm)
    em.q("vector", lambda v: v.tensor_copy(colA[:, 0:24], p_r[:]), inc=ev)
    em.wait("vector", ev)
    lA3 = colA[:, 0:24].rearrange("p (a c) -> p a c", a=8)
    l0s, l1s, l2s = lA3[:, :, 0], lA3[:, :, 1], lA3[:, :, 2]
    ge01, ge02, ge12 = colA[:, 32:40], colA[:, 40:48], colA[:, 48:56]
    nd0 = colA[:, 56:64]
    cB3 = colB[:, 0:16].rearrange("p (a c) -> p a c", a=8)
    d0s, d1s = cB3[:, :, 0], cB3[:, :, 1]
    em.q("vector", lambda v: v.tensor_tensor(ge01, l0s, l1s, op=ALU.is_ge))
    em.q("vector", lambda v: v.tensor_tensor(ge02, l0s, l2s, op=ALU.is_ge))
    em.q("vector", lambda v: v.tensor_tensor(ge12, l1s, l2s, op=ALU.is_ge),
         inc=ev)
    em.wait("vector", ev)
    em.q("vector", lambda v: v.tensor_mul(d0s, ge01, ge02), inc=ev)
    em.wait("vector", ev)
    em.q("vector", lambda v: v.tensor_scalar(
        nd0, d0s, -1.0, 1.0, op0=ALU.mult, op1=ALU.add), inc=ev)
    em.wait("vector", ev)
    em.q("vector", lambda v: v.tensor_mul(d1s, ge12, nd0), inc=ev)
    em.wait("vector", ev)
    em.wait("vector", mm)
    em.q("vector", lambda v: v.tensor_copy(tdn[:], p_dn[:]), inc=ev)
    em.wait("tensor", ev)
    for dt in range(KT):
        for q2 in range(2):
            em.q("tensor", lambda t, dt=dt, q2=q2: t.matmul(
                p_gb[:], wupS[:, dt * 128:(dt + 1) * 128],
                tdn[:, q2 * 512:(q2 + 1) * 512],
                start=True, stop=True), inc=mm)
            em.wait("vector", mm)
            em.q("vector", lambda v, q2=q2: v.tensor_copy(
                B1[:, q2 * 512:(q2 + 1) * 512], p_gb[:]), inc=ev)
            em.wait("tensor", ev)
        em.wait("sync", ev)
        em.q("sync", lambda s, dt=dt: s.dma_start(
            out=io["sp_lr"][dt], in_=B1[:, 0:1024]), inc=sg, k=16)
        em.wait("vector", sg)
    em.wait("gpsimd", sg)
    em.q("gpsimd", lambda g: g.collective_compute(
        "AllGather", ALU.bypass, replica_groups=[list(range(NCORE))],
        ins=[io["cc_xn_in"].opt()], outs=[io["cc_xn_out"].opt()]), inc=cc, k=1)
    em.wait("gpsimd", cc)
    em.barrier("endF0")
    fctx.close()



def phaseF1(G):
    nc, em, io = G["nc"], G["em"], G["io"]
    ld, gw, mm, ev, sc, sg, cc = (G[k] for k in
                                  ("ld", "gw", "mm", "ev", "sc", "sg", "cc"))
    P1 = G["P1"]
    rowD = G["rowD"]

    fctx = ExitStack()
    fsb = lambda n, s, d=F32: fctx.enter_context(nc.sbuf_tensor(n, list(s), d))
    fps = lambda n, s, d=F32: fctx.enter_context(nc.psum_tensor(n, list(s), d))
    rz16 = fsb("rz16", (1, 512), BF16)
    dmS = fsb("dmS", (128, 2048), BF16)
    f1a = ExitStack()
    fsa = lambda n, s, d=F32: f1a.enter_context(nc.sbuf_tensor(n, list(s), d))
    xnp = fsa("xnp", (128, KT * 1024), BF16)
    wqS = fsa("wqS", (128, KT * 256), BF16)
    wkS = fsa("wkS", (128, KT * 256), BF16)
    wvS = fsa("wvS", (128, KT * 256), BF16)
    evq = fsa("evq", (128, 1024), BF16)

    p_q0 = fps("pG_q0", (128, 512)); p_q1 = fps("pG_q1", (128, 512))
    p_S0 = fps("pG_S0", (128, 512)); p_S1 = fps("pG_S1", (128, 512))
    p_O = fps("pG_O", (128, 512))
    p_Z = fps("pG_Z", (1, 512))
    p_bc = fps("pG_bc", (128, 512))
    p_tv = fps("pG_tv", (128, 1024), BF16)

    em.q("sync", lambda s: s.dma_start(out=dmS[:], in_=G["cst"]["dmask"]),
         inc=ld, k=16)
    for wS, wn in ((wqS, "fwq"), (wkS, "fwk"), (wvS, "fwv")):
        em.q("gpsimd", lambda g, wS=wS, wn=wn: g.dma_start(
            out=wS[:].rearrange("p (a b c) -> p a b c", a=KT, b=2),
            in_=io[wn]), inc=gw, k=16)
    em.wait("tensor", gw)
    em.wait("tensor", ld)

    wq3 = wqS[:].rearrange("p (a b c) -> p a b c", a=KT, b=2)
    wk3 = wkS[:].rearrange("p (a b c) -> p a b c", a=KT, b=2)
    wv3 = wvS[:].rearrange("p (a b c) -> p a b c", a=KT, b=2)
    for piece in range(NCORE):
        em.wait("sync", mm)
        for kt in range(KT):
            em.q("sync", lambda s, piece=piece, kt=kt: s.dma_start(
                out=xnp[:, kt * 1024:(kt + 1) * 1024],
                in_=io["cc_xn_out"][piece * D + kt * 128:
                                    piece * D + (kt + 1) * 128, :]),
                inc=ld, k=16)
        em.wait("tensor", ld)
        for w3, stn in ((wq3, "st_ffq"), (wk3, "st_ffk"), (wv3, "st_ffv")):
            for cb in range(2):
                for q2, ps in ((0, p_q0), (1, p_q1)):
                    for kt in range(KT):
                        em.q("tensor", lambda t, kt=kt, cb=cb, q2=q2, ps=ps,
                             w3=w3: t.matmul(
                                 ps[:], w3[:, kt, cb, :],
                                 xnp[:, kt * 1024 + q2 * 512:
                                     kt * 1024 + (q2 + 1) * 512],
                                 start=(kt == 0), stop=(kt == KT - 1)),
                             inc=(mm if kt == KT - 1 else None))
                em.wait("vector", mm)
                em.q("vector", lambda v: v.tensor_copy(evq[:, 0:512], p_q0[:]))
                em.q("vector", lambda v: v.tensor_copy(evq[:, 512:1024], p_q1[:]),
                     inc=ev)
                em.wait("sync", ev)
                em.q("sync", lambda s, stn=stn, cb=cb, piece=piece: s.dma_start(
                    out=io[stn][cb][:, piece * 1024:(piece + 1) * 1024],
                    in_=evq[:]), inc=sg, k=16)
                em.wait("vector", sg)
                em.wait("tensor", sg)
    em.barrier("endF1proj")
    f1a.close()
    f1b = ExitStack()
    fsbb = lambda n, s, d=F32: f1b.enter_context(nc.sbuf_tensor(n, list(s), d))
    QS = fsbb("QS", (128, T), BF16)
    KS = fsbb("KS", (128, T), BF16)
    VS = fsbb("VS", (128, T), BF16)
    VtS = fsbb("VtS", (128, T), BF16)
    Pt16 = fsbb("Pt16", (128, 2 * 512), BF16)
    O16 = fsbb("O16", (128, 512), BF16)
    for hh in range(2):
        em.wait("sync", sg)
        em.wait("sync", mm)
        em.wait("sync", sc)
        em.wait("sync", ev)
        em.q("sync", lambda s, hh=hh: s.dma_start(out=QS[:], in_=io["st_ffq"][hh]),
             inc=ld, k=16)
        em.q("sync", lambda s, hh=hh: s.dma_start(out=KS[:], in_=io["st_ffk"][hh]),
             inc=ld, k=16)
        em.q("sync", lambda s, hh=hh: s.dma_start(out=VS[:], in_=io["st_ffv"][hh]),
             inc=ld, k=16)
        em.wait("tensor", ld)
        for b8 in range(8):
            for k in range(8):
                kv = b8 * 8 + k
                em.q("tensor", lambda t, kv=kv, k=k: t.transpose(
                    p_tv[:, k * 128:(k + 1) * 128],
                    VS[:, kv * 128:(kv + 1) * 128], G["idS"][:]),
                    inc=(mm if k == 7 else None))
            em.wait("vector", mm)
            em.q("vector", lambda v, b8=b8: v.tensor_copy(
                VtS[:, b8 * 1024:(b8 + 1) * 1024],
                p_tv[:].rearrange("p (a w) -> p a w", a=8).rearrange(
                    "p a w -> p (a w)")), inc=ev)
            em.wait("tensor", ev)
        for qb in range(16):
            nkv = 4 * qb + 4
            pr = G["pr"]
            for kv in range(nkv):
                ps = p_S0 if kv % 2 == 0 else p_S1
                pb = 0 if kv % 2 == 0 else 1
                em.q("tensor", lambda t, kv=kv, qb=qb, ps=ps: t.matmul(
                    ps[:], KS[:, kv * 128:(kv + 1) * 128],
                    QS[:, qb * 512:(qb + 1) * 512], start=True, stop=True),
                    inc=mm)
                em.wait("scalar", mm)
                if kv >= 2:
                    em.wait("scalar", pr, pr.n - 1)
                em.q("scalar", lambda s, ps=ps, pb=pb: s.activation(
                    Pt16[:, pb * 512:(pb + 1) * 512], ps[:], AF.Exp,
                    scale=SCALE, bias=-C0F), inc=sc)
                em.wait("tensor", sc)
                if kv >= 4 * qb:
                    dj = kv - 4 * qb
                    em.wait("vector", sc)
                    em.q("vector", lambda v, pb=pb, dj=dj: v.tensor_mul(
                        Pt16[:, pb * 512:(pb + 1) * 512],
                        Pt16[:, pb * 512:(pb + 1) * 512],
                        dmS[:, dj * 512:(dj + 1) * 512]), inc=ev)
                    em.wait("tensor", ev)
                em.q("tensor", lambda t, kv=kv, pb=pb, nkv=nkv: t.matmul(
                    p_O[:], VtS[:, kv * 128:(kv + 1) * 128],
                    Pt16[:, pb * 512:(pb + 1) * 512],
                    start=(kv == 0), stop=(kv == nkv - 1)))
                em.q("tensor", lambda t, kv=kv, pb=pb, nkv=nkv: t.matmul(
                    p_Z[:], G["onesC16"][:],
                    Pt16[:, pb * 512:(pb + 1) * 512],
                    start=(kv == 0), stop=(kv == nkv - 1)),
                    inc=[(mm, 1)] if kv == nkv - 1 else [(pr, 1)])
            em.wait("vector", mm)
            em.q("vector", lambda v: v.reciprocal(rowD[0:1, 0:512], p_Z[:]),
                 inc=ev)
            em.wait("vector", ev)
            em.q("vector", lambda v: v.tensor_copy(rz16[:], rowD[0:1, 0:512]),
                 inc=ev)
            em.wait("tensor", ev)
            em.q("tensor", lambda t: t.matmul(
                p_bc[:], G["onesR16"][:], rz16[:], start=True, stop=True),
                inc=mm)
            em.q("vector", lambda v: v.tensor_copy(P1[:, 0:512], p_O[:]), inc=ev)
            em.wait("vector", mm)
            em.wait("vector", ev)
            em.q("vector", lambda v: v.tensor_mul(O16[:], P1[:, 0:512], p_bc[:]),
                 inc=ev)
            em.wait("sync", ev)
            qb2 = qb // 2
            em.q("sync", lambda s, hh=hh, qb=qb, qb2=qb2: s.dma_start(
                out=io["cc_ctx_in"][qb2 * 256 + hh * 128:
                                    qb2 * 256 + (hh + 1) * 128,
                                    (qb % 2) * 512:(qb % 2) * 512 + 512],
                in_=O16[:]), inc=sg, k=16)
            em.wait("vector", sg)
            em.wait("tensor", sg)
    em.wait("gpsimd", sg)
    em.q("gpsimd", lambda g: g.collective_compute(
        "AllGather", ALU.bypass, replica_groups=[list(range(NCORE))],
        ins=[io["cc_ctx_in"].opt()], outs=[io["cc_ctx_out"].opt()]), inc=cc, k=1)
    em.wait("gpsimd", cc)
    em.barrier("endF1")
    f1b.close()
    fctx.close()



def phaseF2(G):
    nc, em, io = G["nc"], G["em"], G["io"]
    ld, gw, mm, ev, sc, sg = (G[k] for k in ("ld", "gw", "mm", "ev", "sc", "sg"))
    P1, P2, P3, P4 = G["P1"], G["P2"], G["P3"], G["P4"]
    B1, B2, B3 = G["B1"], G["B2"], G["B3"]
    rowA, colB = G["rowA"], G["colB"]
    colA = G["colA"]
    rowC, rowD = G["rowC"], G["rowD"]

    fctx = ExitStack()
    fsb = lambda n, s, d=F32: fctx.enter_context(nc.sbuf_tensor(n, list(s), d))
    fps = lambda n, s, d=F32: fctx.enter_context(nc.psum_tensor(n, list(s), d))
    l2sS = fsb("l2sS", (128, KT)); l2bS = fsb("l2bS", (128, KT))
    b0S = fsb("b0S", (128, KT))
    D0bc = fsb("D0bc", (128, 1024)); D1bc = fsb("D1bc", (128, 1024))
    cidxS = fsb("cidxS", (128, KT), U32)
    f2a = ExitStack()
    ctxS = f2a.enter_context(nc.sbuf_tensor("ctxS", [128, KT * 1024], BF16))
    woSl = f2a.enter_context(nc.sbuf_tensor("woSl", [128, 2 * 2048], BF16))

    p_w0 = fps("pH_w0", (128, 512)); p_w1 = fps("pH_w1", (128, 512))
    p_stA = fps("pH_stA", (128, 512))
    p_stB = fps("pH_stB", (128, 512))
    p_sa = p_stA[0:1, :]; p_sb = p_stA[64:65, :]
    p_sc2 = p_stB[0:1, :]; p_sd = p_stB[64:65, :]
    p_gb = fps("pH_gb", (128, 512))
    p_tr = fps("pH_tr", (128, 128))
    p_trb = fps("pH_trb", (128, 1024), BF16)

    em.q("sync", lambda s: s.dma_start(out=l2sS[:], in_=io["l2s"]), inc=ld, k=16)
    em.q("sync", lambda s: s.dma_start(out=l2bS[:], in_=io["l2b"]), inc=ld, k=16)
    em.q("sync", lambda s: s.dma_start(out=b0S[:], in_=io["b0"]), inc=ld, k=16)
    em.q("sync", lambda s: s.dma_start(out=cidxS[:], in_=io["ctxidx"]),
         inc=ld, k=16)
    em.wait("gpsimd", ld)
    for kt in range(KT):
        em.q("gpsimd", lambda g, kt=kt: g.indirect_dma_start(
            out=ctxS[:, kt * 1024:(kt + 1) * 1024], out_offset=None,
            in_=io["cc_ctx_out"],
            in_offset=IndirectOffsetOnAxis(cidxS[:, kt:kt + 1], 0)),
            inc=gw, k=16)
    em.wait("tensor", gw)
    em.wait("tensor", ld)
    wo_marks = []
    for dt in range(KT):
        if dt >= 2:
            em.wait("gpsimd", mm, wo_marks[dt - 2])
        em.q("gpsimd", lambda g, dt=dt: g.dma_start(
            out=woSl[:, (dt % 2) * 2048:(dt % 2) * 2048 + 2048].rearrange(
                "p (a c) -> p a c", a=KT),
            in_=io["fwo"][dt]), inc=gw, k=16)
        em.wait("sync", ev)
        em.wait("sync", sc)
        em.q("sync", lambda s, dt=dt: s.dma_start(
            out=B1[:, 0:1024], in_=io["sp_hs"][dt]), inc=ld, k=16)
        em.wait("tensor", gw)
        em.wait("tensor", ld)
        wsl = woSl[:, (dt % 2) * 2048:(dt % 2) * 2048 + 2048]
        for q2, ps in ((0, p_w0), (1, p_w1)):
            for kt in range(KT):
                em.q("tensor", lambda t, kt=kt, q2=q2, ps=ps, wsl=wsl: t.matmul(
                    ps[:], wsl[:, kt * 128:(kt + 1) * 128],
                    ctxS[:, kt * 1024 + q2 * 512:kt * 1024 + (q2 + 1) * 512],
                    start=(kt == 0), stop=(kt == KT - 1)),
                    inc=(mm if kt == KT - 1 else None))
        em.wait("vector", mm)
        em.wait("vector", G["pr"])
        em.q("vector", lambda v: v.tensor_add(P1[:, 0:512], B1[:, 0:512],
                                              p_w0[:]))
        em.q("vector", lambda v: v.tensor_add(P1[:, 512:1024], B1[:, 512:1024],
                                              p_w1[:]), inc=ev)
        em.wait("tensor", ev)
        em.wait("scalar", ev)
        em.q("scalar", lambda s: s.activation(B3[:, 0:1024], P1[:, 0:1024],
                                              AF.Square), inc=sc)
        em.q("vector", lambda v: v.tensor_copy(B2[:, 0:1024], P1[:, 0:1024]),
             inc=ev)
        em.wait("sync", ev)
        em.q("sync", lambda s, dt=dt: s.dma_start(
            out=io["sp_xt"][dt], in_=B2[:, 0:1024]), inc=sg, k=16)
        em.wait("tensor", sc)
        em.wait("tensor", ev)
        for q2, ps in ((0, p_sa), (1, p_sb)):
            em.q("tensor", lambda t, q2=q2, ps=ps, dt=dt: t.matmul(
                ps[:], G["onesC16"][:], B2[:, q2 * 512:(q2 + 1) * 512],
                start=(dt == 0), stop=(dt == KT - 1)))
        for q2, ps in ((0, p_sc2), (1, p_sd)):
            em.q("tensor", lambda t, q2=q2, ps=ps, dt=dt: t.matmul(
                ps[:], G["onesC16"][:], B3[:, q2 * 512:(q2 + 1) * 512],
                start=(dt == 0), stop=(dt == KT - 1)),
                inc=(mm if (dt == KT - 1 and q2 == 1) else None))
        wo_marks.append(mm.n)
        em.wait("vector", sg)
    em.wait("vector", mm)
    em.q("vector", lambda v: v.tensor_scalar(
        rowA[0:1, 0:512], p_sa, 1.0 / D, None, op0=ALU.mult))
    em.q("vector", lambda v: v.tensor_scalar(
        rowA[0:1, 512:1024], p_sb, 1.0 / D, None, op0=ALU.mult))
    em.q("vector", lambda v: v.tensor_scalar(
        rowC[0:1, 0:512], p_sc2, 1.0 / D, None, op0=ALU.mult))
    em.q("vector", lambda v: v.tensor_scalar(
        rowC[0:1, 512:1024], p_sd, 1.0 / D, None, op0=ALU.mult), inc=ev)
    em.wait("vector", ev)
    em.q("vector", lambda v: v.tensor_mul(rowD[0:1, 0:1024],
                                          rowA[0:1, 0:1024], rowA[0:1, 0:1024]),
         inc=ev)
    em.wait("vector", ev)
    em.q("vector", lambda v: v.tensor_sub(rowC[0:1, 0:1024], rowC[0:1, 0:1024],
                                          rowD[0:1, 0:1024]), inc=ev)
    em.wait("scalar", ev)
    em.q("scalar", lambda s: s.activation(rowC[0:1, 0:1024], rowC[0:1, 0:1024],
                                          AF.Sqrt, bias=1e-5), inc=sc)
    em.wait("vector", sc)
    em.q("vector", lambda v: v.reciprocal(rowC[0:1, 0:1024], rowC[0:1, 0:1024]),
         inc=ev)
    em.wait("tensor", ev)
    for q2 in range(2):
        em.q("tensor", lambda t, q2=q2: t.matmul(
            p_gb[:], G["onesR"][:], rowA[0:1, q2 * 512:(q2 + 1) * 512],
            start=True, stop=True), inc=mm)
        em.wait("vector", mm)
        em.q("vector", lambda v, q2=q2: v.tensor_copy(
            P3[:, q2 * 512:(q2 + 1) * 512], p_gb[:]), inc=ev)
        em.wait("tensor", ev)
        em.q("tensor", lambda t, q2=q2: t.matmul(
            p_gb[:], G["onesR"][:], rowC[0:1, q2 * 512:(q2 + 1) * 512],
            start=True, stop=True), inc=mm)
        em.wait("vector", mm)
        em.q("vector", lambda v, q2=q2: v.tensor_copy(
            P4[:, q2 * 512:(q2 + 1) * 512], p_gb[:]), inc=ev)
        em.wait("tensor", ev)
    em.barrier("endF2wo")
    f2a.close()
    f2b = ExitStack()
    xn2S = f2b.enter_context(nc.sbuf_tensor("xn2S", [128, KT * 1024], BF16))
    aS = f2b.enter_context(nc.sbuf_tensor("aS", [128, KT * 1024], BF16))
    ACC = f2b.enter_context(nc.sbuf_tensor("ACC", [128, KT * 1024], BF16))
    f1Sl = f2b.enter_context(nc.sbuf_tensor("f1Sl", [128, 2 * 2048], BF16))
    f2Sl = f2b.enter_context(nc.sbuf_tensor("f2Sl", [128, 2 * 2048], BF16))
    for kt in range(KT):
        em.wait("sync", ev)
        em.q("sync", lambda s, kt=kt: s.dma_start(
            out=B1[:, 0:1024], in_=io["sp_xt"][kt]), inc=ld, k=16)
        em.wait("vector", ld)
        em.q("vector", lambda v: v.tensor_sub(P1[:, 0:1024], B1[:, 0:1024],
                                              P3[:, 0:1024]), inc=ev)
        em.wait("vector", ev)
        em.q("vector", lambda v: v.tensor_mul(P1[:, 0:1024], P1[:, 0:1024],
                                              P4[:, 0:1024]), inc=ev)
        em.wait("vector", ev)
        em.q("vector", lambda v, kt=kt: v.tensor_scalar(
            xn2S[:, kt * 1024:(kt + 1) * 1024], P1[:, 0:1024],
            l2sS[:, kt:kt + 1], l2bS[:, kt:kt + 1],
            op0=ALU.mult, op1=ALU.add), inc=ev)
    em.wait("tensor", ev)
    for tt in range(8):
        em.q("tensor", lambda t, tt=tt: t.transpose(
            p_tr[0:1, :], colB[:, 2 * tt:2 * tt + 1], G["idS32"][:]), inc=mm)
        em.wait("vector", mm)
        em.q("vector", lambda v: v.tensor_copy(rowA[0:1, 1024:1152],
                                               p_tr[0:1, 0:128]), inc=ev)
        em.wait("tensor", ev)
        em.q("tensor", lambda t, tt=tt: t.transpose(
            p_tr[0:1, :], colB[:, 2 * tt + 1:2 * tt + 2], G["idS32"][:]), inc=mm)
        em.wait("vector", mm)
        em.q("vector", lambda v: v.tensor_copy(rowC[0:1, 1024:1152],
                                               p_tr[0:1, 0:128]), inc=ev)
        em.wait("tensor", ev)
        em.q("tensor", lambda t, tt=tt: t.matmul(
            p_gb[:, 0:128], G["onesR"][:], rowA[0:1, 1024:1152],
            start=True, stop=True), inc=mm)
        em.wait("vector", mm)
        em.q("vector", lambda v, tt=tt: v.tensor_copy(
            D0bc[:, tt * 128:(tt + 1) * 128], p_gb[:, 0:128]), inc=ev)
        em.wait("tensor", ev)
        em.q("tensor", lambda t, tt=tt: t.matmul(
            p_gb[:, 0:128], G["onesR"][:], rowC[0:1, 1024:1152],
            start=True, stop=True), inc=mm)
        em.wait("vector", mm)
        em.q("vector", lambda v, tt=tt: v.tensor_copy(
            D1bc[:, tt * 128:(tt + 1) * 128], p_gb[:, 0:128]), inc=ev)
        em.wait("tensor", ev)
    f1_marks = []
    for fb in range(4):
        for ft in range(16):
            gft = fb * 16 + ft
            if len(f1_marks) >= 2:
                em.wait("gpsimd", mm, f1_marks[-2])
            em.q("gpsimd", lambda g, gft=gft, ft=ft: g.dma_start(
                out=f1Sl[:, (ft % 2) * 2048:(ft % 2) * 2048 + 2048].rearrange(
                    "p (a c) -> p a c", a=KT),
                in_=io["ff1"][gft]), inc=gw, k=16)
            em.wait("tensor", gw)
            fsl = f1Sl[:, (ft % 2) * 2048:(ft % 2) * 2048 + 2048]
            for q2, ps in ((0, p_w0), (1, p_w1)):
                for kt in range(KT):
                    em.q("tensor", lambda t, kt=kt, q2=q2, ps=ps, fsl=fsl:
                         t.matmul(
                             ps[:], fsl[:, kt * 128:(kt + 1) * 128],
                             xn2S[:, kt * 1024 + q2 * 512:
                                  kt * 1024 + (q2 + 1) * 512],
                             start=(kt == 0), stop=(kt == KT - 1)),
                         inc=(mm if kt == KT - 1 else None))
                em.wait("scalar", mm)
                em.q("scalar", lambda s, ps=ps: s.activation(
                    P1[:, 0:512], ps[:], AF.Square), inc=sc)
                em.wait("vector", sc)
                em.q("vector", lambda v: v.tensor_scalar(
                    P1[:, 0:512], P1[:, 0:512], GB, GA,
                    op0=ALU.mult, op1=ALU.add), inc=ev)
                em.wait("vector", ev)
                em.q("vector", lambda v, ps=ps: v.tensor_mul(
                    P2[:, 0:512], P1[:, 0:512], ps[:]), inc=ev)
                em.wait("scalar", ev)
                em.q("scalar", lambda s: s.activation(
                    P2[:, 0:512], P2[:, 0:512], AF.Sigmoid), inc=sc)
                em.wait("vector", sc)
                em.q("vector", lambda v, ps=ps, ft=ft, q2=q2: v.tensor_mul(
                    aS[:, ft * 1024 + q2 * 512:ft * 1024 + (q2 + 1) * 512],
                    P2[:, 0:512], ps[:]), inc=ev)
                em.wait("tensor", ev)
            f1_marks.append(mm.n)
        for dt in range(KT):
            em.wait("gpsimd", mm)
            em.q("gpsimd", lambda g, dt=dt, fb=fb: g.dma_start(
                out=f2Sl[:, (dt % 2) * 2048:(dt % 2) * 2048 + 2048].rearrange(
                    "p (a c) -> p a c", a=KT),
                in_=io["ff2"][dt, fb]), inc=gw, k=16)
            em.wait("tensor", gw)
            fsl = f2Sl[:, (dt % 2) * 2048:(dt % 2) * 2048 + 2048]
            for q2, ps in ((0, p_w0), (1, p_w1)):
                for fk in range(16):
                    em.q("tensor", lambda t, fk=fk, q2=q2, ps=ps, fsl=fsl:
                         t.matmul(
                             ps[:], fsl[:, fk * 128:(fk + 1) * 128],
                             aS[:, fk * 1024 + q2 * 512:
                                fk * 1024 + (q2 + 1) * 512],
                             start=(fk == 0), stop=(fk == KT - 1)),
                         inc=(mm if fk == KT - 1 else None))
            em.wait("vector", mm)
            if fb == 0:
                em.q("vector", lambda v, dt=dt: v.tensor_copy(
                    ACC[:, dt * 1024:dt * 1024 + 512], p_w0[:]))
                em.q("vector", lambda v, dt=dt: v.tensor_copy(
                    ACC[:, dt * 1024 + 512:dt * 1024 + 1024], p_w1[:]), inc=ev)
            else:
                em.q("vector", lambda v, dt=dt: v.tensor_add(
                    ACC[:, dt * 1024:dt * 1024 + 512],
                    ACC[:, dt * 1024:dt * 1024 + 512], p_w0[:]))
                em.q("vector", lambda v, dt=dt: v.tensor_add(
                    ACC[:, dt * 1024 + 512:dt * 1024 + 1024],
                    ACC[:, dt * 1024 + 512:dt * 1024 + 1024], p_w1[:]), inc=ev)
            em.wait("tensor", ev)
    if DBGRAW:
        em.wait("sync", ev)
        em.q("sync", lambda s: s.dma_start(
            out=io["st_raw"][:, 0:KT * 1024], in_=ACC[:]), inc=sg, k=16)
        em.q("sync", lambda s: s.dma_start(
            out=io["st_dbg3"].rearrange("a p c -> p (a c)")
            if False else io["st_dbg3"].rearrange("a p c -> p a c"),
            in_=xn2S[:].rearrange("p (a c) -> p a c", a=NCH * 2)
            if False else xn2S[:].rearrange("p (a c) -> p a c", a=NCH)),
            inc=sg, k=16)
        em.q("sync", lambda s: s.dma_start(
            out=io["st_dbg4"].rearrange("a p c -> p a c"),
            in_=aS[:].rearrange("p (a c) -> p a c", a=NCH)), inc=sg, k=16)
        em.q("sync", lambda s: s.dma_start(out=io["st_dbg2"][:, 0:32],
                                           in_=colA[:, 0:32]), inc=sg, k=16)
        em.q("sync", lambda s: s.dma_start(out=io["st_dbg2"][:, 32:64],
                                           in_=colB[:, 0:32]), inc=sg, k=16)
    for dt in range(KT):
        em.wait("sync", ev)
        em.q("sync", lambda s, dt=dt: s.dma_start(
            out=B1[:, 0:1024], in_=io["sp_xt"][dt]), inc=ld, k=16)
        em.q("sync", lambda s, dt=dt: s.dma_start(
            out=B2[:, 0:1024], in_=io["sp_hs"][dt]), inc=ld, k=16)
        em.q("sync", lambda s, dt=dt: s.dma_start(
            out=B3[:, 0:1024], in_=io["sp_lr"][dt]), inc=ld, k=16)
        em.q("sync", lambda s, dt=dt: s.dma_start(
            out=P4[:, 0:1024].rearrange("p (a w) -> p a w", a=NCH),
            in_=io["sp_x1"][:, dt].rearrange("a p w -> p a w")), inc=ld, k=16)
        em.wait("vector", ld)
        em.q("vector", lambda v, dt=dt: v.tensor_add(
            P1[:, 0:1024], B1[:, 0:1024], ACC[:, dt * 1024:(dt + 1) * 1024]),
            inc=ev)
        em.wait("vector", ev)
        em.q("vector", lambda v, dt=dt: v.scalar_tensor_tensor(
            P2[:, 0:1024], B2[:, 0:1024], b0S[:, dt:dt + 1], P1[:, 0:1024],
            op0=ALU.add, op1=ALU.subtract), inc=ev)
        em.wait("vector", ev)
        em.q("vector", lambda v: v.tensor_mul(P2[:, 0:1024], P2[:, 0:1024],
                                              D0bc[:, 0:1024]), inc=ev)
        em.wait("vector", ev)
        em.q("vector", lambda v: v.tensor_add(P1[:, 0:1024], P1[:, 0:1024],
                                              P2[:, 0:1024]), inc=ev)
        em.wait("vector", ev)
        em.q("vector", lambda v: v.tensor_add(P2[:, 0:1024], B2[:, 0:1024],
                                              B3[:, 0:1024]), inc=ev)
        em.wait("vector", ev)
        em.q("vector", lambda v, dt=dt: v.tensor_sub(
            P2[:, 0:1024], P2[:, 0:1024], B1[:, 0:1024]), inc=ev)
        em.wait("vector", ev)
        em.q("vector", lambda v, dt=dt: v.scalar_tensor_tensor(
            P2[:, 0:1024], ACC[:, dt * 1024:(dt + 1) * 1024], -1.0,
            P2[:, 0:1024], op0=ALU.mult, op1=ALU.add), inc=ev)
        em.wait("vector", ev)
        em.q("vector", lambda v: v.tensor_mul(P2[:, 0:1024], P2[:, 0:1024],
                                              D1bc[:, 0:1024]), inc=ev)
        em.wait("vector", ev)
        em.q("vector", lambda v: v.tensor_add(P1[:, 0:1024], P1[:, 0:1024],
                                              P2[:, 0:1024]), inc=ev)
        em.wait("vector", ev)
        em.q("vector", lambda v: v.tensor_add(P1[:, 0:1024], P1[:, 0:1024],
                                              P4[:, 0:1024]), inc=ev)
        em.wait("sync", ev)
        em.q("sync", lambda s, dt=dt: s.dma_start(
            out=io["sp_x2"][dt], in_=P1[:, 0:1024]), inc=sg, k=16)
        em.wait("vector", sg)
    fin = G["fin"]
    for j in range(NCH):
        em.wait("sync", ev)
        em.q("sync", lambda s, j=j: s.dma_start(
            out=P1[:, 0:2048].rearrange("p (a w) -> p a w", a=KT),
            in_=io["sp_x2"][:, :, j * 128:(j + 1) * 128].rearrange(
                "a p w -> p a w")), inc=ld, k=16)
        em.wait("vector", ld)
        em.q("vector", lambda v: v.tensor_copy(B1[:], P1[:]), inc=ev)
        em.wait("tensor", ev)
        for b in range(2):
            for k in range(8):
                kt = b * 8 + k
                em.q("tensor", lambda t, kt=kt, k=k: t.transpose(
                    p_trb[:, k * 128:(k + 1) * 128],
                    B1[:, kt * 128:(kt + 1) * 128], G["idS"][:]),
                    inc=(mm if k == 7 else None))
            em.wait("vector", mm)
            em.q("vector", lambda v, b=b: v.tensor_copy(
                P2[:].rearrange("p (a w) -> p a w", a=KT)[:, b * 8:(b + 1) * 8, :],
                p_trb[:].rearrange("p (a w) -> p a w", a=8)), inc=ev)
            em.wait("tensor", ev)
        em.wait("sync", ev)
        em.q("sync", lambda s, j=j: s.dma_start(
            out=io["y_strip"][j * 128:(j + 1) * 128, :], in_=P2[:]),
            inc=fin, k=16)
        em.wait("vector", fin)
    em.wait("sync", fin)
    f2b.close()
    fctx.close()



def kernel(**inputs):
    maps = prep_inputs(inputs)
    nc = build_program(taps=TAPS)
    for k, shp in EXPECTED_IN_SHAPES.items():
        assert tuple(maps[0][k].shape) == shp, (k, maps[0][k].shape, shp)
    res = run_bass_kernel_spmd(nc, maps, core_ids=list(range(NCORE)))
    out = np.concatenate([res.results[c]["y_strip"] for c in range(NCORE)], 0)
    return out.reshape(1, T, D)
